# version 10
# baseline (speedup 1.0000x reference)
import numpy as np
import ml_dtypes
import concourse.bass as bass
import concourse.mybir as mybir
from concourse.bass_utils import run_bass_kernel_spmd

F32 = mybir.dt.float32
BF16 = mybir.dt.bfloat16
U32 = mybir.dt.uint32
AF = mybir.ActivationFunctionType
ALU = mybir.AluOpType
AX = mybir.AxisListType.X

NC = 8
D = 2048
T = 8192
TL = T // NC
DH = 128
NH = 16
HPC = NH // NC
NDT = D // 128
QB = 512
NQB = T // QB
NRT = TL // 128
RG = [list(range(NC))]
EPS = 1e-5
SM_SCALE = float(np.float32(1.0) / np.sqrt(np.float32(DH)))
LOG_RING = 6
EXP_RING = 12
GRP = 3


def _attention_schedule():
    blocks = []
    g = 0
    outmm = pairs = masks = groups_done = 0
    outmm_of, pair_of = {}, {}
    for qb in range(NQB):
        for h in range(HPC):
            if g % GRP:
                g += GRP - (g % GRP)
            nt = 4 * (qb + 1)
            g0 = g
            tiles = []
            for i in range(nt):
                gg = g0 + i
                diag = i >= nt - 4
                mi = None
                if diag:
                    masks += 1
                    mi = masks
                outmm += 1
                outmm_of[gg] = outmm
                tiles.append(dict(g=gg, kvt=i, diag=diag, mask_idx=mi,
                                  out_idx=outmm, esl=gg % EXP_RING))
            grps = []
            for i in range(0, nt, GRP):
                sub = tiles[i:i + GRP]
                grps.append(dict(gseq=groups_done, tiles=sub,
                                 slot0=sub[0]["g"] % LOG_RING,
                                 esl0=sub[0]["g"] % EXP_RING))
                for tt in sub:
                    tt["gseq"] = groups_done
                groups_done += 1
            prs = []
            for p in range(nt // 2):
                t0, t1 = tiles[2 * p], tiles[2 * p + 1]
                pairs += 1
                pair_of[t0["g"]] = pairs
                pair_of[t1["g"]] = pairs
                t0["pair_idx"] = pairs
                t1["pair_idx"] = pairs
                prs.append(dict(t0=t0, t1=t1, idx=pairs))
            blocks.append(dict(qb=qb, h=h, b=len(blocks), nt=nt, grps=grps,
                               tiles=tiles, prs=prs, groups_after=groups_done,
                               outmm_after=outmm, pairs_after=pairs,
                               masks_after=masks))
            g = g0 + nt
    ecount = 0
    allg = []
    for blk_ in blocks:
        for grp in blk_["grps"]:
            ecount += 1
            grp["eg"] = ecount
            for tt in grp["tiles"]:
                tt["eg"] = ecount
            allg.append(grp)
        ecount += 3
        blk_["eg_after"] = ecount
    for gi, grp in enumerate(allg):
        grp["war_eg"] = allg[gi - 2]["eg"] if gi >= 2 else 0
    for blk_ in blocks:
        for grp in blk_["grps"]:
            wo = wp = 0
            for tt in grp["tiles"]:
                prev = tt["g"] - EXP_RING
                wo = max(wo, outmm_of.get(prev, 0))
                wp = max(wp, pair_of.get(prev, 0))
            grp["war_out"] = wo
            grp["war_pair"] = wp
    return blocks


SCHED = _attention_schedule()


def pair_of_tile(blko, tt):
    return tt["pair_idx"]

N_GROUPS = SCHED[-1]["groups_after"]
N_OUTMM = SCHED[-1]["outmm_after"]
N_PAIRS = SCHED[-1]["pairs_after"]


def build_nc(stop_phase=9, dbg=()):
    nc = bass.Bass("TRN2", target_bir_lowering=False, debug=False,
                   detect_race_conditions=False)

    x_own = nc.dram_tensor("x_own", [TL, D], F32, kind="ExternalInput").ap()
    wq = nc.dram_tensor("wq", [D, D], F32, kind="ExternalInput").ap()
    wk_sl = nc.dram_tensor("wk_sl", [D, HPC * DH], F32, kind="ExternalInput").ap()
    wv_sl = nc.dram_tensor("wv_sl", [D, HPC * DH], F32, kind="ExternalInput").ap()
    wo_sl = nc.dram_tensor("wo_sl", [D // NC, D], F32, kind="ExternalInput").ap()
    wf_sl = nc.dram_tensor("wf_sl", [D // NC, D], F32, kind="ExternalInput").ap()
    bq_t = nc.dram_tensor("bq_t", [128, NDT], F32, kind="ExternalInput").ap()
    bk_t = nc.dram_tensor("bk_t", [128, HPC], F32, kind="ExternalInput").ap()
    bv_t = nc.dram_tensor("bv_t", [128, HPC], F32, kind="ExternalInput").ap()
    bo_r = nc.dram_tensor("bo_r", [1, D], F32, kind="ExternalInput").ap()
    bf_r = nc.dram_tensor("bf_r", [1, D], F32, kind="ExternalInput").ap()
    masks_in = nc.dram_tensor("masks", [128, 4 * QB], BF16, kind="ExternalInput").ap()
    ident_in = nc.dram_tensor("ident", [128, 128], F32, kind="ExternalInput").ap()
    identb_in = nc.dram_tensor("identb", [128, 128], BF16, kind="ExternalInput").ap()
    ones_r_in = nc.dram_tensor("ones_r", [1, 128], F32, kind="ExternalInput").ap()
    ones_c_in = nc.dram_tensor("ones_c", [128, 1], F32, kind="ExternalInput").ap()
    po_idx_in = nc.dram_tensor("po_idx", [128, NH], U32, kind="ExternalInput").ap()

    y = nc.dram_tensor("y", [TL, D], F32, kind="ExternalOutput").ap()

    ln_stage = [nc.dram_tensor(f"ln_stage{k}", [D, QB], BF16) for k in range(2)]
    ln_ag = [nc.dram_tensor(f"ln_ag{k}", [NC * D, QB], BF16,
                            addr_space="Shared") for k in range(2)]
    wo_stage = nc.dram_tensor("wo_stage", [D // NC, D], BF16)
    wo_ag = nc.dram_tensor("wo_ag", [D, D], BF16, addr_space="Shared")
    bwt_stage = nc.dram_tensor("bwt_stage", [D, D // NC], BF16)
    bwt_ag = nc.dram_tensor("bwt_ag", [NC * D, D // NC], BF16,
                            addr_space="Shared")
    ot_stage = [nc.dram_tensor(f"ot_stage{j}", [HPC * DH, 4 * QB], BF16)
                for j in range(4)]
    ot_ag = nc.dram_tensor("ot_ag", [T, 4 * QB], BF16,
                           addr_space="Shared")

    dbg_aps = {}

    def dbg_t(name, shape, dtype):
        if name in dbg:
            dbg_aps[name] = nc.dram_tensor("dbg_" + name, shape, dtype,
                                           kind="ExternalOutput").ap()

    dbg_t("lnxt", [128, NDT * TL], BF16)
    dbg_t("st", [128, 7 * NRT], F32)
    dbg_t("qt", [128, HPC * T], BF16)
    dbg_t("kt", [128, HPC * T], BF16)
    dbg_t("v", [128, HPC * T], BF16)
    dbg_t("ot", [128, HPC * T], BF16)
    dbg_t("bwt", [128, NDT * 256], BF16)
    dbg_t("bal", [128, 8], F32)
    dbg_t("st2", [128, 8 * NRT], F32)
    dbg_t("sums", [32, 512], F32)
    dbg_t("kt4", [128, 2 * 8192], BF16)
    dbg_t("v4", [128, 2 * 8192], BF16)
    dbg_t("qt4", [128, 2 * 8192], BF16)
    dbg_t("x1", [128, NRT * D], F32)

    cms = []
    cm = {}

    def salloc(name, shape, dtype, side="left"):
        c = nc.sbuf_tensor(name, shape, dtype, side=side)
        t = c.__enter__()
        cm[name] = c
        return t

    def sfree(*names):
        for n in names:
            cm.pop(n).__exit__(None, None, None)

    big_a = salloc("big_a", [128, NDT * TL], BF16)
    masks_sb = salloc("masks_sb", [128, 4 * QB], BF16)
    ident = salloc("ident_sb", [128, 128], F32)
    identB = salloc("identB_sb", [128, 128], BF16)
    ones_r = salloc("ones_r_sb", [1, 128], F32)
    ones_rb = salloc("ones_rb_sb", [1, 128], BF16)
    ones_c = salloc("ones_c_sb", [128, 1], F32)
    bqt = salloc("bqt_sb", [128, NDT], F32)
    bkt = salloc("bkt_sb", [128, HPC], F32)
    bvt = salloc("bvt_sb", [128, HPC], F32)
    bor = salloc("bor_sb", [1, D], BF16)
    bfr = salloc("bfr_sb", [1, D], BF16)
    poix = salloc("poix_sb", [128, NH], U32)
    st_mu = salloc("st_mu", [128, NRT], F32)
    st_muv = salloc("st_muv", [128, NRT], F32)
    st_m2 = salloc("st_m2", [128, NRT], F32)
    st_m2e = salloc("st_m2e", [128, NRT], F32)
    st_sq = salloc("st_sq", [128, NRT], F32)
    st_v = salloc("st_v", [128, NRT], F32)
    st_l = salloc("st_l", [128, NRT], F32)
    st_r = salloc("st_r", [128, NRT], F32)
    st_sqd = salloc("st_sqd", [128, NRT], F32)
    t_sqd = salloc("t_sqd", [128, NRT], F32)
    jnk = salloc("jnk", [128, 4], F32)
    jnk2 = salloc("jnk2", [128, 4], F32)
    jexp = salloc("jexp", [128, 512], BF16)
    bscr = salloc("bscr", [128, 1024], F32)
    ascr = salloc("ascr", [128, 1024], F32)

    pA_cm = nc.psum_tensor("pA", [128, 3 * 512], F32)
    pB_cm = nc.psum_tensor("pB", [128, 3 * 512], F32)
    pout_cm = nc.psum_tensor("pout", [128, 512], F32)
    paux_cm = nc.psum_tensor("paux", [128, 512], F32)
    pA, pB = pA_cm.__enter__(), pB_cm.__enter__()
    pout, paux = pout_cm.__enter__(), paux_cm.__enter__()
    cms += [pA_cm, pB_cm, pout_cm, paux_cm]

    def pslot(i):
        i = i % LOG_RING
        t = pA if i < 3 else pB
        return t[:, (i % 3) * 512:((i % 3) + 1) * 512]

    def pgroup(slot0, w):
        t = pA if slot0 == 0 else pB
        return t[:, :w]

    semcms = []

    def sem(name):
        c = nc.semaphore(name)
        s = c.__enter__()
        semcms.append(c)
        return s

    s0_ld = sem("s0_ld"); s0_ca = sem("s0_ca")
    s1_ld = sem("s1_ld"); s1_sq = sem("s1_sq"); s1_s2 = sem("s1_s2")
    s1_red = sem("s1_red"); s1_rs = sem("s1_rs")
    s1_st = sem("s1_st"); s1_ln = sem("s1_ln"); s1_tp = sem("s1_tp")
    s1_te = sem("s1_te"); s1_gp = sem("s1_gp")
    cc_ln = [sem("cc_ln0"), sem("cc_ln1")]
    s2_ld = sem("s2_ld"); s2_ca = sem("s2_ca"); s2_mm = sem("s2_mm")
    s2_ev = sem("s2_ev")
    sw_ld = sem("sw_ld"); sw_vx = sem("sw_vx"); sw_tp = sem("sw_tp")
    sw_te = sem("sw_te"); sw_gp = sem("sw_gp")
    cc_wo = sem("cc_wo"); cc_bw = sem("cc_bw")
    s3_ld = sem("s3_ld"); s3_mm = sem("s3_mm"); s3_ev = sem("s3_ev")
    s3_evV = sem("s3_evV"); s3_tp = sem("s3_tp"); s3_te = sem("s3_te")
    s4_log = sem("s4_log")
    s4_xr = sem("s4_xr")
    s4_acc = sem("s4_acc")
    s3_eg = sem("s3_eg"); s3_tg = sem("s3_tg")
    s4_wr = sem("s4_wr"); s4_warm = sem("s4_warm"); s_chain = sem("s_chain")
    gp1 = sem("gp1"); gp2 = sem("gp2"); gpw = sem("gpw"); gp3 = sem("gp3")
    g4b = sem("g4b"); gp5 = sem("gp5"); gp6 = sem("gp6"); gp7 = sem("gp7")
    ga2 = sem("ga2"); ga3 = sem("ga3"); s_stb = sem("s_stb")
    s0_fl = sem("s0_fl"); s1_fl = sem("s1_fl"); s2_fl = sem("s2_fl")
    s3_fl = sem("s3_fl"); sw_fl = sem("sw_fl"); s5_fl = sem("s5_fl")
    s7_fl = sem("s7_fl"); s5_gf = sem("s5_gf")
    s3_lk = sem("s3_lk"); s3_lv = sem("s3_lv"); s3_lb = sem("s3_lb")
    s2_lq = sem("s2_lq")
    s4_prs = sem("s4_prs"); s4_out = sem("s4_out"); s4_sum = sem("s4_sum")
    s4_se = sem("s4_se")
    s4_rcp = sem("s4_rcp"); s4_bct = sem("s4_bct"); s4_div = sem("s4_div")
    s4_gp = sem("s4_gp")
    cc_ot = [sem(f"cc_ot{j}") for j in range(4)]
    s5_ld = sem("s5_ld"); s5_gp = sem("s5_gp"); s5_mm = sem("s5_mm")
    s5_ev = sem("s5_ev")
    s6_rs = sem("s6_rs"); s6_ln = sem("s6_ln"); s6_tp = sem("s6_tp"); s6_te = sem("s6_te")
    s7_ld = sem("s7_ld"); s7_mm = sem("s7_mm"); s7_ev = sem("s7_ev")
    s_st = sem("s_st")
    s_dbg = sem("s_dbg")
    sfl_v = sem("sfl_v"); sfl_s = sem("sfl_s")

    def _mkord(entries):
        return entries, {k: 16 * (i + 1) for i, (k, _, _) in enumerate(entries)}

    _o = []
    for _p in range(NRT):
        _o.append((("ln", _p), s1_ln, _p + 1))
        for _g in range(4 * _p, 4 * _p + 4):
            _o.append((("tp", _g), s1_tp, _g + 1))
        if _p == 3:
            _o.append((("te", 0), s1_te, 16))
    _o.append((("te", 1), s1_te, 32))
    g1_ord, g1_idx = _mkord(_o)
    g1a_ord = g1_ord[:21]
    g1b_ord = g1_ord[21:]

    g2_ord, g2_idx = _mkord([(("ca", _c), s2_ca, 2 * _c + 2)
                             for _c in range(NDT)])

    _o = [(("bw", 0), sw_vx, 4)]
    _o += [(("tp", _k), sw_tp, _k + 1) for _k in range(4)]
    _o.append((("bw", 1), sw_vx, 6))
    _o += [(("tp", _k), sw_tp, _k + 1) for _k in range(4, 8)]
    gw_ord, gw_idx = _mkord(_o)

    _o = [(("wv",), sw_vx, 14)]
    for _i in range(16):
        for _h in range(HPC):
            _o.append((("vt", _i, _h), s3_mm, 4 * _i + 2 + _h + 1))
        for _h in range(HPC):
            _o.append((("vf", _i, _h), s3_evV, 2 * _i + _h + 1))
        for _h in range(HPC):
            _o.append((("tp3", _i, _h), s3_tp, 2 * _i + _h + 1))
        for _h in range(HPC):
            _o.append((("ev", _i, _h), s3_ev, 2 * _i + _h + 1))
    g3_ord, g3_idx = _mkord(_o)

    g5_ord, g5_idx = _mkord([(("mm", _m), s5_mm, _m + 1) for _m in range(32)])

    _o = []
    for _p in range(NRT):
        _o.append((("ln6", _p), s6_ln, _p + 1))
        for _g in range(4 * _p, 4 * _p + 4):
            _o.append((("tp6", _g), s6_tp, _g + 1))
        _o.append((("te6", _p), s6_te, 4 * _p + 4))
    g6_ord, g6_idx = _mkord(_o)

    g7_ord, g7_idx = _mkord([(("mm", _m), s7_mm, _m + 1) for _m in range(32)])

    def emit_grants(e, gsem, entries, row):
        for _, raw, val in entries:
            e.wait_ge(raw, val)
            e.dma_start(out=jnk[row:row + 1, 0:4], in_=ident_in[0:1, 0:4]
                        ).then_inc(gsem, 16)

    def trailer(e, fl):
        e.dma_start(out=jnk[3:4, 0:4], in_=ident_in[0:1, 0:4]).then_inc(fl, 16)

    gsb_n = [0]
    spb_n = [0]

    def sp_selfb(e, raw, val):
        e.wait_ge(raw, val)
        e.dma_start(out=jnk[3:4, 0:4], in_=ident_in[0:1, 0:4]).then_inc(s_stb, 16)
        spb_n[0] += 16
        e.wait_ge(s_stb, spb_n[0])

    def gp_selfb(e, raw, val):
        e.wait_ge(raw, val)
        e.dma_start(out=jnk[1:2, 0:4], in_=ident_in[0:1, 0:4]).then_inc(g4b, 16)
        gsb_n[0] += 16
        e.wait_ge(g4b, gsb_n[0])

    N_P0C = 9
    blk_cm = nc.Block()
    blk = blk_cm.__enter__()

    def finish(extra_waits=()):
        ndbg = len(dbg_aps)

        @blk.gpsimd
        def _(e):
            for s, v in extra_waits:
                e.wait_ge(s, v)
            if ndbg:
                e.wait_ge(s_dbg, 16 * ndbg)

        blk_cm.__exit__(None, None, None)
        return nc, dbg_aps

    btmp = salloc("btmp", [1, D], F32)

    @blk.sync
    def _(e):
        p0list = [(masks_sb, masks_in), (ident, ident_in),
                  (identB, identb_in),
                  (ones_r, ones_r_in), (ones_c, ones_c_in),
                  (bqt, bq_t), (bkt, bk_t), (bvt, bv_t),
                  (poix, po_idx_in)]
        assert len(p0list) == N_P0C
        for dst, src in p0list:
            e.dma_start(out=dst[:], in_=src).then_inc(s0_ld, 16)
            trailer(e, s0_fl)
        e.dma_start(out=btmp[:], in_=bo_r).then_inc(s0_ld, 16)
        trailer(e, s0_fl)
        sp_selfb(e, s0_ca, 1)
        e.dma_start(out=btmp[:], in_=bf_r).then_inc(s0_ld, 16)
        trailer(e, s0_fl)

    @blk.vector
    def _(e):
        e.wait_ge(s0_ld, 16 * (N_P0C + 1))
        e.wait_ge(s0_fl, 16 * (N_P0C + 1))
        e.tensor_copy(bor[:], btmp[:]).then_inc(s0_ca, 1)
        e.wait_ge(s0_ld, 16 * (N_P0C + 2))
        e.wait_ge(s0_fl, 16 * (N_P0C + 2))
        e.tensor_copy(bfr[:], btmp[:]).then_inc(s0_ca, 1)
        e.tensor_copy(ones_rb[:], ones_r[:]).then_inc(s0_ca, 1)

    sfree("btmp")
    qt_sb = salloc("qt_sb", [128, HPC * T], BF16)
    kt_sb = salloc("kt_sb", [128, HPC * T], BF16)
    v_sb = salloc("v_sb", [128, HPC * T], BF16)
    xstg = [salloc(f"xstg{i}", [128, D], F32) for i in range(2)]
    xln = [salloc(f"xln{i}", [128, D], F32) for i in range(2)]
    scr1 = [salloc(f"scr1{i}", [128, D], F32) for i in range(2)]

    @blk.sync
    def _(e):
        for rt in range(NRT):
            if rt >= 2:
                sp_selfb(e, s1_red, rt - 1)
            e.dma_start(out=xstg[rt % 2][:],
                        in_=x_own[rt * 128:(rt + 1) * 128, :]).then_inc(s1_ld, 16)
            trailer(e, s1_fl)
        for rt in range(NRT):
            if rt == 0:
                e.wait_ge(s1_red, NRT)
                e.wait_ge(sfl_s, 1)
            if rt >= 2:
                sp_selfb(e, s1_ln, rt - 1)
            e.dma_start(out=xstg[rt % 2][:],
                        in_=x_own[rt * 128:(rt + 1) * 128, :]).then_inc(s1_ld, 16)
            trailer(e, s1_fl)

    @blk.scalar
    def _(e):
        e.wait_ge(sfl_v, 1)
        e.activation(ascr[:], masks_sb[:, 0:1024], AF.Identity)
        e.activation(ascr[:], masks_sb[:, 0:1024], AF.Identity).then_inc(sfl_s, 1)
        e.wait_ge(sfl_v, 2)
        e.activation(ascr[:], masks_sb[:, 0:1024], AF.Identity)
        e.activation(ascr[:], masks_sb[:, 0:1024], AF.Identity)
        e.activation(st_l[:], st_v[:], AF.Ln)
        e.activation(ascr[:], masks_sb[:, 0:1024], AF.Identity)
        e.activation(ascr[:], masks_sb[:, 0:1024], AF.Identity)
        e.activation(st_r[:], st_l[:], AF.Exp, scale=-0.5)
        e.activation(ascr[:], masks_sb[:, 0:1024], AF.Identity)
        e.activation(ascr[:], masks_sb[:, 0:1024], AF.Identity).then_inc(s1_rs, 1)

    @blk.vector
    def _(e):
        for r in range(NRT):
            e.wait_ge(s1_ld, 16 * (r + 1))
            e.wait_ge(s1_fl, 16 * (r + 1))
            e.tensor_tensor(scr1[r % 2][:], xstg[r % 2][:], xstg[r % 2][:],
                            ALU.mult)
            e.reduce_sum(st_mu[:, r:r + 1], xstg[r % 2][:],
                         axis=AX).then_inc(s1_red, 1)
            if r < NRT - 1:
                e.reduce_sum(st_sq[:, r:r + 1], scr1[r % 2][:], axis=AX)
            else:
                e.reduce_sum(st_sq[:, r:r + 1], scr1[r % 2][:],
                             axis=AX).then_inc(sfl_v, 1)
        e.wait_ge(sfl_s, 1)
        e.tensor_scalar(st_muv[:], st_mu[:], 1.0 / D, None, ALU.mult)
        e.tensor_scalar(st_sqd[:], st_sq[:], 1.0 / D, None, ALU.mult)
        e.tensor_copy(bscr[:], scr1[0][:, 0:1024])
        e.tensor_copy(bscr[:], scr1[1][:, 0:1024])
        e.tensor_tensor(st_m2[:], st_muv[:], st_muv[:], ALU.mult)
        e.tensor_copy(bscr[:], scr1[0][:, 0:1024])
        e.tensor_copy(bscr[:], scr1[1][:, 0:1024])
        e.tensor_scalar(st_m2e[:], st_m2[:], EPS, None, ALU.subtract)
        e.tensor_copy(bscr[:], scr1[0][:, 0:1024])
        e.tensor_copy(bscr[:], scr1[1][:, 0:1024])
        e.tensor_tensor(st_v[:], st_sqd[:], st_m2e[:],
                        ALU.subtract).then_inc(sfl_v, 1)
        for rr in range(NRT + 2):
            rte = rr - 2
            if 0 <= rte < NRT:
                for grp4 in range(4):
                    gidx = 4 * rte + grp4
                    bank = paux if gidx % 2 == 0 else pout
                    e.wait_ge(gp1, g1_idx[("tp", gidx)])
                    dt0 = grp4 * 4
                    dst = big_a[:].rearrange("p (dt t) -> p dt t", dt=NDT)[
                        :, dt0:dt0 + 4, rte * 128:(rte + 1) * 128]
                    srcv = bank[:].rearrange("p (j t) -> p j t", j=4)
                    e.tensor_copy(dst, srcv).then_inc(s1_te, 1)
            p = rr - 1
            if 0 <= p < NRT:
                e.wait_ge(s1_ld, 16 * (NRT + p + 1))
                e.wait_ge(s1_fl, 16 * (NRT + p + 1))
                e.wait_ge(s1_rs, 1)
                if p >= 2:
                    e.wait_ge(s1_tp, 4 * (p - 1))
                e.tensor_scalar(xln[p % 2][:], xstg[p % 2][:],
                                st_muv[:, p:p + 1], st_r[:, p:p + 1],
                                ALU.subtract, ALU.mult).then_inc(s1_ln, 1)

    @blk.tensor
    def _(e):
        for rt in range(NRT):
            e.wait_ge(gp1, g1_idx[("ln", rt)])
            for grp4 in range(4):
                gidx = 4 * rt + grp4
                bank = paux if gidx % 2 == 0 else pout
                if gidx >= 2:
                    e.wait_ge(s1_te, gidx - 1)
                for j in range(4):
                    dt = grp4 * 4 + j
                    last = (j == 3)
                    ins = e.transpose(bank[:, j * 128:(j + 1) * 128],
                                      xln[rt % 2][:, dt * 128:(dt + 1) * 128],
                                      ident[:])
                    if last:
                        ins.then_inc(s1_tp, 1)

    @blk.gpsimd
    def _(e):
        for k in range(2):
            emit_grants(e, gp1, g1a_ord if k == 0 else g1b_ord, 1)
            gp_selfb(e, s1_te, 16 * (k + 1))
            for dt in range(NDT):
                e.dma_start(
                    out=ln_stage[k].ap()[dt * 128:(dt + 1) * 128, :],
                    in_=big_a[:, dt * TL + k * 512:dt * TL + (k + 1) * 512],
                ).then_inc(s1_gp, 16)
            e.wait_ge(s1_gp, 16 * NDT * (k + 1))
            e.collective_compute(
                "AllGather", ALU.bypass, replica_groups=RG,
                ins=[ln_stage[k].ap().opt()],
                outs=[ln_ag[k].ap().opt()],
            ).then_inc(cc_ln[k])

    if "lnxt" in dbg_aps:
        @blk.sync
        def _(e):
            e.wait_ge(s1_te, 32)
            e.dma_start(out=dbg_aps["lnxt"], in_=big_a[:]).then_inc(s_dbg, 16)

    if "st" in dbg_aps:
        @blk.sync
        def _(e):
            e.wait_ge(s1_te, 32)
            for ii, tile in enumerate([st_mu, st_muv, st_m2, st_m2e, st_sq,
                                       st_v, st_r]):
                e.dma_start(out=dbg_aps["st"][:, ii * NRT:(ii + 1) * NRT],
                            in_=tile[:]).then_inc(s_dbg, 16)
    if "st2" in dbg_aps:
        @blk.sync
        def _(e):
            e.wait_ge(s1_te, 32)
            e.dma_start(out=dbg_aps["st2"][:, 0:NRT],
                        in_=st_sqd[:]).then_inc(s_dbg, 16)

    if stop_phase <= 1:
        return finish()

    sfree("scr11", "scr10", "xln1", "xln0", "xstg1", "xstg0")
    wqstg = salloc("wqstg", [128, 8 * 128], F32)
    wqB = [salloc(f"wqB{i}", [128, D], BF16) for i in range(2)]

    @blk.sync
    def _(e):
        for ct in range(NDT):
            for half in range(2):
                if ct == 0 and half == 0:
                    e.wait_ge(s1_ln, NRT)
                    e.wait_ge(s1_tp, 4 * NRT)
                if ct > 0 or half > 0:
                    sp_selfb(e, s2_ca, 2 * ct + half)
                for j in range(8):
                    r0 = (half * 8 + j) * 128
                    e.dma_start(out=wqstg[:, j * 128:(j + 1) * 128],
                                in_=wq[r0:r0 + 128, ct * 128:(ct + 1) * 128]
                                ).then_inc(s2_lq, 16)
                e.dma_start(out=jnk[3:4, 0:4], in_=ident_in[0:1, 0:4]
                            ).then_inc(s2_ld, 16)
                trailer(e, s2_fl)

    @blk.vector
    def _(e):
        for ct in range(NDT):
            for half in range(2):
                e.wait_ge(s2_lq, 16 * 8 * (2 * ct + half + 1))
                e.wait_ge(s2_ld, 16 * (2 * ct + half + 1))
                e.wait_ge(s2_fl, 16 * (2 * ct + half + 1))
                if ct >= 2 and half == 0:
                    e.wait_ge(s2_mm, 2 * (ct - 1))
                e.tensor_copy(wqB[ct % 2][:, half * 1024:(half + 1) * 1024],
                              wqstg[:]).then_inc(s2_ca, 1)

    @blk.tensor
    def _(e):
        for ct in range(NDT):
            for tch in range(2):
                slot = (0 if ct % 2 == 0 else 3) + tch
                if ct == 0 and tch == 0:
                    e.wait_ge(s0_ca, 3)
                    e.wait_ge(gp1, g1_idx[("te", 0)])
                if ct == 0 and tch == 1:
                    e.wait_ge(gp1, g1_idx[("te", 1)])
                if tch == 0:
                    e.wait_ge(gp2, g2_idx[("ca", ct)])
                    if ct >= 2:
                        e.wait_ge(s2_ev, min(2 * (ct - 2) + 3, 32))
                ps = pslot(slot)
                for dt in range(NDT):
                    ins = e.matmul(ps, wqB[ct % 2][:, dt * 128:(dt + 1) * 128],
                                   big_a[:, dt * TL + tch * 512:
                                         dt * TL + (tch + 1) * 512],
                                   start=(dt == 0), stop=(dt == NDT - 1))
                    if dt == NDT - 1:
                        ins.then_inc(s2_mm, 1)

    @blk.scalar
    def _(e):
        for ct in range(NDT):
            for tch in range(2):
                slot = (0 if ct % 2 == 0 else 3) + tch
                k = 2 * ct + tch + 1
                if k <= 31:
                    e.wait_ge(s2_mm, k + 1)
                else:
                    e.wait_ge(s2_mm, 32)
                    e.dma_start(out=jnk[2:3, 0:4], in_=ident_in[0:1, 0:4]
                                ).then_inc(ga2, 16)
                    e.wait_ge(ga2, 16)
                dst = qt_sb[:].rearrange("p (hh t j) -> p hh t j",
                                         hh=HPC, j=NDT)[:, tch, :, ct]
                e.activation(dst, pslot(slot), AF.Identity,
                             bias=bqt[:, ct:ct + 1]).then_inc(s2_ev, 1)

    @blk.gpsimd
    def _(e):
        emit_grants(e, gp2, g2_ord, 1)

    if "qt" in dbg_aps:
        @blk.sync
        def _(e):
            sp_selfb(e, s2_ev, 32)
            e.dma_start(out=dbg_aps["qt"], in_=qt_sb[:]).then_inc(s_dbg, 16)

    if stop_phase <= 2:
        return finish()

    wfstg = salloc("wfstg", [128, D], F32)
    wfstg2 = salloc("wfstg2", [128, D], F32)
    bwf = salloc("bwf", [128, D], F32)
    bsgn = salloc("bsgn", [128, D], F32)
    bal = salloc("bal", [128, 4], F32)
    bal2 = salloc("bal2", [128, 2], F32)
    wo_bf = salloc("wo_bf", [128, D], BF16)
    bwt_sb = salloc("bwt_sb", [128, NDT * 256], BF16)

    @blk.sync
    def _(e):
        for rt in range(2):
            if rt > 0:
                sp_selfb(e, sw_vx, 1)
            e.dma_start(out=wfstg[:],
                        in_=wo_sl[rt * 128:(rt + 1) * 128, :]).then_inc(sw_ld, 16)
            trailer(e, sw_fl)
        sp_selfb(e, sw_vx, 2)
        e.dma_start(out=wfstg[:],
                    in_=wf_sl[0:128, :]).then_inc(sw_ld, 16)
        trailer(e, sw_fl)
        e.dma_start(out=wfstg2[:],
                    in_=wf_sl[128:256, :]).then_inc(sw_ld, 16)
        trailer(e, sw_fl)

    @blk.vector
    def _(e):
        for rt in range(2):
            e.wait_ge(sw_ld, 16 * (rt + 1))
            e.wait_ge(sw_fl, 16 * (rt + 1))
            if rt > 0:
                e.wait_ge(sw_gp, 16)
            e.tensor_copy(wo_bf[:], wfstg[:]).then_inc(sw_vx, 1)
        e.wait_ge(sw_ld, 48)
        e.wait_ge(sw_fl, 48)
        e.reduce_sum(bal[:, 0:1], wfstg[:], axis=AX,
                     apply_absolute_value=True)
        e.wait_ge(sw_ld, 64)
        e.wait_ge(sw_fl, 64)
        e.reduce_sum(bal[:, 1:2], wfstg2[:], axis=AX,
                     apply_absolute_value=True).then_inc(sfl_v, 1)
        e.wait_ge(sfl_s, 2)
        e.tensor_scalar(bal2[:, 0:1], bal[:, 0:1], 1.0 / D, None, ALU.mult)
        e.tensor_scalar(bal2[:, 1:2], bal[:, 1:2], 1.0 / D, None, ALU.mult)
        for rt in range(2):
            wsrc = wfstg if rt == 0 else wfstg2
            if rt > 0:
                e.wait_ge(sw_tp, 4)
            e.tensor_scalar(bwf[:], wsrc[:], 0.0, None, ALU.is_ge)
            e.tensor_scalar(bsgn[:], bwf[:], 2.0, 1.0,
                            ALU.mult, ALU.subtract).then_inc(sw_vx, 1)
            e.tensor_scalar(bwf[:], bsgn[:], bal2[:, rt:rt + 1], None,
                            ALU.mult).then_inc(sw_vx, 1)
            for grp4 in range(4):
                gidx = 32 + 4 * rt + grp4
                bank = paux if gidx % 2 == 0 else pout
                e.wait_ge(gpw, gw_idx[("tp", 4 * rt + grp4)])
                dt0 = grp4 * 4
                dst = bwt_sb[:].rearrange("p (dt c) -> p dt c", dt=NDT)[
                    :, dt0:dt0 + 4, rt * 128:(rt + 1) * 128]
                srcv = bank[:].rearrange("p (j t) -> p j t", j=4)
                e.tensor_copy(dst, srcv).then_inc(sw_te, 1)

    @blk.scalar
    def _(e):
        e.wait_ge(sfl_v, 3)
        e.activation(ascr[:], masks_sb[:, 0:1024], AF.Identity)
        e.activation(ascr[:], masks_sb[:, 0:1024], AF.Identity).then_inc(sfl_s, 1)

    @blk.tensor
    def _(e):
        for rt in range(2):
            e.wait_ge(gpw, gw_idx[("bw", rt)])
            for grp4 in range(4):
                gidx = 32 + 4 * rt + grp4
                bank = paux if gidx % 2 == 0 else pout
                prev = gidx - 2
                if prev < 32:
                    e.wait_ge(s1_te, prev + 1)
                else:
                    e.wait_ge(sw_te, prev - 32 + 1)
                for j in range(4):
                    dt = grp4 * 4 + j
                    ins = e.transpose(bank[:, j * 128:(j + 1) * 128],
                                      bwf[:, dt * 128:(dt + 1) * 128],
                                      ident[:])
                    if j == 3:
                        ins.then_inc(sw_tp, 1)

    @blk.gpsimd
    def _(e):
        for rt in range(2):
            gp_selfb(e, sw_vx, rt + 1)
            e.dma_start(out=wo_stage.ap()[rt * 128:(rt + 1) * 128, :],
                        in_=wo_bf[:]).then_inc(sw_gp, 16)
        e.wait_ge(sw_gp, 32)
        e.collective_compute("AllGather", ALU.bypass, replica_groups=RG,
                             ins=[wo_stage.ap().opt()],
                             outs=[wo_ag.ap().opt()]).then_inc(cc_wo)
        emit_grants(e, gpw, gw_ord, 1)
        gp_selfb(e, sw_te, 8)
        for dt in range(NDT):
            e.dma_start(out=bwt_stage.ap()[dt * 128:(dt + 1) * 128, :],
                        in_=bwt_sb[:, dt * 256:(dt + 1) * 256]
                        ).then_inc(sw_gp, 16)
        e.wait_ge(sw_gp, 32 + 16 * NDT)
        e.collective_compute("AllGather", ALU.bypass, replica_groups=RG,
                             ins=[bwt_stage.ap().opt()],
                             outs=[bwt_ag.ap().opt()]).then_inc(cc_bw)

    if "bwt" in dbg_aps:
        @blk.sync
        def _(e):
            sp_selfb(e, sw_te, 8)
            e.dma_start(out=dbg_aps["bwt"], in_=bwt_sb[:]).then_inc(s_dbg, 16)
    if "bal" in dbg_aps:
        @blk.sync
        def _(e):
            e.wait_ge(sw_te, 8)
            e.dma_start(out=dbg_aps["bal"][:, 0:4], in_=bal[:]).then_inc(s_dbg, 16)
            e.dma_start(out=dbg_aps["bal"][:, 4:6], in_=bal2[:]).then_inc(s_dbg, 16)

    if stop_phase <= 2.5:
        return finish()

    sfree("bwt_sb", "wo_bf", "bal2", "bal", "bsgn", "bwf", "wfstg2", "wfstg",
          "wqB1", "wqB0", "wqstg")
    big_b = salloc("big_b", [128, 2 * NDT * 512], BF16)
    wkB = salloc("wkB", [128, NDT * HPC * DH], BF16)
    wvB = salloc("wvB", [128, NDT * HPC * DH], BF16)
    wkvstg = salloc("wkvstg", [128, 4 * HPC * DH], F32)
    vtf = [salloc(f"vtf{h}", [128, 512], F32) for h in range(HPC)]

    @blk.sync
    def _(e):
        e.wait_ge(s2_ev, 32)
        e.wait_ge(sw_gp, 32 + 16 * NDT)
        e.wait_ge(sw_te, 8)
        for q in range(4):
            if q > 0:
                sp_selfb(e, sw_vx, 6 + q)
            for j in range(4):
                dt = 4 * q + j
                e.dma_start(out=wkvstg[:, j * 256:(j + 1) * 256],
                            in_=wk_sl[dt * 128:(dt + 1) * 128, :]
                            ).then_inc(s3_lk, 16)
            e.dma_start(out=jnk[3:4, 0:4], in_=ident_in[0:1, 0:4]
                        ).then_inc(s3_ld, 16)
            trailer(e, s3_fl)
        for i in range(16):
            k, r = i % 2, i // 2
            e.wait_ge(cc_ln[k], 1)
            if i == 1:
                for q in range(4):
                    sp_selfb(e, sw_vx, 10 + q)
                    for j in range(4):
                        dt = 4 * q + j
                        e.dma_start(out=wkvstg[:, j * 256:(j + 1) * 256],
                                    in_=wv_sl[dt * 128:(dt + 1) * 128, :]
                                    ).then_inc(s3_lv, 16)
                    e.dma_start(out=jnk[3:4, 0:4], in_=ident_in[0:1, 0:4]
                                ).then_inc(s3_ld, 16)
                    trailer(e, s3_fl)
            if i >= 2:
                e.wait_ge(s3_mm, 4 * (i - 1))
            for dt in range(NDT):
                e.dma_start(
                    out=big_b[:, (i % 2) * 8192 + dt * 512:
                              (i % 2) * 8192 + (dt + 1) * 512],
                    in_=ln_ag[k].ap()[r * D + dt * 128:r * D + (dt + 1) * 128, :]
                ).then_inc(s3_lb, 16)
            e.dma_start(out=jnk[3:4, 0:4], in_=ident_in[0:1, 0:4]
                        ).then_inc(s3_ld, 16)
            trailer(e, s3_fl)

    @blk.vector
    def _(e):
        for q in range(4):
            e.wait_ge(s3_lk, 16 * 4 * (q + 1))
            e.wait_ge(s3_ld, 16 * (q + 1))
            e.wait_ge(s3_fl, 16 * (q + 1))
            e.tensor_copy(wkB[:, q * 1024:(q + 1) * 1024],
                          wkvstg[:]).then_inc(sw_vx, 1)
        for q in range(4):
            e.wait_ge(s3_lv, 16 * 4 * (q + 1))
            e.wait_ge(s3_ld, 80 + 16 * (q + 1))
            e.wait_ge(s3_fl, 80 + 16 * (q + 1))
            e.tensor_copy(wvB[:, q * 1024:(q + 1) * 1024],
                          wkvstg[:]).then_inc(sw_vx, 1)

    @blk.tensor
    def _(e):
        for i in range(16):
            e.wait_ge(s3_lb, 16 * 16 * (i + 1))
            e.wait_ge(s3_ld, 80 if i == 0 else 144 + 16 * i)
            e.wait_ge(s3_fl, 80 if i == 0 else 144 + 16 * i)
            if i == 0:
                e.wait_ge(gp3, g3_idx[("wv",)])
                e.wait_ge(s2_ev, 32)
            for h in range(HPC):
                slot = h
                if i >= 1:
                    e.wait_ge(gp3, g3_idx[("ev", i - 1, h)])
                ps = pslot(slot)
                for dt in range(NDT):
                    ins = e.matmul(
                        ps, wkB[:, dt * 256 + h * 128:dt * 256 + (h + 1) * 128],
                        big_b[:, (i % 2) * 8192 + dt * 512:
                              (i % 2) * 8192 + (dt + 1) * 512],
                        start=(dt == 0), stop=(dt == NDT - 1))
                    if dt == NDT - 1:
                        ins.then_inc(s3_mm, 1)
            for h in range(HPC):
                slot = 2 + h
                if i >= 1:
                    e.wait_ge(gp3, g3_idx[("vf", i - 1, h)])
                ps = pslot(slot)
                for dt in range(NDT):
                    ins = e.matmul(
                        ps, wvB[:, dt * 256 + h * 128:dt * 256 + (h + 1) * 128],
                        big_b[:, (i % 2) * 8192 + dt * 512:
                              (i % 2) * 8192 + (dt + 1) * 512],
                        start=(dt == 0), stop=(dt == NDT - 1))
                    if dt == NDT - 1:
                        ins.then_inc(s3_mm, 1)
            for h in range(HPC):
                gidx = 40 + 2 * i + h
                bank = paux if gidx % 2 == 0 else pout
                e.wait_ge(gp3, g3_idx[("vf", i, h)])
                prev = gidx - 2
                if prev < 40:
                    e.wait_ge(sw_te, prev - 32 + 1)
                else:
                    e.wait_ge(s3_te, prev - 40 + 1)
                for j in range(4):
                    ins = e.transpose(bank[:, j * 128:(j + 1) * 128],
                                      vtf[h][:, j * 128:(j + 1) * 128],
                                      ident[:])
                    if j == 3:
                        ins.then_inc(s3_tp, 1)

    @blk.scalar
    def _(e):
        for i in range(16):
            for h in range(HPC):
                e.wait_ge(s3_mm, 4 * i + h + 3)
                ins = e.activation(kt_sb[:, h * T + i * 512:h * T + (i + 1) * 512],
                                   pslot(h), AF.Identity,
                                   bias=bkt[:, h:h + 1])
                ins.then_inc(s3_ev, 1)
        e.activation(jexp[:], masks_sb[:, 0:512], AF.Exp).then_inc(s4_wr, 1)

    @blk.vector
    def _(e):
        for i in range(16):
            for h in range(HPC):
                e.wait_ge(gp3, g3_idx[("vt", i, h)])
                if i >= 1:
                    e.wait_ge(s3_tp, 2 * (i - 1) + h + 1)
                e.tensor_scalar(vtf[h][:], pslot(2 + h), bvt[:, h:h + 1],
                                None, ALU.add).then_inc(s3_evV, 1)
            for h in range(HPC):
                gidx = 40 + 2 * i + h
                bank = paux if gidx % 2 == 0 else pout
                e.wait_ge(gp3, g3_idx[("tp3", i, h)])
                e.tensor_copy(v_sb[:, h * T + i * 512:h * T + (i + 1) * 512],
                              bank[:]).then_inc(s3_te, 1)

    @blk.gpsimd
    def _(e):
        emit_grants(e, gp3, g3_ord, 1)

    if "kt" in dbg_aps:
        @blk.sync
        def _(e):
            sp_selfb(e, s3_ev, 32)
            e.dma_start(out=dbg_aps["kt"], in_=kt_sb[:]).then_inc(s_dbg, 16)
    if "v" in dbg_aps:
        @blk.sync
        def _(e):
            sp_selfb(e, s3_te, 32)
            e.dma_start(out=dbg_aps["v"], in_=v_sb[:]).then_inc(s_dbg, 16)

    if stop_phase <= 3:
        return finish()

    sfree("vtf1", "vtf0", "wkvstg", "wvB", "wkB", "big_b")
    ot_sb = salloc("ot_sb", [128, HPC * T], BF16)
    exp_sb = salloc("exp_sb", [128, EXP_RING * 512], BF16)
    ptmp = salloc("ptmp", [128, 512], BF16)
    acc = salloc("acc", [128, 512], F32)
    rrow = salloc("rrow", [1, 512], F32)
    srow = salloc("srow", [1, 512], F32)
    s_dbs = sem("s_dbs") if "sums" in dbg_aps else None
    rbc = salloc("rbc", [128, 512], F32)

    @blk.tensor
    def _(e):
        for blko in SCHED:
            b, qb, h = blko["b"], blko["qb"], blko["h"]
            e.wait_ge(s3_ev, min(2 * qb + 4, 32))
            e.wait_ge(s3_te, min(2 * qb + 4, 32))
            grps = blko["grps"]
            ng = len(grps)
            first_out = blko["outmm_after"] - blko["nt"] + 1

            def emit_L(gi):
                grp = grps[gi]
                if grp["war_eg"]:
                    e.wait_ge(s4_xr, grp["war_eg"] + 1)
                for ti, tt in enumerate(grp["tiles"]):
                    kvt = tt["kvt"]
                    ins = e.matmul(
                        pslot(tt["g"]),
                        kt_sb[:, h * T + kvt * 128:h * T + (kvt + 1) * 128],
                        qt_sb[:, h * T + qb * 512:h * T + (qb + 1) * 512],
                        start=True, stop=not tt["diag"])
                    if tt["diag"]:
                        md = tt["kvt"] - (blko["nt"] - 4)
                        ins = e.matmul(pslot(tt["g"]), identB[:],
                                       masks_sb[:, md * 512:(md + 1) * 512],
                                       start=False, stop=True)
                    if ti == len(grp["tiles"]) - 1:
                        ins.then_inc(s4_log, 1)

            def emit_O(gi):
                grp = grps[gi]
                e.wait_ge(s4_xr, grp["eg"] + 3)
                for tt in grp["tiles"]:
                    if tt["out_idx"] == first_out and b >= 1:
                        e.wait_ge(s4_div, b)
                    sl = tt["esl"]
                    e.matmul(pout[:],
                             v_sb[:, h * T + tt["kvt"] * 128:
                                  h * T + (tt["kvt"] + 1) * 128],
                             exp_sb[:, sl * 512:(sl + 1) * 512],
                             start=(tt["out_idx"] == first_out),
                             stop=(tt["out_idx"] == blko["outmm_after"]),
                             ).then_inc(s4_out, 1)

            for gi in range(min(4, ng)):
                emit_L(gi)
            for gi in range(4, ng):
                emit_O(gi - 4)
                emit_L(gi)
            for gi in range(max(0, ng - 4), ng):
                emit_O(gi)
            e.wait_ge(s4_acc, b + 1)
            if b >= 1:
                e.wait_ge(s4_div, b)
            e.matmul(paux[0:1, :], ones_c[:], acc[:],
                     start=True, stop=True).then_inc(s4_sum, 1)
            e.wait_ge(s4_se, b + 1)
            e.matmul(paux[:], ones_r[:], srow[:],
                     start=True, stop=True).then_inc(s4_bct, 1)

    @blk.scalar
    def _(e):
        for blko in SCHED:
            for grp in blko["grps"]:
                e.wait_ge(s4_log, grp["gseq"] + 1)
                if grp["war_out"]:
                    e.wait_ge(s4_out, grp["war_out"])
                if grp["war_pair"]:
                    e.wait_ge(s4_prs, grp["war_pair"])
                w = 512 * len(grp["tiles"])
                esl0 = grp["esl0"]
                e.activation(exp_sb[:, esl0 * 512:esl0 * 512 + w],
                             pgroup(grp["slot0"], w), AF.Exp,
                             scale=SM_SCALE).then_inc(s4_xr, 1)
            for _ in range(3):
                e.activation(jexp[:], masks_sb[:, 0:512],
                             AF.Exp).then_inc(s4_xr, 1)
            e.wait_ge(s4_sum, blko["b"] + 1)
            e.activation(srow[:], paux[0:1, :], AF.Identity)
            e.activation(jexp[:], masks_sb[:, 0:512],
                         AF.Exp).then_inc(s4_se, 1)

    @blk.vector
    def _(e):
        for blko in SCHED:
            b, qb, h = blko["b"], blko["qb"], blko["h"]
            for pr in blko["prs"]:
                t0, t1 = pr["t0"], pr["t1"]
                e.wait_ge(s4_xr, t1["eg"] + 2)
                s0, s1x = t0["esl"], t1["esl"]
                e.tensor_tensor(ptmp[:], exp_sb[:, s0 * 512:(s0 + 1) * 512],
                                exp_sb[:, s1x * 512:(s1x + 1) * 512], ALU.add)
                if pr["idx"] == blko["pairs_after"] - blko["nt"] // 2 + 1:
                    e.tensor_copy(acc[:], ptmp[:]).then_inc(s4_prs, 1)
                else:
                    e.tensor_tensor(acc[:], acc[:], ptmp[:],
                                    ALU.add).then_inc(s4_prs, 1)
            e.tensor_copy(bscr[:], exp_sb[:, 0:1024])
            e.tensor_copy(bscr[:], exp_sb[:, 1024:2048]).then_inc(s4_acc, 1)
            if s_dbs is not None and b >= 1:
                e.wait_ge(s_dbs, 16 * b)
            e.wait_ge(s4_bct, b + 1)
            e.tensor_copy(bscr[:], exp_sb[:, 0:1024])

            e.reciprocal(rbc[:], paux[:]).then_inc(s4_rcp, 1)
            e.wait_ge(s4_out, blko["outmm_after"])
            e.tensor_tensor(ot_sb[:, h * T + qb * 512:h * T + (qb + 1) * 512],
                            pout[:], rbc[:], ALU.mult)
            e.tensor_copy(bscr[:], exp_sb[:, 1024:2048]).then_inc(s4_div, 1)
        e.tensor_copy(bscr[:], exp_sb[:, 0:1024]).then_inc(s4_div, 1)

    @blk.sync
    def _(e):
        def bounce(sem_raw, val, sem_granted):
            e.wait_ge(sem_raw, val)
            e.dma_start(out=jnk[0:1, 0:4], in_=ident_in[0:1, 0:4]
                        ).then_inc(sem_granted, 16)

        chain_n = [0]

        def long_bounce(sem_raw, val, sem_granted, n=5):
            e.wait_ge(sem_raw, val)
            for _ in range(n):
                chain_n[0] += 16
                e.dma_start(out=jnk[0:1, 0:4], in_=ident_in[0:1, 0:4]
                            ).then_inc(s_chain, 16)
                e.wait_ge(s_chain, chain_n[0])
            e.dma_start(out=jnk[0:1, 0:4], in_=ident_in[0:1, 0:4]
                        ).then_inc(sem_granted, 16)

        if s_dbs is not None:
            for blko in SCHED:
                b = blko["b"]
                e.wait_ge(s4_se, b + 1)
                e.dma_start(out=dbg_aps["sums"][b:b + 1, :], in_=srow[:]
                            ).then_inc(s_dbs, 16)

    @blk.gpsimd
    def _(e):
        for j in range(4):
            gp_selfb(e, s4_div, 8 * (j + 1))
            for h in range(HPC):
                e.dma_start(out=ot_stage[j].ap()[h * 128:(h + 1) * 128, :],
                            in_=ot_sb[:, h * T + j * 2048:h * T + (j + 1) * 2048]
                            ).then_inc(s4_gp, 16)
            e.wait_ge(s4_gp, 32 * (j + 1))
            e.collective_compute(
                "AllGather", ALU.bypass, replica_groups=RG,
                ins=[ot_stage[j].ap().opt()],
                outs=[ot_ag.ap()[j * 2048:(j + 1) * 2048, :].opt()],
            ).then_inc(cc_ot[j])

    if "ot" in dbg_aps:
        @blk.sync
        def _(e):
            sp_selfb(e, s4_div, 32)
            e.dma_start(out=dbg_aps["ot"], in_=ot_sb[:]).then_inc(s_dbg, 16)
    for _nm, _sb in (("kt4", kt_sb), ("v4", v_sb), ("qt4", qt_sb)):
        if _nm in dbg_aps:
            @blk.sync
            def _(e, _sb=_sb, _nm=_nm):
                e.wait_ge(s4_div, 33)
                e.dma_start(out=dbg_aps[_nm], in_=_sb[:]).then_inc(s_dbg, 16)
    if "sums" in dbg_aps:
        @blk.sync
        def _(e):
            e.wait_ge(s_dbs, 16 * 32)
            e.dma_start(out=jnk[0:1, 0:4], in_=ident_in[0:1, 0:4]
                        ).then_inc(s_dbg, 16)

    if stop_phase <= 4:
        return finish()

    sfree("rbc", "srow", "rrow", "acc", "ptmp", "exp_sb", "ot_sb",
          "v_sb", "kt_sb", "qt_sb")
    x1_sb = salloc("x1_sb", [128, NRT * D], F32)
    wob = salloc("wob", [128, 2 * NDT * 512], BF16)

    @blk.gpsimd
    def _(e):
        for j in range(4):
            e.wait_ge(cc_ot[j], 1)
        for ht in range(NH):
            src = ot_ag.ap().rearrange("a (s t) -> (a s) t", s=2)
            e.indirect_dma_start(
                big_a[:, ht * 1024:(ht + 1) * 1024], None, src,
                bass.IndirectOffsetOnAxis(ap=poix[:, ht:ht + 1], axis=0),
            ).then_inc(s5_gp, 16)
            e.dma_start(out=jnk[1:2, 0:4], in_=ident_in[0:1, 0:4]
                        ).then_inc(s5_gf, 16)
        emit_grants(e, gp5, g5_ord, 1)

    @blk.sync
    def _(e):
        e.wait_ge(s4_div, 32)
        for qt in range(NRT):
            e.dma_start(out=x1_sb[:, qt * D:(qt + 1) * D],
                        in_=x_own[qt * 128:(qt + 1) * 128, :]).then_inc(s5_ld, 16)
            trailer(e, s5_fl)
        e.wait_ge(cc_wo, 1)
        for och in range(4):
            if och >= 2:
                e.wait_ge(s5_mm, 8 * (och - 1))
            for ht in range(NH):
                e.dma_start(
                    out=wob[:, (och % 2) * 8192 + ht * 512:
                            (och % 2) * 8192 + (ht + 1) * 512],
                    in_=wo_ag.ap()[ht * 128:(ht + 1) * 128,
                                   och * 512:(och + 1) * 512],
                ).then_inc(s5_ld, 16)
                trailer(e, s5_fl)

    @blk.tensor
    def _(e):
        e.wait_ge(s4_xr, SCHED[-1]["eg_after"])
        e.wait_ge(s4_div, 32)
        e.wait_ge(s5_gp, 16 * NH)
        e.wait_ge(s5_gf, 16 * NH)
        for och in range(4):
            e.wait_ge(s5_ld, 16 * (NRT + NH * (och + 1)))
            e.wait_ge(s5_fl, 16 * (NRT + NH * (och + 1)))
            for qt in range(NRT):
                m = och * NRT + qt
                if m >= LOG_RING:
                    e.wait_ge(s5_ev, m - LOG_RING + 1)
                ps = pslot(m)
                for ht in range(NH):
                    e.matmul(ps, big_a[:, ht * 1024 + qt * 128:
                                       ht * 1024 + (qt + 1) * 128],
                             wob[:, (och % 2) * 8192 + ht * 512:
                                 (och % 2) * 8192 + (ht + 1) * 512],
                             start=(ht == 0), stop=False)
                e.matmul(ps, ones_rb[:], bor[0:1, och * 512:(och + 1) * 512],
                         start=False, stop=True).then_inc(s5_mm, 1)

    @blk.vector
    def _(e):
        for och in range(4):
            for qt in range(NRT):
                m = och * NRT + qt
                if m < 30:
                    e.wait_ge(s5_mm, m + 2)
                else:
                    e.wait_ge(gp5, g5_idx[("mm", m)])
                if och == 0:
                    e.wait_ge(s5_ld, 16 * (qt + 1))
                    e.wait_ge(s5_fl, 16 * (qt + 1))
                sl = x1_sb[:, qt * D + och * 512:qt * D + (och + 1) * 512]
                e.tensor_tensor(sl, sl, pslot(m), ALU.add).then_inc(s5_ev, 1)

    if stop_phase <= 5:
        if "x1" in dbg_aps:
            @blk.sync
            def _(e):
                sp_selfb(e, s5_ev, 32)
                e.dma_start(out=dbg_aps["x1"], in_=x1_sb[:]).then_inc(s_dbg, 16)
        return finish()

    x2ln = [salloc(f"x2ln{i}", [128, D], F32) for i in range(2)]
    scr6 = [salloc(f"scr6{i}", [128, D], F32) for i in range(2)]
    t_mu = salloc("t_mu", [128, NRT], F32)
    t_muv = salloc("t_muv", [128, NRT], F32)
    t_m2 = salloc("t_m2", [128, NRT], F32)
    t_m2e = salloc("t_m2e", [128, NRT], F32)
    t_sq = salloc("t_sq", [128, NRT], F32)
    t_v = salloc("t_v", [128, NRT], F32)
    t_l = salloc("t_l", [128, NRT], F32)
    t_r = salloc("t_r", [128, NRT], F32)

    @blk.scalar
    def _(e):
        e.wait_ge(sfl_v, 4)
        e.activation(ascr[:], masks_sb[:, 0:1024], AF.Identity)
        e.activation(ascr[:], masks_sb[:, 0:1024], AF.Identity).then_inc(sfl_s, 1)
        e.wait_ge(sfl_v, 5)
        e.activation(ascr[:], masks_sb[:, 0:1024], AF.Identity)
        e.activation(ascr[:], masks_sb[:, 0:1024], AF.Identity)
        e.activation(t_l[:], t_v[:], AF.Ln)
        e.activation(ascr[:], masks_sb[:, 0:1024], AF.Identity)
        e.activation(ascr[:], masks_sb[:, 0:1024], AF.Identity)
        e.activation(t_r[:], t_l[:], AF.Exp, scale=-0.5)
        e.activation(ascr[:], masks_sb[:, 0:1024], AF.Identity)
        e.activation(ascr[:], masks_sb[:, 0:1024], AF.Identity).then_inc(s6_rs, 1)

    @blk.vector
    def _(e):
        for r in range(NRT):
            e.wait_ge(s5_ev, 24 + r + 1)
            e.tensor_tensor(scr6[r % 2][:], x1_sb[:, r * D:(r + 1) * D],
                            x1_sb[:, r * D:(r + 1) * D], ALU.mult)
            e.reduce_sum(t_mu[:, r:r + 1], x1_sb[:, r * D:(r + 1) * D],
                         axis=AX)
            if r < NRT - 1:
                e.reduce_sum(t_sq[:, r:r + 1], scr6[r % 2][:], axis=AX)
            else:
                e.reduce_sum(t_sq[:, r:r + 1], scr6[r % 2][:],
                             axis=AX).then_inc(sfl_v, 1)
        e.wait_ge(sfl_s, 3)
        e.tensor_scalar(t_muv[:], t_mu[:], 1.0 / D, None, ALU.mult)
        e.tensor_scalar(t_sqd[:], t_sq[:], 1.0 / D, None, ALU.mult)
        e.tensor_copy(bscr[:], scr6[0][:, 0:1024])
        e.tensor_copy(bscr[:], scr6[1][:, 0:1024])
        e.tensor_tensor(t_m2[:], t_muv[:], t_muv[:], ALU.mult)
        e.tensor_copy(bscr[:], scr6[0][:, 0:1024])
        e.tensor_copy(bscr[:], scr6[1][:, 0:1024])
        e.tensor_scalar(t_m2e[:], t_m2[:], EPS, None, ALU.subtract)
        e.tensor_copy(bscr[:], scr6[0][:, 0:1024])
        e.tensor_copy(bscr[:], scr6[1][:, 0:1024])
        e.tensor_tensor(t_v[:], t_sqd[:], t_m2e[:],
                        ALU.subtract).then_inc(sfl_v, 1)
        for rr in range(NRT + 2):
            p = rr - 1
            if 0 <= p < NRT:
                e.wait_ge(s6_rs, 1)
                if p >= 2:
                    e.wait_ge(s6_tp, 4 * (p - 1))
                e.tensor_scalar(x2ln[p % 2][:], x1_sb[:, p * D:(p + 1) * D],
                                t_muv[:, p:p + 1], t_r[:, p:p + 1],
                                ALU.subtract, ALU.mult).then_inc(s6_ln, 1)
                if p == NRT - 1:
                    e.tensor_copy(bscr[:], scr6[0][:, 0:1024]).then_inc(s6_ln, 1)
            rte = rr - 2
            if 0 <= rte < NRT:
                e.wait_ge(s5_mm, 24 + rte + 1)
                if rte == 0:
                    e.tensor_copy(bscr[:], scr6[1][:, 0:1024])
                for grp4 in range(4):
                    gidx = 4 * rte + grp4
                    bank = paux if gidx % 2 == 0 else pout
                    e.wait_ge(s6_tp, gidx + 1)
                    dt0 = grp4 * 4
                    dst = big_a[:].rearrange("p (dt t) -> p dt t", dt=NDT)[
                        :, dt0:dt0 + 4, rte * 128:(rte + 1) * 128]
                    srcv = bank[:].rearrange("p (j t) -> p j t", j=4)
                    e.tensor_copy(dst, srcv).then_inc(s6_te, 1)

    @blk.tensor
    def _(e):
        for rt in range(NRT):
            e.wait_ge(s6_ln, rt + 2)
            for grp4 in range(4):
                gidx = 4 * rt + grp4
                bank = paux if gidx % 2 == 0 else pout
                if gidx >= 2:
                    e.wait_ge(s6_te, gidx - 1)
                elif rt == 0 and grp4 == 0:
                    e.wait_ge(s4_div, 32)
                for j in range(4):
                    dt = grp4 * 4 + j
                    ins = e.transpose(bank[:, j * 128:(j + 1) * 128],
                                      x2ln[rt % 2][:, dt * 128:(dt + 1) * 128],
                                      ident[:])
                    if j == 3:
                        ins.then_inc(s6_tp, 1)

    @blk.sync
    def _(e):
        e.wait_ge(cc_bw, 1)
        e.wait_ge(s5_mm, 32)
        for och in range(4):
            if och >= 2:
                e.wait_ge(s7_mm, 8 * (och - 1))
            for dt in range(NDT):
                for rr in range(2):
                    r = 2 * och + rr
                    e.dma_start(
                        out=wob[:, (och % 2) * 8192 + dt * 512 + rr * 256:
                                (och % 2) * 8192 + dt * 512 + (rr + 1) * 256],
                        in_=bwt_ag.ap()[r * D + dt * 128:
                                        r * D + (dt + 1) * 128, :],
                    ).then_inc(s7_ld, 16)
                    trailer(e, s7_fl)

    @blk.tensor
    def _(e):
        e.wait_ge(s5_ev, 32)
        for och in range(4):
            e.wait_ge(s7_ld, 16 * 32 * (och + 1))
            e.wait_ge(s7_fl, 16 * 32 * (och + 1))
            for qt in range(NRT):
                m = och * NRT + qt
                if och == 0:
                    if qt < NRT - 1:
                        e.wait_ge(s6_te, 4 * qt + 5)
                    else:
                        e.wait_ge(gp6, g6_idx[("te6", qt)])
                if m >= LOG_RING:
                    e.wait_ge(s7_ev, m - LOG_RING + 1)
                ps = pslot(m)
                for dt in range(NDT):
                    e.matmul(ps, big_a[:, dt * 1024 + qt * 128:
                                       dt * 1024 + (qt + 1) * 128],
                             wob[:, (och % 2) * 8192 + dt * 512:
                                 (och % 2) * 8192 + (dt + 1) * 512],
                             start=(dt == 0), stop=False)
                e.matmul(ps, ones_rb[:], bfr[0:1, och * 512:(och + 1) * 512],
                         start=False, stop=True).then_inc(s7_mm, 1)

    @blk.vector
    def _(e):
        for och in range(4):
            for qt in range(NRT):
                m = och * NRT + qt
                if m < 30:
                    e.wait_ge(s7_mm, m + 2)
                else:
                    e.wait_ge(gp7, g7_idx[("mm", m)])
                sl = x1_sb[:, qt * D + och * 512:qt * D + (och + 1) * 512]
                e.tensor_tensor(sl, sl, pslot(m), ALU.add).then_inc(s7_ev, 1)

    @blk.gpsimd
    def _(e):
        emit_grants(e, gp6, g6_ord, 1)
        emit_grants(e, gp7, g7_ord, 1)

    @blk.sync
    def _(e):
        for qt in range(NRT):
            if qt < NRT - 2:
                e.wait_ge(s7_ev, 24 + qt + 3)
            else:
                sp_selfb(e, s7_ev, 32)
            e.dma_start(out=y[qt * 128:(qt + 1) * 128, :],
                        in_=x1_sb[:, qt * D:(qt + 1) * D]).then_inc(s_st, 16)

    return finish(extra_waits=[(s_st, 16 * NRT)])



def make_in_maps(inputs):
    x = np.asarray(inputs["L0_attn_x"])
    wq = np.ascontiguousarray(np.asarray(inputs["L0_attn_Wq"]), dtype=np.float32)
    wk = np.asarray(inputs["L0_attn_Wk"])
    wv = np.asarray(inputs["L0_attn_Wv"])
    wo = np.asarray(inputs["L0_attn_Wo"])
    wf = np.asarray(inputs["L0_ffn_weight"])
    bq = np.asarray(inputs["L0_attn_bq"])
    bk = np.asarray(inputs["L0_attn_bk"])
    bv = np.asarray(inputs["L0_attn_bv"])
    bo = np.asarray(inputs["L0_attn_bo"])
    bf = np.asarray(inputs["L0_ffn_bias"])

    xf = np.asarray(x).reshape(T, D)
    masks = np.zeros((128, 4 * QB), dtype=ml_dtypes.bfloat16)
    p = np.arange(128)[:, None]
    i = np.arange(QB)[None, :]
    for jd in range(4):
        masks[:, jd * QB:(jd + 1) * QB] = np.where(
            i >= jd * 128 + p, 0.0, -30000.0).astype(ml_dtypes.bfloat16)
    ident = np.eye(128, dtype=np.float32)
    identb = np.eye(128, dtype=ml_dtypes.bfloat16)
    ones_r = np.ones((1, 128), np.float32)
    ones_c = np.ones((128, 1), np.float32)
    bq_t = np.ascontiguousarray(np.asarray(bq).reshape(NDT, 128).T,
                                dtype=np.float32)
    bo_r = np.ascontiguousarray(np.asarray(bo).reshape(1, D), dtype=np.float32)
    bf_r = np.ascontiguousarray(np.asarray(bf).reshape(1, D), dtype=np.float32)

    in_maps = []
    for c in range(NC):
        d = np.arange(128)[:, None]
        ht = np.arange(NH)[None, :]
        po_idx = (((c // 2) * 2048 + (ht // 2) * 256 + (ht % 2) * 128 + d) * 2
                  + (c % 2)).astype(np.uint32)
        in_maps.append({
            "x_own": np.ascontiguousarray(xf[c * TL:(c + 1) * TL, :],
                                          dtype=np.float32),
            "wq": wq,
            "wk_sl": np.ascontiguousarray(wk[:, c * 256:(c + 1) * 256],
                                          dtype=np.float32),
            "wv_sl": np.ascontiguousarray(wv[:, c * 256:(c + 1) * 256],
                                          dtype=np.float32),
            "wo_sl": np.ascontiguousarray(wo[c * 256:(c + 1) * 256, :],
                                          dtype=np.float32),
            "wf_sl": np.ascontiguousarray(wf[c * 256:(c + 1) * 256, :],
                                          dtype=np.float32),
            "bq_t": bq_t,
            "bk_t": np.ascontiguousarray(
                np.asarray(bk)[c * 256:(c + 1) * 256].reshape(HPC, 128).T,
                dtype=np.float32),
            "bv_t": np.ascontiguousarray(
                np.asarray(bv)[c * 256:(c + 1) * 256].reshape(HPC, 128).T,
                dtype=np.float32),
            "bo_r": bo_r,
            "bf_r": bf_r,
            "masks": masks,
            "ident": ident,
            "identb": identb,
            "ones_r": ones_r,
            "ones_c": ones_c,
            "po_idx": np.ascontiguousarray(po_idx),
        })
    return in_maps


def kernel(**inputs):
    nc, _ = build_nc()
    in_maps = make_in_maps(inputs)
    res = run_bass_kernel_spmd(nc, in_maps, core_ids=list(range(NC)))
    out = np.concatenate([np.asarray(res.results[c]["y"]) for c in range(NC)],
                         axis=0)
    return out.reshape(1, T, D).astype(np.float32, copy=False)


# version 11
# speedup vs baseline: 1.1348x; 1.1348x over previous
import numpy as np
import ml_dtypes
import concourse.bass as bass
import concourse.mybir as mybir
from concourse.bass_utils import run_bass_kernel_spmd

F32 = mybir.dt.float32
BF16 = mybir.dt.bfloat16
U32 = mybir.dt.uint32
AF = mybir.ActivationFunctionType
ALU = mybir.AluOpType
AX = mybir.AxisListType.X

NC = 8
D = 2048
T = 8192
TL = T // NC
DH = 128
NH = 16
HPC = NH // NC
NDT = D // 128
QB = 512
NQB = T // QB
NRT = TL // 128
RG = [list(range(NC))]
EPS = 1e-5
SM_SCALE = float(np.float32(1.0) / np.sqrt(np.float32(DH)))
LOG_RING = 6
EXP_RING = 12
GRP = 3


def _attention_schedule():
    blocks = []
    g = 0
    outmm = pairs = masks = groups_done = 0
    outmm_of, pair_of = {}, {}
    for qb in range(NQB):
        for h in range(HPC):
            if g % GRP:
                g += GRP - (g % GRP)
            nt = 4 * (qb + 1)
            g0 = g
            tiles = []
            for i in range(nt):
                gg = g0 + i
                diag = i >= nt - 4
                mi = None
                if diag:
                    masks += 1
                    mi = masks
                outmm += 1
                outmm_of[gg] = outmm
                tiles.append(dict(g=gg, kvt=i, diag=diag, mask_idx=mi,
                                  out_idx=outmm, esl=gg % EXP_RING))
            grps = []
            for i in range(0, nt, GRP):
                sub = tiles[i:i + GRP]
                grps.append(dict(gseq=groups_done, tiles=sub,
                                 slot0=sub[0]["g"] % LOG_RING,
                                 esl0=sub[0]["g"] % EXP_RING))
                for tt in sub:
                    tt["gseq"] = groups_done
                groups_done += 1
            prs = []
            for p in range(nt // 2):
                t0, t1 = tiles[2 * p], tiles[2 * p + 1]
                pairs += 1
                pair_of[t0["g"]] = pairs
                pair_of[t1["g"]] = pairs
                t0["pair_idx"] = pairs
                t1["pair_idx"] = pairs
                prs.append(dict(t0=t0, t1=t1, idx=pairs))
            blocks.append(dict(qb=qb, h=h, b=len(blocks), nt=nt, grps=grps,
                               tiles=tiles, prs=prs, groups_after=groups_done,
                               outmm_after=outmm, pairs_after=pairs,
                               masks_after=masks))
            g = g0 + nt
    ecount = 0
    allg = []
    for blk_ in blocks:
        for grp in blk_["grps"]:
            ecount += 1
            grp["eg"] = ecount
            for tt in grp["tiles"]:
                tt["eg"] = ecount
            allg.append(grp)
        ecount += 3
        blk_["eg_after"] = ecount
    for gi, grp in enumerate(allg):
        grp["war_eg"] = allg[gi - 2]["eg"] if gi >= 2 else 0
    for blk_ in blocks:
        for grp in blk_["grps"]:
            wo = wp = 0
            for tt in grp["tiles"]:
                prev = tt["g"] - EXP_RING
                wo = max(wo, outmm_of.get(prev, 0))
                wp = max(wp, pair_of.get(prev, 0))
            grp["war_out"] = wo
            grp["war_pair"] = wp
    return blocks


SCHED = _attention_schedule()


def pair_of_tile(blko, tt):
    return tt["pair_idx"]

N_GROUPS = SCHED[-1]["groups_after"]
N_OUTMM = SCHED[-1]["outmm_after"]
N_PAIRS = SCHED[-1]["pairs_after"]


def build_nc(stop_phase=9, dbg=()):
    nc = bass.Bass("TRN2", target_bir_lowering=False, debug=False,
                   detect_race_conditions=False)

    x_own = nc.dram_tensor("x_own", [TL, D], F32, kind="ExternalInput").ap()
    wq = nc.dram_tensor("wq", [D, D], F32, kind="ExternalInput").ap()
    wk_sl = nc.dram_tensor("wk_sl", [D, HPC * DH], F32, kind="ExternalInput").ap()
    wv_sl = nc.dram_tensor("wv_sl", [D, HPC * DH], F32, kind="ExternalInput").ap()
    wo_sl = nc.dram_tensor("wo_sl", [D // NC, D], F32, kind="ExternalInput").ap()
    wf_sl = nc.dram_tensor("wf_sl", [D // NC, D], F32, kind="ExternalInput").ap()
    bq_t = nc.dram_tensor("bq_t", [128, NDT], F32, kind="ExternalInput").ap()
    bk_t = nc.dram_tensor("bk_t", [128, HPC], F32, kind="ExternalInput").ap()
    bv_t = nc.dram_tensor("bv_t", [128, HPC], F32, kind="ExternalInput").ap()
    bo_r = nc.dram_tensor("bo_r", [1, D], F32, kind="ExternalInput").ap()
    bf_r = nc.dram_tensor("bf_r", [1, D], F32, kind="ExternalInput").ap()
    masks_in = nc.dram_tensor("masks", [128, 4 * QB], BF16, kind="ExternalInput").ap()
    ident_in = nc.dram_tensor("ident", [128, 128], F32, kind="ExternalInput").ap()
    identb_in = nc.dram_tensor("identb", [128, 128], BF16, kind="ExternalInput").ap()
    ones_r_in = nc.dram_tensor("ones_r", [1, 128], F32, kind="ExternalInput").ap()
    ones_c_in = nc.dram_tensor("ones_c", [128, 1], F32, kind="ExternalInput").ap()
    po_idx_in = nc.dram_tensor("po_idx", [128, NH], U32, kind="ExternalInput").ap()

    y = nc.dram_tensor("y", [TL, D], F32, kind="ExternalOutput").ap()

    ln_stage = [nc.dram_tensor(f"ln_stage{k}", [D, QB], BF16) for k in range(2)]
    ln_ag = [nc.dram_tensor(f"ln_ag{k}", [NC * D, QB], BF16,
                            addr_space="Shared") for k in range(2)]
    wo_stage = nc.dram_tensor("wo_stage", [D // NC, D], BF16)
    wo_ag = nc.dram_tensor("wo_ag", [D, D], BF16, addr_space="Shared")
    bwt_stage = nc.dram_tensor("bwt_stage", [D, D // NC], BF16)
    bwt_ag = nc.dram_tensor("bwt_ag", [NC * D, D // NC], BF16,
                            addr_space="Shared")
    ot_stage = [nc.dram_tensor(f"ot_stage{j}", [HPC * DH, 4 * QB], BF16)
                for j in range(4)]
    ot_ag = nc.dram_tensor("ot_ag", [T, 4 * QB], BF16,
                           addr_space="Shared")

    dbg_aps = {}

    def dbg_t(name, shape, dtype):
        if name in dbg:
            dbg_aps[name] = nc.dram_tensor("dbg_" + name, shape, dtype,
                                           kind="ExternalOutput").ap()

    dbg_t("lnxt", [128, NDT * TL], BF16)
    dbg_t("st", [128, 7 * NRT], F32)
    dbg_t("qt", [128, HPC * T], BF16)
    dbg_t("kt", [128, HPC * T], BF16)
    dbg_t("v", [128, HPC * T], BF16)
    dbg_t("ot", [128, HPC * T], BF16)
    dbg_t("bwt", [128, NDT * 256], BF16)
    dbg_t("bal", [128, 8], F32)
    dbg_t("st2", [128, 8 * NRT], F32)
    dbg_t("sums", [32, 512], F32)
    dbg_t("kt4", [128, 2 * 8192], BF16)
    dbg_t("v4", [128, 2 * 8192], BF16)
    dbg_t("qt4", [128, 2 * 8192], BF16)
    dbg_t("x1", [128, NRT * D], F32)

    cms = []
    cm = {}

    def salloc(name, shape, dtype, side="left"):
        c = nc.sbuf_tensor(name, shape, dtype, side=side)
        t = c.__enter__()
        cm[name] = c
        return t

    def sfree(*names):
        for n in names:
            cm.pop(n).__exit__(None, None, None)

    big_a = salloc("big_a", [128, NDT * TL], BF16)
    masks_sb = salloc("masks_sb", [128, 4 * QB], BF16)
    ident = salloc("ident_sb", [128, 128], F32)
    identB = salloc("identB_sb", [128, 128], BF16)
    ones_r = salloc("ones_r_sb", [1, 128], F32)
    ones_rb = salloc("ones_rb_sb", [1, 128], BF16)
    ones_c = salloc("ones_c_sb", [128, 1], F32)
    bqt = salloc("bqt_sb", [128, NDT], F32)
    bkt = salloc("bkt_sb", [128, HPC], F32)
    bvt = salloc("bvt_sb", [128, HPC], F32)
    bor = salloc("bor_sb", [1, D], BF16)
    bfr = salloc("bfr_sb", [1, D], BF16)
    poix = salloc("poix_sb", [128, NH], U32)
    st_mu = salloc("st_mu", [128, NRT], F32)
    st_muv = salloc("st_muv", [128, NRT], F32)
    st_m2 = salloc("st_m2", [128, NRT], F32)
    st_m2e = salloc("st_m2e", [128, NRT], F32)
    st_sq = salloc("st_sq", [128, NRT], F32)
    st_v = salloc("st_v", [128, NRT], F32)
    st_l = salloc("st_l", [128, NRT], F32)
    st_r = salloc("st_r", [128, NRT], F32)
    st_sqd = salloc("st_sqd", [128, NRT], F32)
    t_sqd = salloc("t_sqd", [128, NRT], F32)
    jnk = salloc("jnk", [128, 4], F32)
    jnk2 = salloc("jnk2", [128, 4], F32)
    jexp = salloc("jexp", [128, 512], BF16)
    bscr = salloc("bscr", [128, 1024], F32)
    ascr = salloc("ascr", [128, 1024], F32)

    pA_cm = nc.psum_tensor("pA", [128, 3 * 512], F32)
    pB_cm = nc.psum_tensor("pB", [128, 3 * 512], F32)
    pout_cm = nc.psum_tensor("pout", [128, 512], F32)
    paux_cm = nc.psum_tensor("paux", [128, 512], F32)
    pA, pB = pA_cm.__enter__(), pB_cm.__enter__()
    pout, paux = pout_cm.__enter__(), paux_cm.__enter__()
    cms += [pA_cm, pB_cm, pout_cm, paux_cm]

    def pslot(i):
        i = i % LOG_RING
        t = pA if i < 3 else pB
        return t[:, (i % 3) * 512:((i % 3) + 1) * 512]

    def pgroup(slot0, w):
        t = pA if slot0 == 0 else pB
        return t[:, :w]

    semcms = []

    def sem(name):
        c = nc.semaphore(name)
        s = c.__enter__()
        semcms.append(c)
        return s

    s0_ld = sem("s0_ld"); s0_ca = sem("s0_ca")
    s1_ld = sem("s1_ld"); s1_sq = sem("s1_sq"); s1_s2 = sem("s1_s2")
    s1_red = sem("s1_red"); s1_rs = sem("s1_rs")
    s1_st = sem("s1_st"); s1_ln = sem("s1_ln"); s1_tp = sem("s1_tp")
    s1_te = sem("s1_te"); s1_gp = sem("s1_gp")
    cc_ln = [sem("cc_ln0"), sem("cc_ln1")]
    s2_ld = sem("s2_ld"); s2_ca = sem("s2_ca"); s2_mm = sem("s2_mm")
    s2_ev = sem("s2_ev")
    sw_ld = sem("sw_ld"); sw_vx = sem("sw_vx"); sw_tp = sem("sw_tp")
    sw_te = sem("sw_te"); sw_gp = sem("sw_gp")
    cc_wo = sem("cc_wo"); cc_bw = sem("cc_bw")
    s3_ld = sem("s3_ld"); s3_mm = sem("s3_mm"); s3_ev = sem("s3_ev")
    s3_evV = sem("s3_evV"); s3_tp = sem("s3_tp"); s3_te = sem("s3_te")
    s4_log = sem("s4_log")
    s4_xr = sem("s4_xr")
    s4_acc = sem("s4_acc")
    s3_eg = sem("s3_eg"); s3_tg = sem("s3_tg")
    s4_wr = sem("s4_wr"); s4_warm = sem("s4_warm"); s_chain = sem("s_chain")
    gp1 = sem("gp1"); gp2 = sem("gp2"); gpw = sem("gpw"); gp3 = sem("gp3")
    g4b = sem("g4b"); gp5 = sem("gp5"); gp6 = sem("gp6"); gp7 = sem("gp7")
    ga2 = sem("ga2"); ga3 = sem("ga3"); s_stb = sem("s_stb")
    s0_fl = sem("s0_fl"); s1_fl = sem("s1_fl"); s2_fl = sem("s2_fl")
    s3_fl = sem("s3_fl"); sw_fl = sem("sw_fl"); s5_fl = sem("s5_fl")
    s7_fl = sem("s7_fl"); s5_gf = sem("s5_gf")
    s3_lk = sem("s3_lk"); s3_lv = sem("s3_lv"); s3_lb = sem("s3_lb")
    s2_lq = sem("s2_lq")
    s4_prs = sem("s4_prs"); s4_out = sem("s4_out"); s4_sum = sem("s4_sum")
    s4_se = sem("s4_se")
    s4_rcp = sem("s4_rcp"); s4_bct = sem("s4_bct"); s4_div = sem("s4_div")
    s4_gp = sem("s4_gp")
    cc_ot = [sem(f"cc_ot{j}") for j in range(4)]
    s5_ld = sem("s5_ld"); s5_gp = sem("s5_gp"); s5_mm = sem("s5_mm")
    s5_ev = sem("s5_ev")
    s6_rs = sem("s6_rs"); s6_ln = sem("s6_ln"); s6_tp = sem("s6_tp"); s6_te = sem("s6_te")
    s7_ld = sem("s7_ld"); s7_mm = sem("s7_mm"); s7_ev = sem("s7_ev")
    s_st = sem("s_st")
    s_dbg = sem("s_dbg")
    sfl_v = sem("sfl_v"); sfl_s = sem("sfl_s")

    def _mkord(entries):
        return entries, {k: 16 * (i + 1) for i, (k, _, _) in enumerate(entries)}

    _o = []
    for _p in range(NRT):
        _o.append((("ln", _p), s1_ln, _p + 1))
        for _g in range(4 * _p, 4 * _p + 4):
            _o.append((("tp", _g), s1_tp, _g + 1))
        if _p == 3:
            _o.append((("te", 0), s1_te, 16))
    _o.append((("te", 1), s1_te, 32))
    g1_ord, g1_idx = _mkord(_o)
    g1a_ord = g1_ord[:21]
    g1b_ord = g1_ord[21:]

    g2_ord, g2_idx = _mkord([(("ca", _c), s2_ca, 2 * _c + 2)
                             for _c in range(NDT)])

    _o = [(("bw", 0), sw_vx, 4)]
    _o += [(("tp", _k), sw_tp, _k + 1) for _k in range(4)]
    _o.append((("bw", 1), sw_vx, 6))
    _o += [(("tp", _k), sw_tp, _k + 1) for _k in range(4, 8)]
    gw_ord, gw_idx = _mkord(_o)

    _o = [(("wv",), sw_vx, 14)]
    for _i in range(16):
        for _h in range(HPC):
            _o.append((("vt", _i, _h), s3_mm, 4 * _i + 2 + _h + 1))
        for _h in range(HPC):
            _o.append((("vf", _i, _h), s3_evV, 2 * _i + _h + 1))
        for _h in range(HPC):
            _o.append((("tp3", _i, _h), s3_tp, 2 * _i + _h + 1))
        for _h in range(HPC):
            _o.append((("ev", _i, _h), s3_ev, 2 * _i + _h + 1))
    g3_ord, g3_idx = _mkord(_o)

    g5_ord, g5_idx = _mkord([(("mm", _m), s5_mm, _m + 1) for _m in range(32)])

    _o = []
    for _p in range(NRT):
        _o.append((("ln6", _p), s6_ln, _p + 1))
        for _g in range(4 * _p, 4 * _p + 4):
            _o.append((("tp6", _g), s6_tp, _g + 1))
        _o.append((("te6", _p), s6_te, 4 * _p + 4))
    g6_ord, g6_idx = _mkord(_o)

    g7_ord, g7_idx = _mkord([(("mm", _m), s7_mm, _m + 1) for _m in range(32)])

    def emit_grants(e, gsem, entries, row):
        for _, raw, val in entries:
            e.wait_ge(raw, val)
            e.dma_start(out=jnk[row:row + 1, 0:4], in_=ident_in[0:1, 0:4]
                        ).then_inc(gsem, 16)

    def trailer(e, fl):
        e.dma_start(out=jnk[3:4, 0:4], in_=ident_in[0:1, 0:4]).then_inc(fl, 16)

    gsb_n = [0]
    spb_n = [0]

    def sp_selfb(e, raw, val):
        e.wait_ge(raw, val)
        e.dma_start(out=jnk[3:4, 0:4], in_=ident_in[0:1, 0:4]).then_inc(s_stb, 16)
        spb_n[0] += 16
        e.wait_ge(s_stb, spb_n[0])

    def gp_selfb(e, raw, val):
        e.wait_ge(raw, val)
        e.dma_start(out=jnk[1:2, 0:4], in_=ident_in[0:1, 0:4]).then_inc(g4b, 16)
        gsb_n[0] += 16
        e.wait_ge(g4b, gsb_n[0])

    N_P0C = 9
    blk_cm = nc.Block()
    blk = blk_cm.__enter__()

    def finish(extra_waits=()):
        ndbg = len(dbg_aps)

        @blk.gpsimd
        def _(e):
            for s, v in extra_waits:
                e.wait_ge(s, v)
            if ndbg:
                e.wait_ge(s_dbg, 16 * ndbg)

        blk_cm.__exit__(None, None, None)
        return nc, dbg_aps

    btmp = salloc("btmp", [1, D], F32)

    @blk.sync
    def _(e):
        p0list = [(masks_sb, masks_in), (ident, ident_in),
                  (identB, identb_in),
                  (ones_r, ones_r_in), (ones_c, ones_c_in),
                  (bqt, bq_t), (bkt, bk_t), (bvt, bv_t),
                  (poix, po_idx_in)]
        assert len(p0list) == N_P0C
        for dst, src in p0list:
            e.dma_start(out=dst[:], in_=src).then_inc(s0_ld, 16)
            trailer(e, s0_fl)
        e.dma_start(out=btmp[:], in_=bo_r).then_inc(s0_ld, 16)
        trailer(e, s0_fl)
        sp_selfb(e, s0_ca, 1)
        e.dma_start(out=btmp[:], in_=bf_r).then_inc(s0_ld, 16)
        trailer(e, s0_fl)

    @blk.vector
    def _(e):
        e.wait_ge(s0_ld, 16 * (N_P0C + 1))
        e.wait_ge(s0_fl, 16 * (N_P0C + 1))
        e.tensor_copy(bor[:], btmp[:]).then_inc(s0_ca, 1)
        e.wait_ge(s0_ld, 16 * (N_P0C + 2))
        e.wait_ge(s0_fl, 16 * (N_P0C + 2))
        e.tensor_copy(bfr[:], btmp[:]).then_inc(s0_ca, 1)
        e.tensor_copy(ones_rb[:], ones_r[:]).then_inc(s0_ca, 1)

    sfree("btmp")
    qt_sb = salloc("qt_sb", [128, HPC * T], BF16)
    kt_sb = salloc("kt_sb", [128, HPC * T], BF16)
    v_sb = salloc("v_sb", [128, HPC * T], BF16)
    xstg = [salloc(f"xstg{i}", [128, D], F32) for i in range(2)]
    xln = [salloc(f"xln{i}", [128, D], F32) for i in range(2)]
    scr1 = [salloc(f"scr1{i}", [128, D], F32) for i in range(2)]

    @blk.sync
    def _(e):
        for rt in range(NRT):
            if rt >= 2:
                sp_selfb(e, s1_red, rt - 1)
            e.dma_start(out=xstg[rt % 2][:],
                        in_=x_own[rt * 128:(rt + 1) * 128, :]).then_inc(s1_ld, 16)
            trailer(e, s1_fl)
        for rt in range(NRT):
            if rt == 0:
                e.wait_ge(s1_red, NRT)
                e.wait_ge(sfl_s, 1)
            if rt >= 2:
                sp_selfb(e, s1_ln, rt - 1)
            e.dma_start(out=xstg[rt % 2][:],
                        in_=x_own[rt * 128:(rt + 1) * 128, :]).then_inc(s1_ld, 16)
            trailer(e, s1_fl)

    @blk.scalar
    def _(e):
        e.wait_ge(sfl_v, 1)
        e.activation(ascr[:], masks_sb[:, 0:1024], AF.Identity)
        e.activation(ascr[:], masks_sb[:, 0:1024], AF.Identity).then_inc(sfl_s, 1)
        e.wait_ge(sfl_v, 2)
        e.activation(ascr[:], masks_sb[:, 0:1024], AF.Identity)
        e.activation(ascr[:], masks_sb[:, 0:1024], AF.Identity)
        e.activation(st_l[:], st_v[:], AF.Ln)
        e.activation(ascr[:], masks_sb[:, 0:1024], AF.Identity)
        e.activation(ascr[:], masks_sb[:, 0:1024], AF.Identity)
        e.activation(st_r[:], st_l[:], AF.Exp, scale=-0.5)
        e.activation(ascr[:], masks_sb[:, 0:1024], AF.Identity)
        e.activation(ascr[:], masks_sb[:, 0:1024], AF.Identity).then_inc(s1_rs, 1)

    @blk.vector
    def _(e):
        for r in range(NRT):
            e.wait_ge(s1_ld, 16 * (r + 1))
            e.wait_ge(s1_fl, 16 * (r + 1))
            e.tensor_tensor(scr1[r % 2][:], xstg[r % 2][:], xstg[r % 2][:],
                            ALU.mult)
            e.reduce_sum(st_mu[:, r:r + 1], xstg[r % 2][:],
                         axis=AX).then_inc(s1_red, 1)
            if r < NRT - 1:
                e.reduce_sum(st_sq[:, r:r + 1], scr1[r % 2][:], axis=AX)
            else:
                e.reduce_sum(st_sq[:, r:r + 1], scr1[r % 2][:],
                             axis=AX).then_inc(sfl_v, 1)
        e.wait_ge(sfl_s, 1)
        e.tensor_scalar(st_muv[:], st_mu[:], 1.0 / D, None, ALU.mult)
        e.tensor_scalar(st_sqd[:], st_sq[:], 1.0 / D, None, ALU.mult)
        e.tensor_copy(bscr[:], scr1[0][:, 0:1024])
        e.tensor_copy(bscr[:], scr1[1][:, 0:1024])
        e.tensor_tensor(st_m2[:], st_muv[:], st_muv[:], ALU.mult)
        e.tensor_copy(bscr[:], scr1[0][:, 0:1024])
        e.tensor_copy(bscr[:], scr1[1][:, 0:1024])
        e.tensor_scalar(st_m2e[:], st_m2[:], EPS, None, ALU.subtract)
        e.tensor_copy(bscr[:], scr1[0][:, 0:1024])
        e.tensor_copy(bscr[:], scr1[1][:, 0:1024])
        e.tensor_tensor(st_v[:], st_sqd[:], st_m2e[:],
                        ALU.subtract).then_inc(sfl_v, 1)
        for rr in range(NRT + 2):
            rte = rr - 2
            if 0 <= rte < NRT:
                for grp4 in range(4):
                    gidx = 4 * rte + grp4
                    bank = paux if gidx % 2 == 0 else pout
                    e.wait_ge(gp1, g1_idx[("tp", gidx)])
                    dt0 = grp4 * 4
                    dst = big_a[:].rearrange("p (dt t) -> p dt t", dt=NDT)[
                        :, dt0:dt0 + 4, rte * 128:(rte + 1) * 128]
                    srcv = bank[:].rearrange("p (j t) -> p j t", j=4)
                    e.tensor_copy(dst, srcv).then_inc(s1_te, 1)
            p = rr - 1
            if 0 <= p < NRT:
                e.wait_ge(s1_ld, 16 * (NRT + p + 1))
                e.wait_ge(s1_fl, 16 * (NRT + p + 1))
                e.wait_ge(s1_rs, 1)
                if p >= 2:
                    e.wait_ge(s1_tp, 4 * (p - 1))
                e.tensor_scalar(xln[p % 2][:], xstg[p % 2][:],
                                st_muv[:, p:p + 1], st_r[:, p:p + 1],
                                ALU.subtract, ALU.mult).then_inc(s1_ln, 1)

    @blk.tensor
    def _(e):
        for rt in range(NRT):
            e.wait_ge(gp1, g1_idx[("ln", rt)])
            for grp4 in range(4):
                gidx = 4 * rt + grp4
                bank = paux if gidx % 2 == 0 else pout
                if gidx >= 2:
                    e.wait_ge(s1_te, gidx - 1)
                for j in range(4):
                    dt = grp4 * 4 + j
                    last = (j == 3)
                    ins = e.transpose(bank[:, j * 128:(j + 1) * 128],
                                      xln[rt % 2][:, dt * 128:(dt + 1) * 128],
                                      ident[:])
                    if last:
                        ins.then_inc(s1_tp, 1)

    @blk.gpsimd
    def _(e):
        for k in range(2):
            emit_grants(e, gp1, g1a_ord if k == 0 else g1b_ord, 1)
            gp_selfb(e, s1_te, 16 * (k + 1))
            for dt in range(NDT):
                e.dma_start(
                    out=ln_stage[k].ap()[dt * 128:(dt + 1) * 128, :],
                    in_=big_a[:, dt * TL + k * 512:dt * TL + (k + 1) * 512],
                ).then_inc(s1_gp, 16)
            e.wait_ge(s1_gp, 16 * NDT * (k + 1))
            e.collective_compute(
                "AllGather", ALU.bypass, replica_groups=RG,
                ins=[ln_stage[k].ap().opt()],
                outs=[ln_ag[k].ap().opt()],
            ).then_inc(cc_ln[k])

    if "lnxt" in dbg_aps:
        @blk.sync
        def _(e):
            e.wait_ge(s1_te, 32)
            e.dma_start(out=dbg_aps["lnxt"], in_=big_a[:]).then_inc(s_dbg, 16)

    if "st" in dbg_aps:
        @blk.sync
        def _(e):
            e.wait_ge(s1_te, 32)
            for ii, tile in enumerate([st_mu, st_muv, st_m2, st_m2e, st_sq,
                                       st_v, st_r]):
                e.dma_start(out=dbg_aps["st"][:, ii * NRT:(ii + 1) * NRT],
                            in_=tile[:]).then_inc(s_dbg, 16)
    if "st2" in dbg_aps:
        @blk.sync
        def _(e):
            e.wait_ge(s1_te, 32)
            e.dma_start(out=dbg_aps["st2"][:, 0:NRT],
                        in_=st_sqd[:]).then_inc(s_dbg, 16)

    if stop_phase <= 1:
        return finish()

    sfree("scr11", "scr10", "xln1", "xln0", "xstg1", "xstg0")
    wqstg = salloc("wqstg", [128, 8 * 128], F32)
    wqB = [salloc(f"wqB{i}", [128, D], BF16) for i in range(2)]

    @blk.sync
    def _(e):
        for ct in range(NDT):
            for half in range(2):
                if ct == 0 and half == 0:
                    e.wait_ge(s1_ln, NRT)
                    e.wait_ge(s1_tp, 4 * NRT)
                if ct > 0 or half > 0:
                    sp_selfb(e, s2_ca, 2 * ct + half)
                for j in range(8):
                    r0 = (half * 8 + j) * 128
                    e.dma_start(out=wqstg[:, j * 128:(j + 1) * 128],
                                in_=wq[r0:r0 + 128, ct * 128:(ct + 1) * 128]
                                ).then_inc(s2_lq, 16)
                e.dma_start(out=jnk[3:4, 0:4], in_=ident_in[0:1, 0:4]
                            ).then_inc(s2_ld, 16)
                trailer(e, s2_fl)

    @blk.vector
    def _(e):
        for ct in range(NDT):
            for half in range(2):
                e.wait_ge(s2_lq, 16 * 8 * (2 * ct + half + 1))
                e.wait_ge(s2_ld, 16 * (2 * ct + half + 1))
                e.wait_ge(s2_fl, 16 * (2 * ct + half + 1))
                if ct >= 2 and half == 0:
                    e.wait_ge(s2_mm, 2 * (ct - 1))
                e.tensor_copy(wqB[ct % 2][:, half * 1024:(half + 1) * 1024],
                              wqstg[:]).then_inc(s2_ca, 1)

    @blk.tensor
    def _(e):
        for ct in range(NDT):
            for tch in range(2):
                slot = (0 if ct % 2 == 0 else 3) + tch
                if ct == 0 and tch == 0:
                    e.wait_ge(s0_ca, 3)
                    e.wait_ge(gp1, g1_idx[("te", 0)])
                if ct == 0 and tch == 1:
                    e.wait_ge(gp1, g1_idx[("te", 1)])
                if tch == 0:
                    e.wait_ge(gp2, g2_idx[("ca", ct)])
                    if ct >= 2:
                        e.wait_ge(s2_ev, min(2 * (ct - 2) + 3, 32))
                ps = pslot(slot)
                for dt in range(NDT):
                    ins = e.matmul(ps, wqB[ct % 2][:, dt * 128:(dt + 1) * 128],
                                   big_a[:, dt * TL + tch * 512:
                                         dt * TL + (tch + 1) * 512],
                                   start=(dt == 0), stop=(dt == NDT - 1))
                    if dt == NDT - 1:
                        ins.then_inc(s2_mm, 1)

    @blk.scalar
    def _(e):
        for ct in range(NDT):
            for tch in range(2):
                slot = (0 if ct % 2 == 0 else 3) + tch
                k = 2 * ct + tch + 1
                if k <= 31:
                    e.wait_ge(s2_mm, k + 1)
                else:
                    e.wait_ge(s2_mm, 32)
                    e.dma_start(out=jnk[2:3, 0:4], in_=ident_in[0:1, 0:4]
                                ).then_inc(ga2, 16)
                    e.wait_ge(ga2, 16)
                dst = qt_sb[:].rearrange("p (hh t j) -> p hh t j",
                                         hh=HPC, j=NDT)[:, tch, :, ct]
                e.activation(dst, pslot(slot), AF.Identity,
                             bias=bqt[:, ct:ct + 1]).then_inc(s2_ev, 1)

    @blk.gpsimd
    def _(e):
        emit_grants(e, gp2, g2_ord, 1)

    if "qt" in dbg_aps:
        @blk.sync
        def _(e):
            sp_selfb(e, s2_ev, 32)
            e.dma_start(out=dbg_aps["qt"], in_=qt_sb[:]).then_inc(s_dbg, 16)

    if stop_phase <= 2:
        return finish()

    wfstg = salloc("wfstg", [128, D], F32)
    wfstg2 = salloc("wfstg2", [128, D], F32)
    bwf = salloc("bwf", [128, D], F32)
    bsgn = salloc("bsgn", [128, D], F32)
    bal = salloc("bal", [128, 4], F32)
    bal2 = salloc("bal2", [128, 2], F32)
    wo_bf = salloc("wo_bf", [128, D], BF16)
    bwt_sb = salloc("bwt_sb", [128, NDT * 256], BF16)

    @blk.sync
    def _(e):
        for rt in range(2):
            if rt > 0:
                sp_selfb(e, sw_vx, 1)
            e.dma_start(out=wfstg[:],
                        in_=wo_sl[rt * 128:(rt + 1) * 128, :]).then_inc(sw_ld, 16)
            trailer(e, sw_fl)
        sp_selfb(e, sw_vx, 2)
        e.dma_start(out=wfstg[:],
                    in_=wf_sl[0:128, :]).then_inc(sw_ld, 16)
        trailer(e, sw_fl)
        e.dma_start(out=wfstg2[:],
                    in_=wf_sl[128:256, :]).then_inc(sw_ld, 16)
        trailer(e, sw_fl)

    @blk.vector
    def _(e):
        for rt in range(2):
            e.wait_ge(sw_ld, 16 * (rt + 1))
            e.wait_ge(sw_fl, 16 * (rt + 1))
            if rt > 0:
                e.wait_ge(sw_gp, 16)
            e.tensor_copy(wo_bf[:], wfstg[:]).then_inc(sw_vx, 1)
        e.wait_ge(sw_ld, 48)
        e.wait_ge(sw_fl, 48)
        e.reduce_sum(bal[:, 0:1], wfstg[:], axis=AX,
                     apply_absolute_value=True)
        e.wait_ge(sw_ld, 64)
        e.wait_ge(sw_fl, 64)
        e.reduce_sum(bal[:, 1:2], wfstg2[:], axis=AX,
                     apply_absolute_value=True).then_inc(sfl_v, 1)
        e.wait_ge(sfl_s, 2)
        e.tensor_scalar(bal2[:, 0:1], bal[:, 0:1], 1.0 / D, None, ALU.mult)
        e.tensor_scalar(bal2[:, 1:2], bal[:, 1:2], 1.0 / D, None, ALU.mult)
        for rt in range(2):
            wsrc = wfstg if rt == 0 else wfstg2
            if rt > 0:
                e.wait_ge(sw_tp, 4)
            e.tensor_scalar(bwf[:], wsrc[:], 0.0, None, ALU.is_ge)
            e.tensor_scalar(bsgn[:], bwf[:], 2.0, 1.0,
                            ALU.mult, ALU.subtract).then_inc(sw_vx, 1)
            e.tensor_scalar(bwf[:], bsgn[:], bal2[:, rt:rt + 1], None,
                            ALU.mult).then_inc(sw_vx, 1)
            for grp4 in range(4):
                gidx = 32 + 4 * rt + grp4
                bank = paux if gidx % 2 == 0 else pout
                e.wait_ge(gpw, gw_idx[("tp", 4 * rt + grp4)])
                dt0 = grp4 * 4
                dst = bwt_sb[:].rearrange("p (dt c) -> p dt c", dt=NDT)[
                    :, dt0:dt0 + 4, rt * 128:(rt + 1) * 128]
                srcv = bank[:].rearrange("p (j t) -> p j t", j=4)
                e.tensor_copy(dst, srcv).then_inc(sw_te, 1)

    @blk.scalar
    def _(e):
        e.wait_ge(sfl_v, 3)
        e.activation(ascr[:], masks_sb[:, 0:1024], AF.Identity)
        e.activation(ascr[:], masks_sb[:, 0:1024], AF.Identity).then_inc(sfl_s, 1)

    @blk.tensor
    def _(e):
        for rt in range(2):
            e.wait_ge(gpw, gw_idx[("bw", rt)])
            for grp4 in range(4):
                gidx = 32 + 4 * rt + grp4
                bank = paux if gidx % 2 == 0 else pout
                prev = gidx - 2
                if prev < 32:
                    e.wait_ge(s1_te, prev + 1)
                else:
                    e.wait_ge(sw_te, prev - 32 + 1)
                for j in range(4):
                    dt = grp4 * 4 + j
                    ins = e.transpose(bank[:, j * 128:(j + 1) * 128],
                                      bwf[:, dt * 128:(dt + 1) * 128],
                                      ident[:])
                    if j == 3:
                        ins.then_inc(sw_tp, 1)

    @blk.gpsimd
    def _(e):
        for rt in range(2):
            gp_selfb(e, sw_vx, rt + 1)
            e.dma_start(out=wo_stage.ap()[rt * 128:(rt + 1) * 128, :],
                        in_=wo_bf[:]).then_inc(sw_gp, 16)
        e.wait_ge(sw_gp, 32)
        e.collective_compute("AllGather", ALU.bypass, replica_groups=RG,
                             ins=[wo_stage.ap().opt()],
                             outs=[wo_ag.ap().opt()]).then_inc(cc_wo)
        emit_grants(e, gpw, gw_ord, 1)
        gp_selfb(e, sw_te, 8)
        for dt in range(NDT):
            e.dma_start(out=bwt_stage.ap()[dt * 128:(dt + 1) * 128, :],
                        in_=bwt_sb[:, dt * 256:(dt + 1) * 256]
                        ).then_inc(sw_gp, 16)
        e.wait_ge(sw_gp, 32 + 16 * NDT)
        e.collective_compute("AllGather", ALU.bypass, replica_groups=RG,
                             ins=[bwt_stage.ap().opt()],
                             outs=[bwt_ag.ap().opt()]).then_inc(cc_bw)

    if "bwt" in dbg_aps:
        @blk.sync
        def _(e):
            sp_selfb(e, sw_te, 8)
            e.dma_start(out=dbg_aps["bwt"], in_=bwt_sb[:]).then_inc(s_dbg, 16)
    if "bal" in dbg_aps:
        @blk.sync
        def _(e):
            e.wait_ge(sw_te, 8)
            e.dma_start(out=dbg_aps["bal"][:, 0:4], in_=bal[:]).then_inc(s_dbg, 16)
            e.dma_start(out=dbg_aps["bal"][:, 4:6], in_=bal2[:]).then_inc(s_dbg, 16)

    if stop_phase <= 2.5:
        return finish()

    sfree("bwt_sb", "wo_bf", "bal2", "bal", "bsgn", "bwf", "wfstg2", "wfstg",
          "wqB1", "wqB0", "wqstg")
    big_b = salloc("big_b", [128, 2 * NDT * 512], BF16)
    wkB = salloc("wkB", [128, NDT * HPC * DH], BF16)
    wvB = salloc("wvB", [128, NDT * HPC * DH], BF16)
    wkvstg = salloc("wkvstg", [128, 4 * HPC * DH], F32)
    vtf = [salloc(f"vtf{h}", [128, 512], F32) for h in range(HPC)]

    @blk.sync
    def _(e):
        e.wait_ge(s2_ev, 32)
        e.wait_ge(sw_gp, 32 + 16 * NDT)
        e.wait_ge(sw_te, 8)
        for q in range(4):
            if q > 0:
                sp_selfb(e, sw_vx, 6 + q)
            for j in range(4):
                dt = 4 * q + j
                e.dma_start(out=wkvstg[:, j * 256:(j + 1) * 256],
                            in_=wk_sl[dt * 128:(dt + 1) * 128, :]
                            ).then_inc(s3_lk, 16)
            e.dma_start(out=jnk[3:4, 0:4], in_=ident_in[0:1, 0:4]
                        ).then_inc(s3_ld, 16)
            trailer(e, s3_fl)
        for i in range(16):
            k, r = i % 2, i // 2
            e.wait_ge(cc_ln[k], 1)
            if i == 1:
                for q in range(4):
                    sp_selfb(e, sw_vx, 10 + q)
                    for j in range(4):
                        dt = 4 * q + j
                        e.dma_start(out=wkvstg[:, j * 256:(j + 1) * 256],
                                    in_=wv_sl[dt * 128:(dt + 1) * 128, :]
                                    ).then_inc(s3_lv, 16)
                    e.dma_start(out=jnk[3:4, 0:4], in_=ident_in[0:1, 0:4]
                                ).then_inc(s3_ld, 16)
                    trailer(e, s3_fl)
            if i >= 2:
                e.wait_ge(s3_mm, 4 * (i - 1))
            for dt in range(NDT):
                e.dma_start(
                    out=big_b[:, (i % 2) * 8192 + dt * 512:
                              (i % 2) * 8192 + (dt + 1) * 512],
                    in_=ln_ag[k].ap()[r * D + dt * 128:r * D + (dt + 1) * 128, :]
                ).then_inc(s3_lb, 16)
            e.dma_start(out=jnk[3:4, 0:4], in_=ident_in[0:1, 0:4]
                        ).then_inc(s3_ld, 16)
            trailer(e, s3_fl)

    @blk.vector
    def _(e):
        for q in range(4):
            e.wait_ge(s3_lk, 16 * 4 * (q + 1))
            e.wait_ge(s3_ld, 16 * (q + 1))
            e.wait_ge(s3_fl, 16 * (q + 1))
            e.tensor_copy(wkB[:, q * 1024:(q + 1) * 1024],
                          wkvstg[:]).then_inc(sw_vx, 1)
        for q in range(4):
            e.wait_ge(s3_lv, 16 * 4 * (q + 1))
            e.wait_ge(s3_ld, 80 + 16 * (q + 1))
            e.wait_ge(s3_fl, 80 + 16 * (q + 1))
            e.tensor_copy(wvB[:, q * 1024:(q + 1) * 1024],
                          wkvstg[:]).then_inc(sw_vx, 1)

    @blk.tensor
    def _(e):
        for i in range(16):
            e.wait_ge(s3_lb, 16 * 16 * (i + 1))
            e.wait_ge(s3_ld, 80 if i == 0 else 144 + 16 * i)
            e.wait_ge(s3_fl, 80 if i == 0 else 144 + 16 * i)
            if i == 0:
                e.wait_ge(gp3, g3_idx[("wv",)])
                e.wait_ge(s2_ev, 32)
            for h in range(HPC):
                slot = h
                if i >= 1:
                    e.wait_ge(gp3, g3_idx[("ev", i - 1, h)])
                ps = pslot(slot)
                for dt in range(NDT):
                    ins = e.matmul(
                        ps, wkB[:, dt * 256 + h * 128:dt * 256 + (h + 1) * 128],
                        big_b[:, (i % 2) * 8192 + dt * 512:
                              (i % 2) * 8192 + (dt + 1) * 512],
                        start=(dt == 0), stop=(dt == NDT - 1))
                    if dt == NDT - 1:
                        ins.then_inc(s3_mm, 1)
            for h in range(HPC):
                slot = 2 + h
                if i >= 1:
                    e.wait_ge(gp3, g3_idx[("vf", i - 1, h)])
                ps = pslot(slot)
                for dt in range(NDT):
                    ins = e.matmul(
                        ps, wvB[:, dt * 256 + h * 128:dt * 256 + (h + 1) * 128],
                        big_b[:, (i % 2) * 8192 + dt * 512:
                              (i % 2) * 8192 + (dt + 1) * 512],
                        start=(dt == 0), stop=(dt == NDT - 1))
                    if dt == NDT - 1:
                        ins.then_inc(s3_mm, 1)
            for h in range(HPC):
                gidx = 40 + 2 * i + h
                bank = paux if gidx % 2 == 0 else pout
                e.wait_ge(gp3, g3_idx[("vf", i, h)])
                prev = gidx - 2
                if prev < 40:
                    e.wait_ge(sw_te, prev - 32 + 1)
                else:
                    e.wait_ge(s3_te, prev - 40 + 1)
                for j in range(4):
                    ins = e.transpose(bank[:, j * 128:(j + 1) * 128],
                                      vtf[h][:, j * 128:(j + 1) * 128],
                                      ident[:])
                    if j == 3:
                        ins.then_inc(s3_tp, 1)

    @blk.scalar
    def _(e):
        for i in range(16):
            for h in range(HPC):
                e.wait_ge(s3_mm, 4 * i + h + 3)
                ins = e.activation(kt_sb[:, h * T + i * 512:h * T + (i + 1) * 512],
                                   pslot(h), AF.Identity,
                                   bias=bkt[:, h:h + 1])
                ins.then_inc(s3_ev, 1)
        e.activation(jexp[:], masks_sb[:, 0:512], AF.Exp).then_inc(s4_wr, 1)

    @blk.vector
    def _(e):
        for i in range(16):
            for h in range(HPC):
                e.wait_ge(gp3, g3_idx[("vt", i, h)])
                if i >= 1:
                    e.wait_ge(s3_tp, 2 * (i - 1) + h + 1)
                e.tensor_scalar(vtf[h][:], pslot(2 + h), bvt[:, h:h + 1],
                                None, ALU.add).then_inc(s3_evV, 1)
            for h in range(HPC):
                gidx = 40 + 2 * i + h
                bank = paux if gidx % 2 == 0 else pout
                e.wait_ge(gp3, g3_idx[("tp3", i, h)])
                e.tensor_copy(v_sb[:, h * T + i * 512:h * T + (i + 1) * 512],
                              bank[:]).then_inc(s3_te, 1)

    @blk.gpsimd
    def _(e):
        emit_grants(e, gp3, g3_ord, 1)

    if "kt" in dbg_aps:
        @blk.sync
        def _(e):
            sp_selfb(e, s3_ev, 32)
            e.dma_start(out=dbg_aps["kt"], in_=kt_sb[:]).then_inc(s_dbg, 16)
    if "v" in dbg_aps:
        @blk.sync
        def _(e):
            sp_selfb(e, s3_te, 32)
            e.dma_start(out=dbg_aps["v"], in_=v_sb[:]).then_inc(s_dbg, 16)

    if stop_phase <= 3:
        return finish()

    sfree("vtf1", "vtf0", "wkvstg", "wvB", "wkB", "big_b")
    ot_sb = salloc("ot_sb", [128, HPC * T], BF16)
    exp_sb = salloc("exp_sb", [128, EXP_RING * 512], BF16)
    ptmp = salloc("ptmp", [128, 512], BF16)
    acc = salloc("acc", [128, 512], F32)
    rrow = salloc("rrow", [1, 512], F32)
    srow = salloc("srow", [1, 512], F32)
    s_dbs = sem("s_dbs") if "sums" in dbg_aps else None
    rbc = salloc("rbc", [128, 512], F32)

    @blk.tensor
    def _(e):
        e.wait_ge(s3_eg, 16)
        e.wait_ge(s3_evV, 32)
        e.wait_ge(s3_tg, 16)
        for blko in SCHED:
            b, qb, h = blko["b"], blko["qb"], blko["h"]
            e.wait_ge(s3_ev, min(2 * qb + 4, 32))
            e.wait_ge(s3_te, min(2 * qb + 4, 32))
            grps = blko["grps"]
            ng = len(grps)
            first_out = blko["outmm_after"] - blko["nt"] + 1

            def emit_L(gi):
                grp = grps[gi]
                if grp["war_eg"]:
                    e.wait_ge(s4_xr, grp["war_eg"] + 1)
                for ti, tt in enumerate(grp["tiles"]):
                    kvt = tt["kvt"]
                    ins = e.matmul(
                        pslot(tt["g"]),
                        kt_sb[:, h * T + kvt * 128:h * T + (kvt + 1) * 128],
                        qt_sb[:, h * T + qb * 512:h * T + (qb + 1) * 512],
                        start=True, stop=not tt["diag"])
                    if tt["diag"]:
                        md = tt["kvt"] - (blko["nt"] - 4)
                        ins = e.matmul(pslot(tt["g"]), identB[:],
                                       masks_sb[:, md * 512:(md + 1) * 512],
                                       start=False, stop=True)
                    if ti == len(grp["tiles"]) - 1:
                        ins.then_inc(s4_log, 1)

            def emit_O(gi):
                grp = grps[gi]
                e.wait_ge(s4_xr, grp["eg"] + 3)
                for tt in grp["tiles"]:
                    if tt["out_idx"] == first_out and b >= 1:
                        e.wait_ge(s4_div, b)
                    sl = tt["esl"]
                    e.matmul(pout[:],
                             v_sb[:, h * T + tt["kvt"] * 128:
                                  h * T + (tt["kvt"] + 1) * 128],
                             exp_sb[:, sl * 512:(sl + 1) * 512],
                             start=(tt["out_idx"] == first_out),
                             stop=(tt["out_idx"] == blko["outmm_after"]),
                             ).then_inc(s4_out, 1)

            for gi in range(min(4, ng)):
                emit_L(gi)
            for gi in range(4, ng):
                emit_O(gi - 4)
                emit_L(gi)
            for gi in range(max(0, ng - 4), ng):
                emit_O(gi)
            e.wait_ge(s4_acc, b + 1)
            if b >= 1:
                e.wait_ge(s4_div, b)
            e.matmul(paux[0:1, :], ones_c[:], acc[:],
                     start=True, stop=True).then_inc(s4_sum, 1)
            e.wait_ge(s4_se, b + 1)
            e.matmul(paux[:], ones_r[:], srow[:],
                     start=True, stop=True).then_inc(s4_bct, 1)

    @blk.scalar
    def _(e):
        e.wait_ge(s4_warm, 16)
        for blko in SCHED:
            for grp in blko["grps"]:
                e.wait_ge(s4_log, grp["gseq"] + 1)
                if grp["war_out"]:
                    e.wait_ge(s4_out, grp["war_out"])
                if grp["war_pair"]:
                    e.wait_ge(s4_prs, grp["war_pair"])
                w = 512 * len(grp["tiles"])
                esl0 = grp["esl0"]
                e.activation(exp_sb[:, esl0 * 512:esl0 * 512 + w],
                             pgroup(grp["slot0"], w), AF.Exp,
                             scale=SM_SCALE).then_inc(s4_xr, 1)
            for _ in range(3):
                e.activation(jexp[:, 0:256], masks_sb[:, 0:256],
                             AF.Exp).then_inc(s4_xr, 1)
            e.wait_ge(s4_sum, blko["b"] + 1)
            e.activation(srow[:], paux[0:1, :], AF.Identity)
            e.activation(jexp[:], masks_sb[:, 0:512],
                         AF.Exp).then_inc(s4_se, 1)

    @blk.vector
    def _(e):
        for blko in SCHED:
            b, qb, h = blko["b"], blko["qb"], blko["h"]
            for pr in blko["prs"]:
                t0, t1 = pr["t0"], pr["t1"]
                e.wait_ge(s4_xr, t1["eg"] + 2)
                s0, s1x = t0["esl"], t1["esl"]
                e.tensor_tensor(ptmp[:], exp_sb[:, s0 * 512:(s0 + 1) * 512],
                                exp_sb[:, s1x * 512:(s1x + 1) * 512], ALU.add)
                if pr["idx"] == blko["pairs_after"] - blko["nt"] // 2 + 1:
                    e.tensor_copy(acc[:], ptmp[:]).then_inc(s4_prs, 1)
                else:
                    e.tensor_tensor(acc[:], acc[:], ptmp[:],
                                    ALU.add).then_inc(s4_prs, 1)
            e.tensor_copy(bscr[:], exp_sb[:, 0:1024])
            e.tensor_copy(bscr[:], exp_sb[:, 1024:2048]).then_inc(s4_acc, 1)
            if s_dbs is not None and b >= 1:
                e.wait_ge(s_dbs, 16 * b)
            e.wait_ge(s4_bct, b + 1)
            e.tensor_copy(bscr[:], exp_sb[:, 0:1024])

            e.reciprocal(rbc[:], paux[:]).then_inc(s4_rcp, 1)
            e.wait_ge(s4_out, blko["outmm_after"])
            e.tensor_tensor(ot_sb[:, h * T + qb * 512:h * T + (qb + 1) * 512],
                            pout[:], rbc[:], ALU.mult)
            e.tensor_copy(bscr[:], exp_sb[:, 1024:2048]).then_inc(s4_div, 1)
        e.tensor_copy(bscr[:], exp_sb[:, 0:1024]).then_inc(s4_div, 1)

    @blk.sync
    def _(e):
        def bounce(sem_raw, val, sem_granted):
            e.wait_ge(sem_raw, val)
            e.dma_start(out=jnk[0:1, 0:4], in_=ident_in[0:1, 0:4]
                        ).then_inc(sem_granted, 16)

        chain_n = [0]

        def long_bounce(sem_raw, val, sem_granted, n=5):
            e.wait_ge(sem_raw, val)
            for _ in range(n):
                chain_n[0] += 16
                e.dma_start(out=jnk[0:1, 0:4], in_=ident_in[0:1, 0:4]
                            ).then_inc(s_chain, 16)
                e.wait_ge(s_chain, chain_n[0])
            e.dma_start(out=jnk[0:1, 0:4], in_=ident_in[0:1, 0:4]
                        ).then_inc(sem_granted, 16)

        long_bounce(s3_ev, 32, s3_eg)
        long_bounce(s3_te, 32, s3_tg)
        long_bounce(s4_wr, 1, s4_warm)
        if s_dbs is not None:
            for blko in SCHED:
                b = blko["b"]
                e.wait_ge(s4_se, b + 1)
                e.dma_start(out=dbg_aps["sums"][b:b + 1, :], in_=srow[:]
                            ).then_inc(s_dbs, 16)

    @blk.gpsimd
    def _(e):
        for j in range(4):
            gp_selfb(e, s4_div, 8 * (j + 1))
            for h in range(HPC):
                e.dma_start(out=ot_stage[j].ap()[h * 128:(h + 1) * 128, :],
                            in_=ot_sb[:, h * T + j * 2048:h * T + (j + 1) * 2048]
                            ).then_inc(s4_gp, 16)
            e.wait_ge(s4_gp, 32 * (j + 1))
            e.collective_compute(
                "AllGather", ALU.bypass, replica_groups=RG,
                ins=[ot_stage[j].ap().opt()],
                outs=[ot_ag.ap()[j * 2048:(j + 1) * 2048, :].opt()],
            ).then_inc(cc_ot[j])

    if "ot" in dbg_aps:
        @blk.sync
        def _(e):
            sp_selfb(e, s4_div, 32)
            e.dma_start(out=dbg_aps["ot"], in_=ot_sb[:]).then_inc(s_dbg, 16)
    for _nm, _sb in (("kt4", kt_sb), ("v4", v_sb), ("qt4", qt_sb)):
        if _nm in dbg_aps:
            @blk.sync
            def _(e, _sb=_sb, _nm=_nm):
                e.wait_ge(s4_div, 33)
                e.dma_start(out=dbg_aps[_nm], in_=_sb[:]).then_inc(s_dbg, 16)
    if "sums" in dbg_aps:
        @blk.sync
        def _(e):
            e.wait_ge(s_dbs, 16 * 32)
            e.dma_start(out=jnk[0:1, 0:4], in_=ident_in[0:1, 0:4]
                        ).then_inc(s_dbg, 16)

    if stop_phase <= 4:
        return finish()

    sfree("rbc", "srow", "rrow", "acc", "ptmp", "exp_sb", "ot_sb",
          "v_sb", "kt_sb", "qt_sb")
    x1_sb = salloc("x1_sb", [128, NRT * D], F32)
    wob = salloc("wob", [128, 2 * NDT * 512], BF16)

    @blk.gpsimd
    def _(e):
        for j in range(4):
            e.wait_ge(cc_ot[j], 1)
        for ht in range(NH):
            src = ot_ag.ap().rearrange("a (s t) -> (a s) t", s=2)
            e.indirect_dma_start(
                big_a[:, ht * 1024:(ht + 1) * 1024], None, src,
                bass.IndirectOffsetOnAxis(ap=poix[:, ht:ht + 1], axis=0),
            ).then_inc(s5_gp, 16)
            e.dma_start(out=jnk[1:2, 0:4], in_=ident_in[0:1, 0:4]
                        ).then_inc(s5_gf, 16)
        emit_grants(e, gp5, g5_ord, 1)

    @blk.sync
    def _(e):
        e.wait_ge(s4_div, 32)
        for qt in range(NRT):
            e.dma_start(out=x1_sb[:, qt * D:(qt + 1) * D],
                        in_=x_own[qt * 128:(qt + 1) * 128, :]).then_inc(s5_ld, 16)
            trailer(e, s5_fl)
        e.wait_ge(cc_wo, 1)
        for och in range(4):
            if och >= 2:
                e.wait_ge(s5_mm, 8 * (och - 1))
            for ht in range(NH):
                e.dma_start(
                    out=wob[:, (och % 2) * 8192 + ht * 512:
                            (och % 2) * 8192 + (ht + 1) * 512],
                    in_=wo_ag.ap()[ht * 128:(ht + 1) * 128,
                                   och * 512:(och + 1) * 512],
                ).then_inc(s5_ld, 16)
                trailer(e, s5_fl)

    @blk.tensor
    def _(e):
        e.wait_ge(s4_xr, SCHED[-1]["eg_after"])
        e.wait_ge(s4_div, 32)
        e.wait_ge(s5_gp, 16 * NH)
        e.wait_ge(s5_gf, 16 * NH)
        for och in range(4):
            e.wait_ge(s5_ld, 16 * (NRT + NH * (och + 1)))
            e.wait_ge(s5_fl, 16 * (NRT + NH * (och + 1)))
            for qt in range(NRT):
                m = och * NRT + qt
                if m >= LOG_RING:
                    e.wait_ge(s5_ev, m - LOG_RING + 1)
                ps = pslot(m)
                for ht in range(NH):
                    e.matmul(ps, big_a[:, ht * 1024 + qt * 128:
                                       ht * 1024 + (qt + 1) * 128],
                             wob[:, (och % 2) * 8192 + ht * 512:
                                 (och % 2) * 8192 + (ht + 1) * 512],
                             start=(ht == 0), stop=False)
                e.matmul(ps, ones_rb[:], bor[0:1, och * 512:(och + 1) * 512],
                         start=False, stop=True).then_inc(s5_mm, 1)

    @blk.vector
    def _(e):
        for och in range(4):
            for qt in range(NRT):
                m = och * NRT + qt
                if m < 30:
                    e.wait_ge(s5_mm, m + 2)
                else:
                    e.wait_ge(gp5, g5_idx[("mm", m)])
                if och == 0:
                    e.wait_ge(s5_ld, 16 * (qt + 1))
                    e.wait_ge(s5_fl, 16 * (qt + 1))
                sl = x1_sb[:, qt * D + och * 512:qt * D + (och + 1) * 512]
                e.tensor_tensor(sl, sl, pslot(m), ALU.add).then_inc(s5_ev, 1)

    if stop_phase <= 5:
        if "x1" in dbg_aps:
            @blk.sync
            def _(e):
                sp_selfb(e, s5_ev, 32)
                e.dma_start(out=dbg_aps["x1"], in_=x1_sb[:]).then_inc(s_dbg, 16)
        return finish()

    x2ln = [salloc(f"x2ln{i}", [128, D], F32) for i in range(2)]
    scr6 = [salloc(f"scr6{i}", [128, D], F32) for i in range(2)]
    t_mu = salloc("t_mu", [128, NRT], F32)
    t_muv = salloc("t_muv", [128, NRT], F32)
    t_m2 = salloc("t_m2", [128, NRT], F32)
    t_m2e = salloc("t_m2e", [128, NRT], F32)
    t_sq = salloc("t_sq", [128, NRT], F32)
    t_v = salloc("t_v", [128, NRT], F32)
    t_l = salloc("t_l", [128, NRT], F32)
    t_r = salloc("t_r", [128, NRT], F32)

    @blk.scalar
    def _(e):
        e.wait_ge(sfl_v, 4)
        e.activation(ascr[:], masks_sb[:, 0:1024], AF.Identity)
        e.activation(ascr[:], masks_sb[:, 0:1024], AF.Identity).then_inc(sfl_s, 1)
        e.wait_ge(sfl_v, 5)
        e.activation(ascr[:], masks_sb[:, 0:1024], AF.Identity)
        e.activation(ascr[:], masks_sb[:, 0:1024], AF.Identity)
        e.activation(t_l[:], t_v[:], AF.Ln)
        e.activation(ascr[:], masks_sb[:, 0:1024], AF.Identity)
        e.activation(ascr[:], masks_sb[:, 0:1024], AF.Identity)
        e.activation(t_r[:], t_l[:], AF.Exp, scale=-0.5)
        e.activation(ascr[:], masks_sb[:, 0:1024], AF.Identity)
        e.activation(ascr[:], masks_sb[:, 0:1024], AF.Identity).then_inc(s6_rs, 1)

    @blk.vector
    def _(e):
        for r in range(NRT):
            e.wait_ge(s5_ev, 24 + r + 1)
            e.tensor_tensor(scr6[r % 2][:], x1_sb[:, r * D:(r + 1) * D],
                            x1_sb[:, r * D:(r + 1) * D], ALU.mult)
            e.reduce_sum(t_mu[:, r:r + 1], x1_sb[:, r * D:(r + 1) * D],
                         axis=AX)
            if r < NRT - 1:
                e.reduce_sum(t_sq[:, r:r + 1], scr6[r % 2][:], axis=AX)
            else:
                e.reduce_sum(t_sq[:, r:r + 1], scr6[r % 2][:],
                             axis=AX).then_inc(sfl_v, 1)
        e.wait_ge(sfl_s, 3)
        e.tensor_scalar(t_muv[:], t_mu[:], 1.0 / D, None, ALU.mult)
        e.tensor_scalar(t_sqd[:], t_sq[:], 1.0 / D, None, ALU.mult)
        e.tensor_copy(bscr[:], scr6[0][:, 0:1024])
        e.tensor_copy(bscr[:], scr6[1][:, 0:1024])
        e.tensor_tensor(t_m2[:], t_muv[:], t_muv[:], ALU.mult)
        e.tensor_copy(bscr[:], scr6[0][:, 0:1024])
        e.tensor_copy(bscr[:], scr6[1][:, 0:1024])
        e.tensor_scalar(t_m2e[:], t_m2[:], EPS, None, ALU.subtract)
        e.tensor_copy(bscr[:], scr6[0][:, 0:1024])
        e.tensor_copy(bscr[:], scr6[1][:, 0:1024])
        e.tensor_tensor(t_v[:], t_sqd[:], t_m2e[:],
                        ALU.subtract).then_inc(sfl_v, 1)
        for rr in range(NRT + 2):
            p = rr - 1
            if 0 <= p < NRT:
                e.wait_ge(s6_rs, 1)
                if p >= 2:
                    e.wait_ge(s6_tp, 4 * (p - 1))
                e.tensor_scalar(x2ln[p % 2][:], x1_sb[:, p * D:(p + 1) * D],
                                t_muv[:, p:p + 1], t_r[:, p:p + 1],
                                ALU.subtract, ALU.mult).then_inc(s6_ln, 1)
                if p == NRT - 1:
                    e.tensor_copy(bscr[:], scr6[0][:, 0:1024]).then_inc(s6_ln, 1)
            rte = rr - 2
            if 0 <= rte < NRT:
                e.wait_ge(s5_mm, 24 + rte + 1)
                if rte == 0:
                    e.tensor_copy(bscr[:], scr6[1][:, 0:1024])
                for grp4 in range(4):
                    gidx = 4 * rte + grp4
                    bank = paux if gidx % 2 == 0 else pout
                    e.wait_ge(s6_tp, gidx + 1)
                    dt0 = grp4 * 4
                    dst = big_a[:].rearrange("p (dt t) -> p dt t", dt=NDT)[
                        :, dt0:dt0 + 4, rte * 128:(rte + 1) * 128]
                    srcv = bank[:].rearrange("p (j t) -> p j t", j=4)
                    e.tensor_copy(dst, srcv).then_inc(s6_te, 1)

    @blk.tensor
    def _(e):
        for rt in range(NRT):
            e.wait_ge(s6_ln, rt + 2)
            for grp4 in range(4):
                gidx = 4 * rt + grp4
                bank = paux if gidx % 2 == 0 else pout
                if gidx >= 2:
                    e.wait_ge(s6_te, gidx - 1)
                elif rt == 0 and grp4 == 0:
                    e.wait_ge(s4_div, 32)
                for j in range(4):
                    dt = grp4 * 4 + j
                    ins = e.transpose(bank[:, j * 128:(j + 1) * 128],
                                      x2ln[rt % 2][:, dt * 128:(dt + 1) * 128],
                                      ident[:])
                    if j == 3:
                        ins.then_inc(s6_tp, 1)

    @blk.sync
    def _(e):
        e.wait_ge(cc_bw, 1)
        e.wait_ge(s5_mm, 32)
        for och in range(4):
            if och >= 2:
                e.wait_ge(s7_mm, 8 * (och - 1))
            for dt in range(NDT):
                for rr in range(2):
                    r = 2 * och + rr
                    e.dma_start(
                        out=wob[:, (och % 2) * 8192 + dt * 512 + rr * 256:
                                (och % 2) * 8192 + dt * 512 + (rr + 1) * 256],
                        in_=bwt_ag.ap()[r * D + dt * 128:
                                        r * D + (dt + 1) * 128, :],
                    ).then_inc(s7_ld, 16)
                    trailer(e, s7_fl)

    @blk.tensor
    def _(e):
        e.wait_ge(s5_ev, 32)
        for och in range(4):
            e.wait_ge(s7_ld, 16 * 32 * (och + 1))
            e.wait_ge(s7_fl, 16 * 32 * (och + 1))
            for qt in range(NRT):
                m = och * NRT + qt
                if och == 0:
                    if qt < NRT - 1:
                        e.wait_ge(s6_te, 4 * qt + 5)
                    else:
                        e.wait_ge(gp6, g6_idx[("te6", qt)])
                if m >= LOG_RING:
                    e.wait_ge(s7_ev, m - LOG_RING + 1)
                ps = pslot(m)
                for dt in range(NDT):
                    e.matmul(ps, big_a[:, dt * 1024 + qt * 128:
                                       dt * 1024 + (qt + 1) * 128],
                             wob[:, (och % 2) * 8192 + dt * 512:
                                 (och % 2) * 8192 + (dt + 1) * 512],
                             start=(dt == 0), stop=False)
                e.matmul(ps, ones_rb[:], bfr[0:1, och * 512:(och + 1) * 512],
                         start=False, stop=True).then_inc(s7_mm, 1)

    @blk.vector
    def _(e):
        for och in range(4):
            for qt in range(NRT):
                m = och * NRT + qt
                if m < 30:
                    e.wait_ge(s7_mm, m + 2)
                else:
                    e.wait_ge(gp7, g7_idx[("mm", m)])
                sl = x1_sb[:, qt * D + och * 512:qt * D + (och + 1) * 512]
                e.tensor_tensor(sl, sl, pslot(m), ALU.add).then_inc(s7_ev, 1)

    @blk.gpsimd
    def _(e):
        emit_grants(e, gp6, g6_ord, 1)
        emit_grants(e, gp7, g7_ord, 1)

    @blk.sync
    def _(e):
        for qt in range(NRT):
            if qt < NRT - 2:
                e.wait_ge(s7_ev, 24 + qt + 3)
            else:
                sp_selfb(e, s7_ev, 32)
            e.dma_start(out=y[qt * 128:(qt + 1) * 128, :],
                        in_=x1_sb[:, qt * D:(qt + 1) * D]).then_inc(s_st, 16)

    return finish(extra_waits=[(s_st, 16 * NRT)])



def make_in_maps(inputs):
    x = np.asarray(inputs["L0_attn_x"])
    wq = np.ascontiguousarray(np.asarray(inputs["L0_attn_Wq"]), dtype=np.float32)
    wk = np.asarray(inputs["L0_attn_Wk"])
    wv = np.asarray(inputs["L0_attn_Wv"])
    wo = np.asarray(inputs["L0_attn_Wo"])
    wf = np.asarray(inputs["L0_ffn_weight"])
    bq = np.asarray(inputs["L0_attn_bq"])
    bk = np.asarray(inputs["L0_attn_bk"])
    bv = np.asarray(inputs["L0_attn_bv"])
    bo = np.asarray(inputs["L0_attn_bo"])
    bf = np.asarray(inputs["L0_ffn_bias"])

    xf = np.asarray(x).reshape(T, D)
    masks = np.zeros((128, 4 * QB), dtype=ml_dtypes.bfloat16)
    p = np.arange(128)[:, None]
    i = np.arange(QB)[None, :]
    for jd in range(4):
        masks[:, jd * QB:(jd + 1) * QB] = np.where(
            i >= jd * 128 + p, 0.0, -30000.0).astype(ml_dtypes.bfloat16)
    ident = np.eye(128, dtype=np.float32)
    identb = np.eye(128, dtype=ml_dtypes.bfloat16)
    ones_r = np.ones((1, 128), np.float32)
    ones_c = np.ones((128, 1), np.float32)
    bq_t = np.ascontiguousarray(np.asarray(bq).reshape(NDT, 128).T,
                                dtype=np.float32)
    bo_r = np.ascontiguousarray(np.asarray(bo).reshape(1, D), dtype=np.float32)
    bf_r = np.ascontiguousarray(np.asarray(bf).reshape(1, D), dtype=np.float32)

    in_maps = []
    for c in range(NC):
        d = np.arange(128)[:, None]
        ht = np.arange(NH)[None, :]
        po_idx = (((c // 2) * 2048 + (ht // 2) * 256 + (ht % 2) * 128 + d) * 2
                  + (c % 2)).astype(np.uint32)
        in_maps.append({
            "x_own": np.ascontiguousarray(xf[c * TL:(c + 1) * TL, :],
                                          dtype=np.float32),
            "wq": wq,
            "wk_sl": np.ascontiguousarray(wk[:, c * 256:(c + 1) * 256],
                                          dtype=np.float32),
            "wv_sl": np.ascontiguousarray(wv[:, c * 256:(c + 1) * 256],
                                          dtype=np.float32),
            "wo_sl": np.ascontiguousarray(wo[c * 256:(c + 1) * 256, :],
                                          dtype=np.float32),
            "wf_sl": np.ascontiguousarray(wf[c * 256:(c + 1) * 256, :],
                                          dtype=np.float32),
            "bq_t": bq_t,
            "bk_t": np.ascontiguousarray(
                np.asarray(bk)[c * 256:(c + 1) * 256].reshape(HPC, 128).T,
                dtype=np.float32),
            "bv_t": np.ascontiguousarray(
                np.asarray(bv)[c * 256:(c + 1) * 256].reshape(HPC, 128).T,
                dtype=np.float32),
            "bo_r": bo_r,
            "bf_r": bf_r,
            "masks": masks,
            "ident": ident,
            "identb": identb,
            "ones_r": ones_r,
            "ones_c": ones_c,
            "po_idx": np.ascontiguousarray(po_idx),
        })
    return in_maps


def kernel(**inputs):
    nc, _ = build_nc()
    in_maps = make_in_maps(inputs)
    res = run_bass_kernel_spmd(nc, in_maps, core_ids=list(range(NC)))
    out = np.concatenate([np.asarray(res.results[c]["y"]) for c in range(NC)],
                         axis=0)
    return out.reshape(1, T, D).astype(np.float32, copy=False)


# version 12
# speedup vs baseline: 1.2323x; 1.0859x over previous
import numpy as np
import ml_dtypes
import concourse.bass as bass
import concourse.mybir as mybir
from concourse.bass_utils import run_bass_kernel_spmd

F32 = mybir.dt.float32
BF16 = mybir.dt.bfloat16
U32 = mybir.dt.uint32
AF = mybir.ActivationFunctionType
ALU = mybir.AluOpType
AX = mybir.AxisListType.X

NC = 8
D = 2048
T = 8192
TL = T // NC
DH = 128
NH = 16
HPC = NH // NC
NDT = D // 128
QB = 512
NQB = T // QB
NRT = TL // 128
RG = [list(range(NC))]
EPS = 1e-5
SM_SCALE = float(np.float32(1.0) / np.sqrt(np.float32(DH)))
LOG_RING = 6
EXP_RING = 12
GRP = 3


def _attention_schedule():
    blocks = []
    g = 0
    outmm = pairs = masks = groups_done = 0
    outmm_of, pair_of = {}, {}
    for qb in range(NQB):
        for h in range(HPC):
            if g % GRP:
                g += GRP - (g % GRP)
            nt = 4 * (qb + 1)
            g0 = g
            tiles = []
            for i in range(nt):
                gg = g0 + i
                diag = i >= nt - 4
                mi = None
                if diag:
                    masks += 1
                    mi = masks
                outmm += 1
                outmm_of[gg] = outmm
                tiles.append(dict(g=gg, kvt=i, diag=diag, mask_idx=mi,
                                  out_idx=outmm, esl=gg % EXP_RING))
            grps = []
            for i in range(0, nt, GRP):
                sub = tiles[i:i + GRP]
                grps.append(dict(gseq=groups_done, tiles=sub,
                                 slot0=sub[0]["g"] % LOG_RING,
                                 esl0=sub[0]["g"] % EXP_RING))
                for tt in sub:
                    tt["gseq"] = groups_done
                groups_done += 1
            prs = []
            for p in range(nt // 2):
                t0, t1 = tiles[2 * p], tiles[2 * p + 1]
                pairs += 1
                pair_of[t0["g"]] = pairs
                pair_of[t1["g"]] = pairs
                t0["pair_idx"] = pairs
                t1["pair_idx"] = pairs
                prs.append(dict(t0=t0, t1=t1, idx=pairs))
            blocks.append(dict(qb=qb, h=h, b=len(blocks), nt=nt, grps=grps,
                               tiles=tiles, prs=prs, groups_after=groups_done,
                               outmm_after=outmm, pairs_after=pairs,
                               masks_after=masks))
            g = g0 + nt
    ecount = 0
    allg = []
    for blk_ in blocks:
        for grp in blk_["grps"]:
            ecount += 1
            grp["eg"] = ecount
            for tt in grp["tiles"]:
                tt["eg"] = ecount
            allg.append(grp)
        ecount += 3
        blk_["eg_after"] = ecount
    for gi, grp in enumerate(allg):
        grp["war_eg"] = allg[gi - 2]["eg"] if gi >= 2 else 0
    for blk_ in blocks:
        for grp in blk_["grps"]:
            wo = wp = 0
            for tt in grp["tiles"]:
                prev = tt["g"] - EXP_RING
                wo = max(wo, outmm_of.get(prev, 0))
                wp = max(wp, pair_of.get(prev, 0))
            grp["war_out"] = wo
            grp["war_pair"] = wp
    return blocks


SCHED = _attention_schedule()


def pair_of_tile(blko, tt):
    return tt["pair_idx"]

N_GROUPS = SCHED[-1]["groups_after"]
N_OUTMM = SCHED[-1]["outmm_after"]
N_PAIRS = SCHED[-1]["pairs_after"]


def build_nc(stop_phase=9, dbg=()):
    nc = bass.Bass("TRN2", target_bir_lowering=False, debug=False,
                   detect_race_conditions=False)

    x_own = nc.dram_tensor("x_own", [TL, D], F32, kind="ExternalInput").ap()
    wq = nc.dram_tensor("wq", [D, D], F32, kind="ExternalInput").ap()
    wk_sl = nc.dram_tensor("wk_sl", [D, HPC * DH], F32, kind="ExternalInput").ap()
    wv_sl = nc.dram_tensor("wv_sl", [D, HPC * DH], F32, kind="ExternalInput").ap()
    wo_sl = nc.dram_tensor("wo_sl", [D // NC, D], F32, kind="ExternalInput").ap()
    wf_sl = nc.dram_tensor("wf_sl", [D // NC, D], F32, kind="ExternalInput").ap()
    bq_t = nc.dram_tensor("bq_t", [128, NDT], F32, kind="ExternalInput").ap()
    bk_t = nc.dram_tensor("bk_t", [128, HPC], F32, kind="ExternalInput").ap()
    bv_t = nc.dram_tensor("bv_t", [128, HPC], F32, kind="ExternalInput").ap()
    bo_r = nc.dram_tensor("bo_r", [1, D], F32, kind="ExternalInput").ap()
    bf_r = nc.dram_tensor("bf_r", [1, D], F32, kind="ExternalInput").ap()
    masks_in = nc.dram_tensor("masks", [128, 4 * QB], BF16, kind="ExternalInput").ap()
    ident_in = nc.dram_tensor("ident", [128, 128], F32, kind="ExternalInput").ap()
    identb_in = nc.dram_tensor("identb", [128, 128], BF16, kind="ExternalInput").ap()
    ones_r_in = nc.dram_tensor("ones_r", [1, 128], F32, kind="ExternalInput").ap()
    ones_c_in = nc.dram_tensor("ones_c", [128, 1], F32, kind="ExternalInput").ap()
    po_idx_in = nc.dram_tensor("po_idx", [128, NH], U32, kind="ExternalInput").ap()

    y = nc.dram_tensor("y", [TL, D], F32, kind="ExternalOutput").ap()

    ln_stage = [nc.dram_tensor(f"ln_stage{k}", [D, QB], BF16) for k in range(2)]
    ln_ag = [nc.dram_tensor(f"ln_ag{k}", [NC * D, QB], BF16,
                            addr_space="Shared") for k in range(2)]
    wo_stage = nc.dram_tensor("wo_stage", [D // NC, D], BF16)
    wo_ag = nc.dram_tensor("wo_ag", [D, D], BF16, addr_space="Shared")
    bwt_stage = nc.dram_tensor("bwt_stage", [D, D // NC], BF16)
    bwt_ag = nc.dram_tensor("bwt_ag", [NC * D, D // NC], BF16,
                            addr_space="Shared")
    ot_stage = [nc.dram_tensor(f"ot_stage{j}", [HPC * DH, 4 * QB], BF16)
                for j in range(4)]
    ot_ag = nc.dram_tensor("ot_ag", [T, 4 * QB], BF16,
                           addr_space="Shared")

    dbg_aps = {}

    def dbg_t(name, shape, dtype):
        if name in dbg:
            dbg_aps[name] = nc.dram_tensor("dbg_" + name, shape, dtype,
                                           kind="ExternalOutput").ap()

    dbg_t("lnxt", [128, NDT * TL], BF16)
    dbg_t("st", [128, 7 * NRT], F32)
    dbg_t("qt", [128, HPC * T], BF16)
    dbg_t("kt", [128, HPC * T], BF16)
    dbg_t("v", [128, HPC * T], BF16)
    dbg_t("ot", [128, HPC * T], BF16)
    dbg_t("bwt", [128, NDT * 256], BF16)
    dbg_t("bal", [128, 8], F32)
    dbg_t("st2", [128, 8 * NRT], F32)
    dbg_t("sums", [32, 512], F32)
    dbg_t("kt4", [128, 2 * 8192], BF16)
    dbg_t("v4", [128, 2 * 8192], BF16)
    dbg_t("qt4", [128, 2 * 8192], BF16)
    dbg_t("x1", [128, NRT * D], F32)

    cms = []
    cm = {}

    def salloc(name, shape, dtype, side="left"):
        c = nc.sbuf_tensor(name, shape, dtype, side=side)
        t = c.__enter__()
        cm[name] = c
        return t

    def sfree(*names):
        for n in names:
            cm.pop(n).__exit__(None, None, None)

    big_a = salloc("big_a", [128, NDT * TL], BF16)
    masks_sb = salloc("masks_sb", [128, 4 * QB], BF16)
    ident = salloc("ident_sb", [128, 128], F32)
    identB = salloc("identB_sb", [128, 128], BF16)
    ones_r = salloc("ones_r_sb", [1, 128], F32)
    ones_rb = salloc("ones_rb_sb", [1, 128], BF16)
    ones_c = salloc("ones_c_sb", [128, 1], F32)
    bqt = salloc("bqt_sb", [128, NDT], F32)
    bkt = salloc("bkt_sb", [128, HPC], F32)
    bvt = salloc("bvt_sb", [128, HPC], F32)
    bor = salloc("bor_sb", [1, D], BF16)
    bfr = salloc("bfr_sb", [1, D], BF16)
    poix = salloc("poix_sb", [128, NH], U32)
    st_mu = salloc("st_mu", [128, NRT], F32)
    st_muv = salloc("st_muv", [128, NRT], F32)
    st_m2 = salloc("st_m2", [128, NRT], F32)
    st_m2e = salloc("st_m2e", [128, NRT], F32)
    st_sq = salloc("st_sq", [128, NRT], F32)
    st_v = salloc("st_v", [128, NRT], F32)
    st_l = salloc("st_l", [128, NRT], F32)
    st_r = salloc("st_r", [128, NRT], F32)
    st_sqd = salloc("st_sqd", [128, NRT], F32)
    t_sqd = salloc("t_sqd", [128, NRT], F32)
    jnk = salloc("jnk", [128, 4], F32)
    jnk2 = salloc("jnk2", [128, 4], F32)
    jexp = salloc("jexp", [128, 512], BF16)
    bscr = salloc("bscr", [128, 1024], F32)
    ascr = salloc("ascr", [128, 1024], F32)

    pA_cm = nc.psum_tensor("pA", [128, 3 * 512], F32)
    pB_cm = nc.psum_tensor("pB", [128, 3 * 512], F32)
    pout_cm = nc.psum_tensor("pout", [128, 512], F32)
    paux_cm = nc.psum_tensor("paux", [128, 512], F32)
    pA, pB = pA_cm.__enter__(), pB_cm.__enter__()
    pout, paux = pout_cm.__enter__(), paux_cm.__enter__()
    cms += [pA_cm, pB_cm, pout_cm, paux_cm]

    def pslot(i):
        i = i % LOG_RING
        t = pA if i < 3 else pB
        return t[:, (i % 3) * 512:((i % 3) + 1) * 512]

    def pgroup(slot0, w):
        t = pA if slot0 == 0 else pB
        return t[:, :w]

    semcms = []

    def sem(name):
        c = nc.semaphore(name)
        s = c.__enter__()
        semcms.append(c)
        return s

    s0_ld = sem("s0_ld"); s0_ca = sem("s0_ca")
    s1_ld = sem("s1_ld"); s1_sq = sem("s1_sq"); s1_s2 = sem("s1_s2")
    s1_red = sem("s1_red"); s1_rs = sem("s1_rs")
    s1_st = sem("s1_st"); s1_ln = sem("s1_ln"); s1_tp = sem("s1_tp")
    s1_te = sem("s1_te"); s1_gp = sem("s1_gp")
    cc_ln = [sem("cc_ln0"), sem("cc_ln1")]
    s2_ld = sem("s2_ld"); s2_ca = sem("s2_ca"); s2_mm = sem("s2_mm")
    s2_ev = sem("s2_ev")
    sw_ld = sem("sw_ld"); sw_vx = sem("sw_vx"); sw_tp = sem("sw_tp")
    sw_te = sem("sw_te"); sw_gp = sem("sw_gp")
    cc_wo = sem("cc_wo"); cc_bw = sem("cc_bw")
    s3_ld = sem("s3_ld"); s3_mm = sem("s3_mm"); s3_ev = sem("s3_ev")
    s3_evV = sem("s3_evV"); s3_tp = sem("s3_tp"); s3_te = sem("s3_te")
    s4_log = sem("s4_log")
    s4_xr = sem("s4_xr")
    s4_acc = sem("s4_acc")
    s3_eg = sem("s3_eg"); s3_tg = sem("s3_tg")
    s4_wr = sem("s4_wr"); s4_warm = sem("s4_warm"); s_chain = sem("s_chain")
    gp1 = sem("gp1"); gp2 = sem("gp2"); gpw = sem("gpw"); gp3 = sem("gp3")
    g4b = sem("g4b"); gp5 = sem("gp5"); gp6 = sem("gp6"); gp7 = sem("gp7")
    ga2 = sem("ga2"); ga3 = sem("ga3"); s_stb = sem("s_stb")
    s0_fl = sem("s0_fl"); s1_fl = sem("s1_fl"); s2_fl = sem("s2_fl")
    s3_fl = sem("s3_fl"); sw_fl = sem("sw_fl"); s5_fl = sem("s5_fl")
    s7_fl = sem("s7_fl"); s5_gf = sem("s5_gf")
    s3_lk = sem("s3_lk"); s3_lv = sem("s3_lv"); s3_lb = sem("s3_lb")
    s2_lq = sem("s2_lq")
    s4_prs = sem("s4_prs"); s4_out = sem("s4_out"); s4_sum = sem("s4_sum")
    s4_se = sem("s4_se")
    s4_rcp = sem("s4_rcp"); s4_bct = sem("s4_bct"); s4_div = sem("s4_div")
    s4_gp = sem("s4_gp")
    cc_ot = [sem(f"cc_ot{j}") for j in range(4)]
    s5_ld = sem("s5_ld"); s5_gp = sem("s5_gp"); s5_mm = sem("s5_mm")
    s5_ev = sem("s5_ev")
    s6_rs = sem("s6_rs"); s6_ln = sem("s6_ln"); s6_tp = sem("s6_tp"); s6_te = sem("s6_te")
    s7_ld = sem("s7_ld"); s7_mm = sem("s7_mm"); s7_ev = sem("s7_ev")
    s_st = sem("s_st")
    s_dbg = sem("s_dbg")
    sfl_v = sem("sfl_v"); sfl_s = sem("sfl_s")

    def _mkord(entries):
        return entries, {k: 16 * (i + 1) for i, (k, _, _) in enumerate(entries)}

    _o = []
    for _p in range(NRT):
        _o.append((("ln", _p), s1_ln, _p + 1))
        for _g in range(4 * _p, 4 * _p + 4):
            _o.append((("tp", _g), s1_tp, _g + 1))
        if _p == 3:
            _o.append((("te", 0), s1_te, 16))
    _o.append((("te", 1), s1_te, 32))
    g1_ord, g1_idx = _mkord(_o)
    g1a_ord = g1_ord[:21]
    g1b_ord = g1_ord[21:]

    g2_ord, g2_idx = _mkord([(("ca", _c), s2_ca, 2 * _c + 2)
                             for _c in range(NDT)])

    _o = [(("bw", 0), sw_vx, 4)]
    _o += [(("tp", _k), sw_tp, _k + 1) for _k in range(4)]
    _o.append((("bw", 1), sw_vx, 6))
    _o += [(("tp", _k), sw_tp, _k + 1) for _k in range(4, 8)]
    gw_ord, gw_idx = _mkord(_o)

    _o = [(("wv",), sw_vx, 14)]
    for _i in range(16):
        for _h in range(HPC):
            _o.append((("vt", _i, _h), s3_mm, 4 * _i + 2 + _h + 1))
        for _h in range(HPC):
            _o.append((("vf", _i, _h), s3_evV, 2 * _i + _h + 1))
        for _h in range(HPC):
            _o.append((("tp3", _i, _h), s3_tp, 2 * _i + _h + 1))
        for _h in range(HPC):
            _o.append((("ev", _i, _h), s3_ev, 2 * _i + _h + 1))
    g3_ord, g3_idx = _mkord(_o)

    g5_ord, g5_idx = _mkord([(("mm", _m), s5_mm, _m + 1) for _m in range(32)])

    _o = []
    for _p in range(NRT):
        _o.append((("ln6", _p), s6_ln, _p + 1))
        for _g in range(4 * _p, 4 * _p + 4):
            _o.append((("tp6", _g), s6_tp, _g + 1))
        _o.append((("te6", _p), s6_te, 4 * _p + 4))
    g6_ord, g6_idx = _mkord(_o)

    g7_ord, g7_idx = _mkord([(("mm", _m), s7_mm, _m + 1) for _m in range(32)])

    def emit_grants(e, gsem, entries, row):
        for _, raw, val in entries:
            e.wait_ge(raw, val)
            e.dma_start(out=jnk[row:row + 1, 0:4], in_=ident_in[0:1, 0:4]
                        ).then_inc(gsem, 16)

    def trailer(e, fl):
        e.dma_start(out=jnk[3:4, 0:4], in_=ident_in[0:1, 0:4]).then_inc(fl, 16)

    gsb_n = [0]
    spb_n = [0]

    def sp_selfb(e, raw, val):
        e.wait_ge(raw, val)
        e.dma_start(out=jnk[3:4, 0:4], in_=ident_in[0:1, 0:4]).then_inc(s_stb, 16)
        spb_n[0] += 16
        e.wait_ge(s_stb, spb_n[0])

    def gp_selfb(e, raw, val):
        e.wait_ge(raw, val)
        e.dma_start(out=jnk[1:2, 0:4], in_=ident_in[0:1, 0:4]).then_inc(g4b, 16)
        gsb_n[0] += 16
        e.wait_ge(g4b, gsb_n[0])

    N_P0C = 9
    blk_cm = nc.Block()
    blk = blk_cm.__enter__()

    def finish(extra_waits=()):
        ndbg = len(dbg_aps)

        @blk.gpsimd
        def _(e):
            for s, v in extra_waits:
                e.wait_ge(s, v)
            if ndbg:
                e.wait_ge(s_dbg, 16 * ndbg)

        blk_cm.__exit__(None, None, None)
        return nc, dbg_aps

    btmp = salloc("btmp", [1, D], F32)

    @blk.sync
    def _(e):
        p0list = [(masks_sb, masks_in), (ident, ident_in),
                  (identB, identb_in),
                  (ones_r, ones_r_in), (ones_c, ones_c_in),
                  (bqt, bq_t), (bkt, bk_t), (bvt, bv_t),
                  (poix, po_idx_in)]
        assert len(p0list) == N_P0C
        for dst, src in p0list:
            e.dma_start(out=dst[:], in_=src).then_inc(s0_ld, 16)
            trailer(e, s0_fl)
        e.dma_start(out=btmp[:], in_=bo_r).then_inc(s0_ld, 16)
        trailer(e, s0_fl)
        sp_selfb(e, s0_ca, 1)
        e.dma_start(out=btmp[:], in_=bf_r).then_inc(s0_ld, 16)
        trailer(e, s0_fl)

    @blk.vector
    def _(e):
        e.wait_ge(s0_ld, 16 * (N_P0C + 1))
        e.wait_ge(s0_fl, 16 * (N_P0C + 1))
        e.tensor_copy(bor[:], btmp[:]).then_inc(s0_ca, 1)
        e.wait_ge(s0_ld, 16 * (N_P0C + 2))
        e.wait_ge(s0_fl, 16 * (N_P0C + 2))
        e.tensor_copy(bfr[:], btmp[:]).then_inc(s0_ca, 1)
        e.tensor_copy(ones_rb[:], ones_r[:]).then_inc(s0_ca, 1)

    sfree("btmp")
    qt_sb = salloc("qt_sb", [128, HPC * T], BF16)
    kt_sb = salloc("kt_sb", [128, HPC * T], BF16)
    v_sb = salloc("v_sb", [128, HPC * T], BF16)
    xstg = [salloc(f"xstg{i}", [128, D], F32) for i in range(2)]
    xln = [salloc(f"xln{i}", [128, D], F32) for i in range(2)]
    scr1 = [salloc(f"scr1{i}", [128, D], F32) for i in range(2)]

    @blk.sync
    def _(e):
        for rt in range(NRT):
            if rt >= 2:
                sp_selfb(e, s1_red, rt - 1)
            e.dma_start(out=xstg[rt % 2][:],
                        in_=x_own[rt * 128:(rt + 1) * 128, :]).then_inc(s1_ld, 16)
            trailer(e, s1_fl)
        for rt in range(NRT):
            if rt == 0:
                e.wait_ge(s1_red, NRT)
                e.wait_ge(sfl_s, 1)
            if rt >= 2:
                sp_selfb(e, s1_ln, rt - 1)
            e.dma_start(out=xstg[rt % 2][:],
                        in_=x_own[rt * 128:(rt + 1) * 128, :]).then_inc(s1_ld, 16)
            trailer(e, s1_fl)

    @blk.scalar
    def _(e):
        e.wait_ge(sfl_v, 1)
        e.activation(ascr[:], masks_sb[:, 0:1024], AF.Identity)
        e.activation(ascr[:], masks_sb[:, 0:1024], AF.Identity).then_inc(sfl_s, 1)
        e.wait_ge(sfl_v, 2)
        e.activation(ascr[:], masks_sb[:, 0:1024], AF.Identity)
        e.activation(ascr[:], masks_sb[:, 0:1024], AF.Identity)
        e.activation(st_l[:], st_v[:], AF.Ln)
        e.activation(ascr[:], masks_sb[:, 0:1024], AF.Identity)
        e.activation(ascr[:], masks_sb[:, 0:1024], AF.Identity)
        e.activation(st_r[:], st_l[:], AF.Exp, scale=-0.5)
        e.activation(ascr[:], masks_sb[:, 0:1024], AF.Identity)
        e.activation(ascr[:], masks_sb[:, 0:1024], AF.Identity).then_inc(s1_rs, 1)

    @blk.vector
    def _(e):
        for r in range(NRT):
            e.wait_ge(s1_ld, 16 * (r + 1))
            e.wait_ge(s1_fl, 16 * (r + 1))
            e.tensor_tensor(scr1[r % 2][:], xstg[r % 2][:], xstg[r % 2][:],
                            ALU.mult)
            e.reduce_sum(st_mu[:, r:r + 1], xstg[r % 2][:],
                         axis=AX).then_inc(s1_red, 1)
            if r < NRT - 1:
                e.reduce_sum(st_sq[:, r:r + 1], scr1[r % 2][:], axis=AX)
            else:
                e.reduce_sum(st_sq[:, r:r + 1], scr1[r % 2][:],
                             axis=AX).then_inc(sfl_v, 1)
        e.wait_ge(sfl_s, 1)
        e.tensor_scalar(st_muv[:], st_mu[:], 1.0 / D, None, ALU.mult)
        e.tensor_scalar(st_sqd[:], st_sq[:], 1.0 / D, None, ALU.mult)
        e.tensor_copy(bscr[:], scr1[0][:, 0:1024])
        e.tensor_copy(bscr[:], scr1[1][:, 0:1024])
        e.tensor_tensor(st_m2[:], st_muv[:], st_muv[:], ALU.mult)
        e.tensor_copy(bscr[:], scr1[0][:, 0:1024])
        e.tensor_copy(bscr[:], scr1[1][:, 0:1024])
        e.tensor_scalar(st_m2e[:], st_m2[:], EPS, None, ALU.subtract)
        e.tensor_copy(bscr[:], scr1[0][:, 0:1024])
        e.tensor_copy(bscr[:], scr1[1][:, 0:1024])
        e.tensor_tensor(st_v[:], st_sqd[:], st_m2e[:],
                        ALU.subtract).then_inc(sfl_v, 1)
        for rr in range(NRT + 2):
            rte = rr - 2
            if 0 <= rte < NRT:
                for grp4 in range(4):
                    gidx = 4 * rte + grp4
                    bank = paux if gidx % 2 == 0 else pout
                    e.wait_ge(gp1, g1_idx[("tp", gidx)])
                    dt0 = grp4 * 4
                    dst = big_a[:].rearrange("p (dt t) -> p dt t", dt=NDT)[
                        :, dt0:dt0 + 4, rte * 128:(rte + 1) * 128]
                    srcv = bank[:].rearrange("p (j t) -> p j t", j=4)
                    e.tensor_copy(dst, srcv).then_inc(s1_te, 1)
            p = rr - 1
            if 0 <= p < NRT:
                e.wait_ge(s1_ld, 16 * (NRT + p + 1))
                e.wait_ge(s1_fl, 16 * (NRT + p + 1))
                e.wait_ge(s1_rs, 1)
                if p >= 2:
                    e.wait_ge(s1_tp, 4 * (p - 1))
                e.tensor_scalar(xln[p % 2][:], xstg[p % 2][:],
                                st_muv[:, p:p + 1], st_r[:, p:p + 1],
                                ALU.subtract, ALU.mult).then_inc(s1_ln, 1)

    @blk.tensor
    def _(e):
        for rt in range(NRT):
            e.wait_ge(gp1, g1_idx[("ln", rt)])
            for grp4 in range(4):
                gidx = 4 * rt + grp4
                bank = paux if gidx % 2 == 0 else pout
                if gidx >= 2:
                    e.wait_ge(s1_te, gidx - 1)
                for j in range(4):
                    dt = grp4 * 4 + j
                    last = (j == 3)
                    ins = e.transpose(bank[:, j * 128:(j + 1) * 128],
                                      xln[rt % 2][:, dt * 128:(dt + 1) * 128],
                                      ident[:])
                    if last:
                        ins.then_inc(s1_tp, 1)

    @blk.gpsimd
    def _(e):
        for k in range(2):
            emit_grants(e, gp1, g1a_ord if k == 0 else g1b_ord, 1)
            gp_selfb(e, s1_te, 16 * (k + 1))
            for dt in range(NDT):
                e.dma_start(
                    out=ln_stage[k].ap()[dt * 128:(dt + 1) * 128, :],
                    in_=big_a[:, dt * TL + k * 512:dt * TL + (k + 1) * 512],
                ).then_inc(s1_gp, 16)
            e.wait_ge(s1_gp, 16 * NDT * (k + 1))
            e.collective_compute(
                "AllGather", ALU.bypass, replica_groups=RG,
                ins=[ln_stage[k].ap().opt()],
                outs=[ln_ag[k].ap().opt()],
            ).then_inc(cc_ln[k])

    if "lnxt" in dbg_aps:
        @blk.sync
        def _(e):
            e.wait_ge(s1_te, 32)
            e.dma_start(out=dbg_aps["lnxt"], in_=big_a[:]).then_inc(s_dbg, 16)

    if "st" in dbg_aps:
        @blk.sync
        def _(e):
            e.wait_ge(s1_te, 32)
            for ii, tile in enumerate([st_mu, st_muv, st_m2, st_m2e, st_sq,
                                       st_v, st_r]):
                e.dma_start(out=dbg_aps["st"][:, ii * NRT:(ii + 1) * NRT],
                            in_=tile[:]).then_inc(s_dbg, 16)
    if "st2" in dbg_aps:
        @blk.sync
        def _(e):
            e.wait_ge(s1_te, 32)
            e.dma_start(out=dbg_aps["st2"][:, 0:NRT],
                        in_=st_sqd[:]).then_inc(s_dbg, 16)

    if stop_phase <= 1:
        return finish()

    sfree("scr11", "scr10", "xln1", "xln0", "xstg1", "xstg0")
    wqstg = salloc("wqstg", [128, 8 * 128], F32)
    wqB = [salloc(f"wqB{i}", [128, D], BF16) for i in range(2)]

    @blk.sync
    def _(e):
        for ct in range(NDT):
            for half in range(2):
                if ct == 0 and half == 0:
                    e.wait_ge(s1_ln, NRT)
                    e.wait_ge(s1_tp, 4 * NRT)
                if ct > 0 or half > 0:
                    sp_selfb(e, s2_ca, 2 * ct + half)
                for j in range(8):
                    r0 = (half * 8 + j) * 128
                    e.dma_start(out=wqstg[:, j * 128:(j + 1) * 128],
                                in_=wq[r0:r0 + 128, ct * 128:(ct + 1) * 128]
                                ).then_inc(s2_lq, 16)
                e.dma_start(out=jnk[3:4, 0:4], in_=ident_in[0:1, 0:4]
                            ).then_inc(s2_ld, 16)
                trailer(e, s2_fl)

    @blk.vector
    def _(e):
        for ct in range(NDT):
            for half in range(2):
                e.wait_ge(s2_lq, 16 * 8 * (2 * ct + half + 1))
                e.wait_ge(s2_ld, 16 * (2 * ct + half + 1))
                e.wait_ge(s2_fl, 16 * (2 * ct + half + 1))
                if ct >= 2 and half == 0:
                    e.wait_ge(s2_mm, 2 * (ct - 1))
                e.tensor_copy(wqB[ct % 2][:, half * 1024:(half + 1) * 1024],
                              wqstg[:]).then_inc(s2_ca, 1)

    @blk.tensor
    def _(e):
        for ct in range(NDT):
            for tch in range(2):
                slot = (0 if ct % 2 == 0 else 3) + tch
                if ct == 0 and tch == 0:
                    e.wait_ge(s0_ca, 3)
                    e.wait_ge(gp1, g1_idx[("te", 0)])
                if ct == 0 and tch == 1:
                    e.wait_ge(gp1, g1_idx[("te", 1)])
                if tch == 0:
                    e.wait_ge(gp2, g2_idx[("ca", ct)])
                    if ct >= 2:
                        e.wait_ge(s2_ev, min(2 * (ct - 2) + 3, 32))
                ps = pslot(slot)
                for dt in range(NDT):
                    ins = e.matmul(ps, wqB[ct % 2][:, dt * 128:(dt + 1) * 128],
                                   big_a[:, dt * TL + tch * 512:
                                         dt * TL + (tch + 1) * 512],
                                   start=(dt == 0), stop=(dt == NDT - 1))
                    if dt == NDT - 1:
                        ins.then_inc(s2_mm, 1)

    @blk.scalar
    def _(e):
        for ct in range(NDT):
            for tch in range(2):
                slot = (0 if ct % 2 == 0 else 3) + tch
                k = 2 * ct + tch + 1
                if k <= 31:
                    e.wait_ge(s2_mm, k + 1)
                else:
                    e.wait_ge(s2_mm, 32)
                    e.dma_start(out=jnk[2:3, 0:4], in_=ident_in[0:1, 0:4]
                                ).then_inc(ga2, 16)
                    e.wait_ge(ga2, 16)
                dst = qt_sb[:].rearrange("p (hh t j) -> p hh t j",
                                         hh=HPC, j=NDT)[:, tch, :, ct]
                e.activation(dst, pslot(slot), AF.Identity,
                             bias=bqt[:, ct:ct + 1]).then_inc(s2_ev, 1)

    @blk.gpsimd
    def _(e):
        emit_grants(e, gp2, g2_ord, 1)

    if "qt" in dbg_aps:
        @blk.sync
        def _(e):
            sp_selfb(e, s2_ev, 32)
            e.dma_start(out=dbg_aps["qt"], in_=qt_sb[:]).then_inc(s_dbg, 16)

    if stop_phase <= 2:
        return finish()

    wfstg = salloc("wfstg", [128, D], F32)
    wfstg2 = salloc("wfstg2", [128, D], F32)
    bwf = salloc("bwf", [128, D], F32)
    bsgn = salloc("bsgn", [128, D], F32)
    bal = salloc("bal", [128, 4], F32)
    bal2 = salloc("bal2", [128, 2], F32)
    wo_bf = salloc("wo_bf", [128, D], BF16)
    bwt_sb = salloc("bwt_sb", [128, NDT * 256], BF16)

    @blk.sync
    def _(e):
        for rt in range(2):
            if rt > 0:
                sp_selfb(e, sw_vx, 1)
            e.dma_start(out=wfstg[:],
                        in_=wo_sl[rt * 128:(rt + 1) * 128, :]).then_inc(sw_ld, 16)
            trailer(e, sw_fl)
        sp_selfb(e, sw_vx, 2)
        e.dma_start(out=wfstg[:],
                    in_=wf_sl[0:128, :]).then_inc(sw_ld, 16)
        trailer(e, sw_fl)
        e.dma_start(out=wfstg2[:],
                    in_=wf_sl[128:256, :]).then_inc(sw_ld, 16)
        trailer(e, sw_fl)

    @blk.vector
    def _(e):
        for rt in range(2):
            e.wait_ge(sw_ld, 16 * (rt + 1))
            e.wait_ge(sw_fl, 16 * (rt + 1))
            if rt > 0:
                e.wait_ge(sw_gp, 16)
            e.tensor_copy(wo_bf[:], wfstg[:]).then_inc(sw_vx, 1)
        e.wait_ge(sw_ld, 48)
        e.wait_ge(sw_fl, 48)
        e.reduce_sum(bal[:, 0:1], wfstg[:], axis=AX,
                     apply_absolute_value=True)
        e.wait_ge(sw_ld, 64)
        e.wait_ge(sw_fl, 64)
        e.reduce_sum(bal[:, 1:2], wfstg2[:], axis=AX,
                     apply_absolute_value=True).then_inc(sfl_v, 1)
        e.wait_ge(sfl_s, 2)
        e.tensor_scalar(bal2[:, 0:1], bal[:, 0:1], 1.0 / D, None, ALU.mult)
        e.tensor_scalar(bal2[:, 1:2], bal[:, 1:2], 1.0 / D, None, ALU.mult)
        for rt in range(2):
            wsrc = wfstg if rt == 0 else wfstg2
            if rt > 0:
                e.wait_ge(sw_tp, 4)
            e.tensor_scalar(bwf[:], wsrc[:], 0.0, None, ALU.is_ge)
            e.tensor_scalar(bsgn[:], bwf[:], 2.0, 1.0,
                            ALU.mult, ALU.subtract).then_inc(sw_vx, 1)
            e.tensor_scalar(bwf[:], bsgn[:], bal2[:, rt:rt + 1], None,
                            ALU.mult).then_inc(sw_vx, 1)
            for grp4 in range(4):
                gidx = 32 + 4 * rt + grp4
                bank = paux if gidx % 2 == 0 else pout
                e.wait_ge(gpw, gw_idx[("tp", 4 * rt + grp4)])
                dt0 = grp4 * 4
                dst = bwt_sb[:].rearrange("p (dt c) -> p dt c", dt=NDT)[
                    :, dt0:dt0 + 4, rt * 128:(rt + 1) * 128]
                srcv = bank[:].rearrange("p (j t) -> p j t", j=4)
                e.tensor_copy(dst, srcv).then_inc(sw_te, 1)

    @blk.scalar
    def _(e):
        e.wait_ge(sfl_v, 3)
        e.activation(ascr[:], masks_sb[:, 0:1024], AF.Identity)
        e.activation(ascr[:], masks_sb[:, 0:1024], AF.Identity).then_inc(sfl_s, 1)

    @blk.tensor
    def _(e):
        for rt in range(2):
            e.wait_ge(gpw, gw_idx[("bw", rt)])
            for grp4 in range(4):
                gidx = 32 + 4 * rt + grp4
                bank = paux if gidx % 2 == 0 else pout
                prev = gidx - 2
                if prev < 32:
                    e.wait_ge(s1_te, prev + 1)
                else:
                    e.wait_ge(sw_te, prev - 32 + 1)
                for j in range(4):
                    dt = grp4 * 4 + j
                    ins = e.transpose(bank[:, j * 128:(j + 1) * 128],
                                      bwf[:, dt * 128:(dt + 1) * 128],
                                      ident[:])
                    if j == 3:
                        ins.then_inc(sw_tp, 1)

    @blk.gpsimd
    def _(e):
        for rt in range(2):
            gp_selfb(e, sw_vx, rt + 1)
            e.dma_start(out=wo_stage.ap()[rt * 128:(rt + 1) * 128, :],
                        in_=wo_bf[:]).then_inc(sw_gp, 16)
        e.wait_ge(sw_gp, 32)
        e.collective_compute("AllGather", ALU.bypass, replica_groups=RG,
                             ins=[wo_stage.ap().opt()],
                             outs=[wo_ag.ap().opt()]).then_inc(cc_wo)
        emit_grants(e, gpw, gw_ord, 1)
        gp_selfb(e, sw_te, 8)
        for dt in range(NDT):
            e.dma_start(out=bwt_stage.ap()[dt * 128:(dt + 1) * 128, :],
                        in_=bwt_sb[:, dt * 256:(dt + 1) * 256]
                        ).then_inc(sw_gp, 16)
        e.wait_ge(sw_gp, 32 + 16 * NDT)
        e.collective_compute("AllGather", ALU.bypass, replica_groups=RG,
                             ins=[bwt_stage.ap().opt()],
                             outs=[bwt_ag.ap().opt()]).then_inc(cc_bw)

    if "bwt" in dbg_aps:
        @blk.sync
        def _(e):
            sp_selfb(e, sw_te, 8)
            e.dma_start(out=dbg_aps["bwt"], in_=bwt_sb[:]).then_inc(s_dbg, 16)
    if "bal" in dbg_aps:
        @blk.sync
        def _(e):
            e.wait_ge(sw_te, 8)
            e.dma_start(out=dbg_aps["bal"][:, 0:4], in_=bal[:]).then_inc(s_dbg, 16)
            e.dma_start(out=dbg_aps["bal"][:, 4:6], in_=bal2[:]).then_inc(s_dbg, 16)

    if stop_phase <= 2.5:
        return finish()

    sfree("bwt_sb", "wo_bf", "bal2", "bal", "bsgn", "bwf", "wfstg2", "wfstg",
          "wqB1", "wqB0", "wqstg")
    big_b = salloc("big_b", [128, 2 * NDT * 512], BF16)
    wkB = salloc("wkB", [128, NDT * HPC * DH], BF16)
    wvB = salloc("wvB", [128, NDT * HPC * DH], BF16)
    wkvstg = salloc("wkvstg", [128, 4 * HPC * DH], F32)
    vtf = [salloc(f"vtf{h}", [128, 512], F32) for h in range(HPC)]

    @blk.sync
    def _(e):
        e.wait_ge(s2_ev, 32)
        e.wait_ge(sw_gp, 32 + 16 * NDT)
        e.wait_ge(sw_te, 8)
        for q in range(4):
            if q > 0:
                sp_selfb(e, sw_vx, 6 + q)
            for j in range(4):
                dt = 4 * q + j
                e.dma_start(out=wkvstg[:, j * 256:(j + 1) * 256],
                            in_=wk_sl[dt * 128:(dt + 1) * 128, :]
                            ).then_inc(s3_lk, 16)
            e.dma_start(out=jnk[3:4, 0:4], in_=ident_in[0:1, 0:4]
                        ).then_inc(s3_ld, 16)
            trailer(e, s3_fl)
        for i in range(16):
            k, r = i % 2, i // 2
            e.wait_ge(cc_ln[k], 1)
            if i == 1:
                for q in range(4):
                    sp_selfb(e, sw_vx, 10 + q)
                    for j in range(4):
                        dt = 4 * q + j
                        e.dma_start(out=wkvstg[:, j * 256:(j + 1) * 256],
                                    in_=wv_sl[dt * 128:(dt + 1) * 128, :]
                                    ).then_inc(s3_lv, 16)
                    e.dma_start(out=jnk[3:4, 0:4], in_=ident_in[0:1, 0:4]
                                ).then_inc(s3_ld, 16)
                    trailer(e, s3_fl)
            if i >= 2:
                e.wait_ge(s3_mm, 4 * (i - 1))
            for dt in range(NDT):
                e.dma_start(
                    out=big_b[:, (i % 2) * 8192 + dt * 512:
                              (i % 2) * 8192 + (dt + 1) * 512],
                    in_=ln_ag[k].ap()[r * D + dt * 128:r * D + (dt + 1) * 128, :]
                ).then_inc(s3_lb, 16)
            e.dma_start(out=jnk[3:4, 0:4], in_=ident_in[0:1, 0:4]
                        ).then_inc(s3_ld, 16)
            trailer(e, s3_fl)

    @blk.vector
    def _(e):
        for q in range(4):
            e.wait_ge(s3_lk, 16 * 4 * (q + 1))
            e.wait_ge(s3_ld, 16 * (q + 1))
            e.wait_ge(s3_fl, 16 * (q + 1))
            e.tensor_copy(wkB[:, q * 1024:(q + 1) * 1024],
                          wkvstg[:]).then_inc(sw_vx, 1)
        for q in range(4):
            e.wait_ge(s3_lv, 16 * 4 * (q + 1))
            e.wait_ge(s3_ld, 80 + 16 * (q + 1))
            e.wait_ge(s3_fl, 80 + 16 * (q + 1))
            e.tensor_copy(wvB[:, q * 1024:(q + 1) * 1024],
                          wkvstg[:]).then_inc(sw_vx, 1)

    @blk.tensor
    def _(e):
        for i in range(16):
            e.wait_ge(s3_lb, 16 * 16 * (i + 1))
            e.wait_ge(s3_ld, 80 if i == 0 else 144 + 16 * i)
            e.wait_ge(s3_fl, 80 if i == 0 else 144 + 16 * i)
            if i == 0:
                e.wait_ge(gp3, g3_idx[("wv",)])
                e.wait_ge(s2_ev, 32)
            for h in range(HPC):
                slot = h
                if i >= 1:
                    e.wait_ge(gp3, g3_idx[("ev", i - 1, h)])
                ps = pslot(slot)
                for dt in range(NDT):
                    ins = e.matmul(
                        ps, wkB[:, dt * 256 + h * 128:dt * 256 + (h + 1) * 128],
                        big_b[:, (i % 2) * 8192 + dt * 512:
                              (i % 2) * 8192 + (dt + 1) * 512],
                        start=(dt == 0), stop=(dt == NDT - 1))
                    if dt == NDT - 1:
                        ins.then_inc(s3_mm, 1)
            for h in range(HPC):
                slot = 2 + h
                if i >= 1:
                    e.wait_ge(gp3, g3_idx[("vf", i - 1, h)])
                ps = pslot(slot)
                for dt in range(NDT):
                    ins = e.matmul(
                        ps, wvB[:, dt * 256 + h * 128:dt * 256 + (h + 1) * 128],
                        big_b[:, (i % 2) * 8192 + dt * 512:
                              (i % 2) * 8192 + (dt + 1) * 512],
                        start=(dt == 0), stop=(dt == NDT - 1))
                    if dt == NDT - 1:
                        ins.then_inc(s3_mm, 1)
            for h in range(HPC):
                gidx = 40 + 2 * i + h
                bank = paux if gidx % 2 == 0 else pout
                e.wait_ge(gp3, g3_idx[("vf", i, h)])
                prev = gidx - 2
                if prev < 40:
                    e.wait_ge(sw_te, prev - 32 + 1)
                else:
                    e.wait_ge(s3_te, prev - 40 + 1)
                for j in range(4):
                    ins = e.transpose(bank[:, j * 128:(j + 1) * 128],
                                      vtf[h][:, j * 128:(j + 1) * 128],
                                      ident[:])
                    if j == 3:
                        ins.then_inc(s3_tp, 1)

    @blk.scalar
    def _(e):
        for i in range(16):
            for h in range(HPC):
                e.wait_ge(s3_mm, 4 * i + h + 3)
                ins = e.activation(kt_sb[:, h * T + i * 512:h * T + (i + 1) * 512],
                                   pslot(h), AF.Identity,
                                   bias=bkt[:, h:h + 1])
                ins.then_inc(s3_ev, 1)
        e.activation(jexp[:], masks_sb[:, 0:512], AF.Exp).then_inc(s4_wr, 1)

    @blk.vector
    def _(e):
        for i in range(16):
            for h in range(HPC):
                e.wait_ge(gp3, g3_idx[("vt", i, h)])
                if i >= 1:
                    e.wait_ge(s3_tp, 2 * (i - 1) + h + 1)
                e.tensor_scalar(vtf[h][:], pslot(2 + h), bvt[:, h:h + 1],
                                None, ALU.add).then_inc(s3_evV, 1)
            for h in range(HPC):
                gidx = 40 + 2 * i + h
                bank = paux if gidx % 2 == 0 else pout
                e.wait_ge(gp3, g3_idx[("tp3", i, h)])
                e.tensor_copy(v_sb[:, h * T + i * 512:h * T + (i + 1) * 512],
                              bank[:]).then_inc(s3_te, 1)

    @blk.gpsimd
    def _(e):
        emit_grants(e, gp3, g3_ord, 1)

    if "kt" in dbg_aps:
        @blk.sync
        def _(e):
            sp_selfb(e, s3_ev, 32)
            e.dma_start(out=dbg_aps["kt"], in_=kt_sb[:]).then_inc(s_dbg, 16)
    if "v" in dbg_aps:
        @blk.sync
        def _(e):
            sp_selfb(e, s3_te, 32)
            e.dma_start(out=dbg_aps["v"], in_=v_sb[:]).then_inc(s_dbg, 16)

    if stop_phase <= 3:
        return finish()

    sfree("vtf1", "vtf0", "wkvstg", "wvB", "wkB", "big_b")
    ot_sb = salloc("ot_sb", [128, HPC * T], BF16)
    exp_sb = salloc("exp_sb", [128, EXP_RING * 512], BF16)
    ptmp = salloc("ptmp", [128, 512], BF16)
    acc = salloc("acc", [128, 512], F32)
    rrow = salloc("rrow", [1, 512], F32)
    srow = salloc("srow", [1, 512], F32)
    s_dbs = sem("s_dbs") if "sums" in dbg_aps else None
    rbc = salloc("rbc", [128, 512], F32)

    @blk.tensor
    def _(e):
        e.wait_ge(s3_eg, 16)
        e.wait_ge(s3_evV, 32)
        e.wait_ge(s3_tg, 16)
        for blko in SCHED:
            b, qb, h = blko["b"], blko["qb"], blko["h"]
            e.wait_ge(s3_ev, min(2 * qb + 4, 32))
            e.wait_ge(s3_te, min(2 * qb + 4, 32))
            grps = blko["grps"]
            ng = len(grps)
            first_out = blko["outmm_after"] - blko["nt"] + 1

            def emit_L(gi):
                grp = grps[gi]
                if grp["war_eg"]:
                    e.wait_ge(s4_xr, grp["war_eg"] + 1)
                for ti, tt in enumerate(grp["tiles"]):
                    kvt = tt["kvt"]
                    ins = e.matmul(
                        pslot(tt["g"]),
                        kt_sb[:, h * T + kvt * 128:h * T + (kvt + 1) * 128],
                        qt_sb[:, h * T + qb * 512:h * T + (qb + 1) * 512],
                        start=True, stop=not tt["diag"])
                    if tt["diag"]:
                        md = tt["kvt"] - (blko["nt"] - 4)
                        ins = e.matmul(pslot(tt["g"]), identB[:],
                                       masks_sb[:, md * 512:(md + 1) * 512],
                                       start=False, stop=True)
                    if ti == len(grp["tiles"]) - 1:
                        ins.then_inc(s4_log, 1)

            def emit_O(gi):
                grp = grps[gi]
                e.wait_ge(s4_xr, grp["eg"] + 2)
                for tt in grp["tiles"]:
                    if tt["out_idx"] == first_out and b >= 1:
                        e.wait_ge(s4_div, b)
                    sl = tt["esl"]
                    e.matmul(pout[:],
                             v_sb[:, h * T + tt["kvt"] * 128:
                                  h * T + (tt["kvt"] + 1) * 128],
                             exp_sb[:, sl * 512:(sl + 1) * 512],
                             start=(tt["out_idx"] == first_out),
                             stop=(tt["out_idx"] == blko["outmm_after"]),
                             ).then_inc(s4_out, 1)

            for gi in range(min(4, ng)):
                emit_L(gi)
            for gi in range(4, ng):
                emit_O(gi - 4)
                emit_L(gi)
            for gi in range(max(0, ng - 4), ng):
                emit_O(gi)
            e.wait_ge(s4_acc, b + 1)
            if b >= 1:
                e.wait_ge(s4_div, b)
            e.matmul(paux[0:1, :], ones_c[:], acc[:],
                     start=True, stop=True).then_inc(s4_sum, 1)
            e.wait_ge(s4_se, b + 1)
            e.matmul(paux[:], ones_r[:], srow[:],
                     start=True, stop=True).then_inc(s4_bct, 1)

    @blk.scalar
    def _(e):
        e.wait_ge(s4_warm, 16)
        for blko in SCHED:
            for grp in blko["grps"]:
                e.wait_ge(s4_log, grp["gseq"] + 1)
                if grp["war_out"]:
                    e.wait_ge(s4_out, grp["war_out"])
                if grp["war_pair"]:
                    e.wait_ge(s4_prs, grp["war_pair"])
                w = 512 * len(grp["tiles"])
                esl0 = grp["esl0"]
                e.activation(exp_sb[:, esl0 * 512:esl0 * 512 + w],
                             pgroup(grp["slot0"], w), AF.Exp,
                             scale=SM_SCALE).then_inc(s4_xr, 1)
            for _ in range(3):
                e.activation(jexp[:], masks_sb[:, 0:512],
                             AF.Exp).then_inc(s4_xr, 1)
            e.wait_ge(s4_sum, blko["b"] + 1)
            e.activation(srow[:], paux[0:1, :], AF.Identity)
            e.activation(jexp[:], masks_sb[:, 0:512],
                         AF.Exp).then_inc(s4_se, 1)

    @blk.vector
    def _(e):
        for blko in SCHED:
            b, qb, h = blko["b"], blko["qb"], blko["h"]
            for pr in blko["prs"]:
                t0, t1 = pr["t0"], pr["t1"]
                e.wait_ge(s4_xr, t1["eg"] + 2)
                s0, s1x = t0["esl"], t1["esl"]
                e.tensor_tensor(ptmp[:], exp_sb[:, s0 * 512:(s0 + 1) * 512],
                                exp_sb[:, s1x * 512:(s1x + 1) * 512], ALU.add)
                if pr["idx"] == blko["pairs_after"] - blko["nt"] // 2 + 1:
                    e.tensor_copy(acc[:], ptmp[:]).then_inc(s4_prs, 1)
                else:
                    e.tensor_tensor(acc[:], acc[:], ptmp[:],
                                    ALU.add).then_inc(s4_prs, 1)
            e.tensor_copy(bscr[:], exp_sb[:, 0:1024])
            e.tensor_copy(bscr[:], exp_sb[:, 1024:2048]).then_inc(s4_acc, 1)
            if s_dbs is not None and b >= 1:
                e.wait_ge(s_dbs, 16 * b)
            e.wait_ge(s4_bct, b + 1)
            e.tensor_copy(bscr[:], exp_sb[:, 0:1024])

            e.reciprocal(rbc[:], paux[:]).then_inc(s4_rcp, 1)
            e.wait_ge(s4_out, blko["outmm_after"])
            e.tensor_tensor(ot_sb[:, h * T + qb * 512:h * T + (qb + 1) * 512],
                            pout[:], rbc[:], ALU.mult)
            e.tensor_copy(bscr[:], exp_sb[:, 1024:2048]).then_inc(s4_div, 1)
        e.tensor_copy(bscr[:], exp_sb[:, 0:1024]).then_inc(s4_div, 1)

    @blk.sync
    def _(e):
        def bounce(sem_raw, val, sem_granted):
            e.wait_ge(sem_raw, val)
            e.dma_start(out=jnk[0:1, 0:4], in_=ident_in[0:1, 0:4]
                        ).then_inc(sem_granted, 16)

        chain_n = [0]

        def long_bounce(sem_raw, val, sem_granted, n=5):
            e.wait_ge(sem_raw, val)
            for _ in range(n):
                chain_n[0] += 16
                e.dma_start(out=jnk[0:1, 0:4], in_=ident_in[0:1, 0:4]
                            ).then_inc(s_chain, 16)
                e.wait_ge(s_chain, chain_n[0])
            e.dma_start(out=jnk[0:1, 0:4], in_=ident_in[0:1, 0:4]
                        ).then_inc(sem_granted, 16)

        long_bounce(s3_ev, 32, s3_eg)
        long_bounce(s3_te, 32, s3_tg)
        long_bounce(s4_wr, 1, s4_warm)
        if s_dbs is not None:
            for blko in SCHED:
                b = blko["b"]
                e.wait_ge(s4_se, b + 1)
                e.dma_start(out=dbg_aps["sums"][b:b + 1, :], in_=srow[:]
                            ).then_inc(s_dbs, 16)

    @blk.gpsimd
    def _(e):
        for j in range(4):
            gp_selfb(e, s4_div, 8 * (j + 1))
            for h in range(HPC):
                e.dma_start(out=ot_stage[j].ap()[h * 128:(h + 1) * 128, :],
                            in_=ot_sb[:, h * T + j * 2048:h * T + (j + 1) * 2048]
                            ).then_inc(s4_gp, 16)
            e.wait_ge(s4_gp, 32 * (j + 1))
            e.collective_compute(
                "AllGather", ALU.bypass, replica_groups=RG,
                ins=[ot_stage[j].ap().opt()],
                outs=[ot_ag.ap()[j * 2048:(j + 1) * 2048, :].opt()],
            ).then_inc(cc_ot[j])

    if "ot" in dbg_aps:
        @blk.sync
        def _(e):
            sp_selfb(e, s4_div, 32)
            e.dma_start(out=dbg_aps["ot"], in_=ot_sb[:]).then_inc(s_dbg, 16)
    for _nm, _sb in (("kt4", kt_sb), ("v4", v_sb), ("qt4", qt_sb)):
        if _nm in dbg_aps:
            @blk.sync
            def _(e, _sb=_sb, _nm=_nm):
                e.wait_ge(s4_div, 33)
                e.dma_start(out=dbg_aps[_nm], in_=_sb[:]).then_inc(s_dbg, 16)
    if "sums" in dbg_aps:
        @blk.sync
        def _(e):
            e.wait_ge(s_dbs, 16 * 32)
            e.dma_start(out=jnk[0:1, 0:4], in_=ident_in[0:1, 0:4]
                        ).then_inc(s_dbg, 16)

    if stop_phase <= 4:
        return finish()

    sfree("rbc", "srow", "rrow", "acc", "ptmp", "exp_sb", "ot_sb",
          "v_sb", "kt_sb", "qt_sb")
    x1_sb = salloc("x1_sb", [128, NRT * D], F32)
    wob = salloc("wob", [128, 2 * NDT * 512], BF16)

    @blk.gpsimd
    def _(e):
        for j in range(4):
            e.wait_ge(cc_ot[j], 1)
        for ht in range(NH):
            src = ot_ag.ap().rearrange("a (s t) -> (a s) t", s=2)
            e.indirect_dma_start(
                big_a[:, ht * 1024:(ht + 1) * 1024], None, src,
                bass.IndirectOffsetOnAxis(ap=poix[:, ht:ht + 1], axis=0),
            ).then_inc(s5_gp, 16)
            e.dma_start(out=jnk[1:2, 0:4], in_=ident_in[0:1, 0:4]
                        ).then_inc(s5_gf, 16)
        emit_grants(e, gp5, g5_ord, 1)

    @blk.sync
    def _(e):
        e.wait_ge(s4_div, 32)
        for qt in range(NRT):
            e.dma_start(out=x1_sb[:, qt * D:(qt + 1) * D],
                        in_=x_own[qt * 128:(qt + 1) * 128, :]).then_inc(s5_ld, 16)
            trailer(e, s5_fl)
        e.wait_ge(cc_wo, 1)
        for och in range(4):
            if och >= 2:
                e.wait_ge(s5_mm, 8 * (och - 1))
            for ht in range(NH):
                e.dma_start(
                    out=wob[:, (och % 2) * 8192 + ht * 512:
                            (och % 2) * 8192 + (ht + 1) * 512],
                    in_=wo_ag.ap()[ht * 128:(ht + 1) * 128,
                                   och * 512:(och + 1) * 512],
                ).then_inc(s5_ld, 16)
                trailer(e, s5_fl)

    @blk.tensor
    def _(e):
        e.wait_ge(s4_xr, SCHED[-1]["eg_after"])
        e.wait_ge(s4_div, 32)
        e.wait_ge(s5_gp, 16 * NH)
        e.wait_ge(s5_gf, 16 * NH)
        for och in range(4):
            e.wait_ge(s5_ld, 16 * (NRT + NH * (och + 1)))
            e.wait_ge(s5_fl, 16 * (NRT + NH * (och + 1)))
            for qt in range(NRT):
                m = och * NRT + qt
                if m >= LOG_RING:
                    e.wait_ge(s5_ev, m - LOG_RING + 1)
                ps = pslot(m)
                for ht in range(NH):
                    e.matmul(ps, big_a[:, ht * 1024 + qt * 128:
                                       ht * 1024 + (qt + 1) * 128],
                             wob[:, (och % 2) * 8192 + ht * 512:
                                 (och % 2) * 8192 + (ht + 1) * 512],
                             start=(ht == 0), stop=False)
                e.matmul(ps, ones_rb[:], bor[0:1, och * 512:(och + 1) * 512],
                         start=False, stop=True).then_inc(s5_mm, 1)

    @blk.vector
    def _(e):
        for och in range(4):
            for qt in range(NRT):
                m = och * NRT + qt
                if m < 30:
                    e.wait_ge(s5_mm, m + 2)
                else:
                    e.wait_ge(gp5, g5_idx[("mm", m)])
                if och == 0:
                    e.wait_ge(s5_ld, 16 * (qt + 1))
                    e.wait_ge(s5_fl, 16 * (qt + 1))
                sl = x1_sb[:, qt * D + och * 512:qt * D + (och + 1) * 512]
                e.tensor_tensor(sl, sl, pslot(m), ALU.add).then_inc(s5_ev, 1)

    if stop_phase <= 5:
        if "x1" in dbg_aps:
            @blk.sync
            def _(e):
                sp_selfb(e, s5_ev, 32)
                e.dma_start(out=dbg_aps["x1"], in_=x1_sb[:]).then_inc(s_dbg, 16)
        return finish()

    x2ln = [salloc(f"x2ln{i}", [128, D], F32) for i in range(2)]
    scr6 = [salloc(f"scr6{i}", [128, D], F32) for i in range(2)]
    t_mu = salloc("t_mu", [128, NRT], F32)
    t_muv = salloc("t_muv", [128, NRT], F32)
    t_m2 = salloc("t_m2", [128, NRT], F32)
    t_m2e = salloc("t_m2e", [128, NRT], F32)
    t_sq = salloc("t_sq", [128, NRT], F32)
    t_v = salloc("t_v", [128, NRT], F32)
    t_l = salloc("t_l", [128, NRT], F32)
    t_r = salloc("t_r", [128, NRT], F32)

    @blk.scalar
    def _(e):
        e.wait_ge(sfl_v, 4)
        e.activation(ascr[:], masks_sb[:, 0:1024], AF.Identity)
        e.activation(ascr[:], masks_sb[:, 0:1024], AF.Identity).then_inc(sfl_s, 1)
        e.wait_ge(sfl_v, 5)
        e.activation(ascr[:], masks_sb[:, 0:1024], AF.Identity)
        e.activation(ascr[:], masks_sb[:, 0:1024], AF.Identity)
        e.activation(t_l[:], t_v[:], AF.Ln)
        e.activation(ascr[:], masks_sb[:, 0:1024], AF.Identity)
        e.activation(ascr[:], masks_sb[:, 0:1024], AF.Identity)
        e.activation(t_r[:], t_l[:], AF.Exp, scale=-0.5)
        e.activation(ascr[:], masks_sb[:, 0:1024], AF.Identity)
        e.activation(ascr[:], masks_sb[:, 0:1024], AF.Identity).then_inc(s6_rs, 1)

    @blk.vector
    def _(e):
        for r in range(NRT):
            e.wait_ge(s5_ev, 24 + r + 1)
            e.tensor_tensor(scr6[r % 2][:], x1_sb[:, r * D:(r + 1) * D],
                            x1_sb[:, r * D:(r + 1) * D], ALU.mult)
            e.reduce_sum(t_mu[:, r:r + 1], x1_sb[:, r * D:(r + 1) * D],
                         axis=AX)
            if r < NRT - 1:
                e.reduce_sum(t_sq[:, r:r + 1], scr6[r % 2][:], axis=AX)
            else:
                e.reduce_sum(t_sq[:, r:r + 1], scr6[r % 2][:],
                             axis=AX).then_inc(sfl_v, 1)
        e.wait_ge(sfl_s, 3)
        e.tensor_scalar(t_muv[:], t_mu[:], 1.0 / D, None, ALU.mult)
        e.tensor_scalar(t_sqd[:], t_sq[:], 1.0 / D, None, ALU.mult)
        e.tensor_copy(bscr[:], scr6[0][:, 0:1024])
        e.tensor_copy(bscr[:], scr6[1][:, 0:1024])
        e.tensor_tensor(t_m2[:], t_muv[:], t_muv[:], ALU.mult)
        e.tensor_copy(bscr[:], scr6[0][:, 0:1024])
        e.tensor_copy(bscr[:], scr6[1][:, 0:1024])
        e.tensor_scalar(t_m2e[:], t_m2[:], EPS, None, ALU.subtract)
        e.tensor_copy(bscr[:], scr6[0][:, 0:1024])
        e.tensor_copy(bscr[:], scr6[1][:, 0:1024])
        e.tensor_tensor(t_v[:], t_sqd[:], t_m2e[:],
                        ALU.subtract).then_inc(sfl_v, 1)
        for rr in range(NRT + 2):
            p = rr - 1
            if 0 <= p < NRT:
                e.wait_ge(s6_rs, 1)
                if p >= 2:
                    e.wait_ge(s6_tp, 4 * (p - 1))
                e.tensor_scalar(x2ln[p % 2][:], x1_sb[:, p * D:(p + 1) * D],
                                t_muv[:, p:p + 1], t_r[:, p:p + 1],
                                ALU.subtract, ALU.mult).then_inc(s6_ln, 1)
                if p == NRT - 1:
                    e.tensor_copy(bscr[:], scr6[0][:, 0:1024]).then_inc(s6_ln, 1)
            rte = rr - 2
            if 0 <= rte < NRT:
                e.wait_ge(s5_mm, 24 + rte + 1)
                if rte == 0:
                    e.tensor_copy(bscr[:], scr6[1][:, 0:1024])
                for grp4 in range(4):
                    gidx = 4 * rte + grp4
                    bank = paux if gidx % 2 == 0 else pout
                    e.wait_ge(s6_tp, gidx + 1)
                    dt0 = grp4 * 4
                    dst = big_a[:].rearrange("p (dt t) -> p dt t", dt=NDT)[
                        :, dt0:dt0 + 4, rte * 128:(rte + 1) * 128]
                    srcv = bank[:].rearrange("p (j t) -> p j t", j=4)
                    e.tensor_copy(dst, srcv).then_inc(s6_te, 1)

    @blk.tensor
    def _(e):
        for rt in range(NRT):
            e.wait_ge(s6_ln, rt + 2)
            for grp4 in range(4):
                gidx = 4 * rt + grp4
                bank = paux if gidx % 2 == 0 else pout
                if gidx >= 2:
                    e.wait_ge(s6_te, gidx - 1)
                elif rt == 0 and grp4 == 0:
                    e.wait_ge(s4_div, 32)
                for j in range(4):
                    dt = grp4 * 4 + j
                    ins = e.transpose(bank[:, j * 128:(j + 1) * 128],
                                      x2ln[rt % 2][:, dt * 128:(dt + 1) * 128],
                                      ident[:])
                    if j == 3:
                        ins.then_inc(s6_tp, 1)

    @blk.sync
    def _(e):
        e.wait_ge(cc_bw, 1)
        e.wait_ge(s5_mm, 32)
        for och in range(4):
            if och >= 2:
                e.wait_ge(s7_mm, 8 * (och - 1))
            for dt in range(NDT):
                for rr in range(2):
                    r = 2 * och + rr
                    e.dma_start(
                        out=wob[:, (och % 2) * 8192 + dt * 512 + rr * 256:
                                (och % 2) * 8192 + dt * 512 + (rr + 1) * 256],
                        in_=bwt_ag.ap()[r * D + dt * 128:
                                        r * D + (dt + 1) * 128, :],
                    ).then_inc(s7_ld, 16)
                    trailer(e, s7_fl)

    @blk.tensor
    def _(e):
        e.wait_ge(s5_ev, 32)
        for och in range(4):
            e.wait_ge(s7_ld, 16 * 32 * (och + 1))
            e.wait_ge(s7_fl, 16 * 32 * (och + 1))
            for qt in range(NRT):
                m = och * NRT + qt
                if och == 0:
                    if qt < NRT - 1:
                        e.wait_ge(s6_te, 4 * qt + 5)
                    else:
                        e.wait_ge(gp6, g6_idx[("te6", qt)])
                if m >= LOG_RING:
                    e.wait_ge(s7_ev, m - LOG_RING + 1)
                ps = pslot(m)
                for dt in range(NDT):
                    e.matmul(ps, big_a[:, dt * 1024 + qt * 128:
                                       dt * 1024 + (qt + 1) * 128],
                             wob[:, (och % 2) * 8192 + dt * 512:
                                 (och % 2) * 8192 + (dt + 1) * 512],
                             start=(dt == 0), stop=False)
                e.matmul(ps, ones_rb[:], bfr[0:1, och * 512:(och + 1) * 512],
                         start=False, stop=True).then_inc(s7_mm, 1)

    @blk.vector
    def _(e):
        for och in range(4):
            for qt in range(NRT):
                m = och * NRT + qt
                if m < 30:
                    e.wait_ge(s7_mm, m + 2)
                else:
                    e.wait_ge(gp7, g7_idx[("mm", m)])
                sl = x1_sb[:, qt * D + och * 512:qt * D + (och + 1) * 512]
                e.tensor_tensor(sl, sl, pslot(m), ALU.add).then_inc(s7_ev, 1)

    @blk.gpsimd
    def _(e):
        emit_grants(e, gp6, g6_ord, 1)
        emit_grants(e, gp7, g7_ord, 1)

    @blk.sync
    def _(e):
        for qt in range(NRT):
            if qt < NRT - 2:
                e.wait_ge(s7_ev, 24 + qt + 3)
            else:
                sp_selfb(e, s7_ev, 32)
            e.dma_start(out=y[qt * 128:(qt + 1) * 128, :],
                        in_=x1_sb[:, qt * D:(qt + 1) * D]).then_inc(s_st, 16)

    return finish(extra_waits=[(s_st, 16 * NRT)])



def make_in_maps(inputs):
    x = np.asarray(inputs["L0_attn_x"])
    wq = np.ascontiguousarray(np.asarray(inputs["L0_attn_Wq"]), dtype=np.float32)
    wk = np.asarray(inputs["L0_attn_Wk"])
    wv = np.asarray(inputs["L0_attn_Wv"])
    wo = np.asarray(inputs["L0_attn_Wo"])
    wf = np.asarray(inputs["L0_ffn_weight"])
    bq = np.asarray(inputs["L0_attn_bq"])
    bk = np.asarray(inputs["L0_attn_bk"])
    bv = np.asarray(inputs["L0_attn_bv"])
    bo = np.asarray(inputs["L0_attn_bo"])
    bf = np.asarray(inputs["L0_ffn_bias"])

    xf = np.asarray(x).reshape(T, D)
    masks = np.zeros((128, 4 * QB), dtype=ml_dtypes.bfloat16)
    p = np.arange(128)[:, None]
    i = np.arange(QB)[None, :]
    for jd in range(4):
        masks[:, jd * QB:(jd + 1) * QB] = np.where(
            i >= jd * 128 + p, 0.0, -30000.0).astype(ml_dtypes.bfloat16)
    ident = np.eye(128, dtype=np.float32)
    identb = np.eye(128, dtype=ml_dtypes.bfloat16)
    ones_r = np.ones((1, 128), np.float32)
    ones_c = np.ones((128, 1), np.float32)
    bq_t = np.ascontiguousarray(np.asarray(bq).reshape(NDT, 128).T,
                                dtype=np.float32)
    bo_r = np.ascontiguousarray(np.asarray(bo).reshape(1, D), dtype=np.float32)
    bf_r = np.ascontiguousarray(np.asarray(bf).reshape(1, D), dtype=np.float32)

    in_maps = []
    for c in range(NC):
        d = np.arange(128)[:, None]
        ht = np.arange(NH)[None, :]
        po_idx = (((c // 2) * 2048 + (ht // 2) * 256 + (ht % 2) * 128 + d) * 2
                  + (c % 2)).astype(np.uint32)
        in_maps.append({
            "x_own": np.ascontiguousarray(xf[c * TL:(c + 1) * TL, :],
                                          dtype=np.float32),
            "wq": wq,
            "wk_sl": np.ascontiguousarray(wk[:, c * 256:(c + 1) * 256],
                                          dtype=np.float32),
            "wv_sl": np.ascontiguousarray(wv[:, c * 256:(c + 1) * 256],
                                          dtype=np.float32),
            "wo_sl": np.ascontiguousarray(wo[c * 256:(c + 1) * 256, :],
                                          dtype=np.float32),
            "wf_sl": np.ascontiguousarray(wf[c * 256:(c + 1) * 256, :],
                                          dtype=np.float32),
            "bq_t": bq_t,
            "bk_t": np.ascontiguousarray(
                np.asarray(bk)[c * 256:(c + 1) * 256].reshape(HPC, 128).T,
                dtype=np.float32),
            "bv_t": np.ascontiguousarray(
                np.asarray(bv)[c * 256:(c + 1) * 256].reshape(HPC, 128).T,
                dtype=np.float32),
            "bo_r": bo_r,
            "bf_r": bf_r,
            "masks": masks,
            "ident": ident,
            "identb": identb,
            "ones_r": ones_r,
            "ones_c": ones_c,
            "po_idx": np.ascontiguousarray(po_idx),
        })
    return in_maps


def kernel(**inputs):
    nc, _ = build_nc()
    in_maps = make_in_maps(inputs)
    res = run_bass_kernel_spmd(nc, in_maps, core_ids=list(range(NC)))
    out = np.concatenate([np.asarray(res.results[c]["y"]) for c in range(NC)],
                         axis=0)
    return out.reshape(1, T, D).astype(np.float32, copy=False)


# version 13 (final)
# speedup vs baseline: 1.2485x; 1.0132x over previous
import numpy as np
import ml_dtypes
import concourse.bass as bass
import concourse.mybir as mybir
from concourse.bass_utils import run_bass_kernel_spmd

F32 = mybir.dt.float32
BF16 = mybir.dt.bfloat16
U32 = mybir.dt.uint32
AF = mybir.ActivationFunctionType
ALU = mybir.AluOpType
AX = mybir.AxisListType.X

NC = 8
D = 2048
T = 8192
TL = T // NC
DH = 128
NH = 16
HPC = NH // NC
NDT = D // 128
QB = 512
NQB = T // QB
NRT = TL // 128
RG = [list(range(NC))]
EPS = 1e-5
SM_SCALE = float(np.float32(1.0) / np.sqrt(np.float32(DH)))
LOG_RING = 6
EXP_RING = 12
GRP = 3


def _attention_schedule():
    blocks = []
    g = 0
    outmm = pairs = masks = groups_done = 0
    outmm_of, pair_of = {}, {}
    for qb in range(NQB):
        for h in range(HPC):
            if g % GRP:
                g += GRP - (g % GRP)
            nt = 4 * (qb + 1)
            g0 = g
            tiles = []
            for i in range(nt):
                gg = g0 + i
                diag = i >= nt - 4
                mi = None
                if diag:
                    masks += 1
                    mi = masks
                outmm += 1
                outmm_of[gg] = outmm
                tiles.append(dict(g=gg, kvt=i, diag=diag, mask_idx=mi,
                                  out_idx=outmm, esl=gg % EXP_RING))
            grps = []
            for i in range(0, nt, GRP):
                sub = tiles[i:i + GRP]
                grps.append(dict(gseq=groups_done, tiles=sub,
                                 slot0=sub[0]["g"] % LOG_RING,
                                 esl0=sub[0]["g"] % EXP_RING))
                for tt in sub:
                    tt["gseq"] = groups_done
                groups_done += 1
            prs = []
            for p in range(nt // 2):
                t0, t1 = tiles[2 * p], tiles[2 * p + 1]
                pairs += 1
                pair_of[t0["g"]] = pairs
                pair_of[t1["g"]] = pairs
                t0["pair_idx"] = pairs
                t1["pair_idx"] = pairs
                prs.append(dict(t0=t0, t1=t1, idx=pairs))
            blocks.append(dict(qb=qb, h=h, b=len(blocks), nt=nt, grps=grps,
                               tiles=tiles, prs=prs, groups_after=groups_done,
                               outmm_after=outmm, pairs_after=pairs,
                               masks_after=masks))
            g = g0 + nt
    ecount = 0
    allg = []
    for blk_ in blocks:
        for grp in blk_["grps"]:
            ecount += 1
            grp["eg"] = ecount
            for tt in grp["tiles"]:
                tt["eg"] = ecount
            allg.append(grp)
        ecount += 3
        blk_["eg_after"] = ecount
    for gi, grp in enumerate(allg):
        grp["war_eg"] = allg[gi - 2]["eg"] if gi >= 2 else 0
    for blk_ in blocks:
        for grp in blk_["grps"]:
            wo = wp = 0
            for tt in grp["tiles"]:
                prev = tt["g"] - EXP_RING
                wo = max(wo, outmm_of.get(prev, 0))
                wp = max(wp, pair_of.get(prev, 0))
            grp["war_out"] = wo
            grp["war_pair"] = wp
    return blocks


SCHED = _attention_schedule()


def pair_of_tile(blko, tt):
    return tt["pair_idx"]

N_GROUPS = SCHED[-1]["groups_after"]
N_OUTMM = SCHED[-1]["outmm_after"]
N_PAIRS = SCHED[-1]["pairs_after"]


def build_nc(stop_phase=9, dbg=()):
    nc = bass.Bass("TRN2", target_bir_lowering=False, debug=False,
                   detect_race_conditions=False)

    x_own = nc.dram_tensor("x_own", [TL, D], F32, kind="ExternalInput").ap()
    wq = nc.dram_tensor("wq", [D, D], F32, kind="ExternalInput").ap()
    wk_sl = nc.dram_tensor("wk_sl", [D, HPC * DH], F32, kind="ExternalInput").ap()
    wv_sl = nc.dram_tensor("wv_sl", [D, HPC * DH], F32, kind="ExternalInput").ap()
    wo_sl = nc.dram_tensor("wo_sl", [D // NC, D], F32, kind="ExternalInput").ap()
    wf_sl = nc.dram_tensor("wf_sl", [D // NC, D], F32, kind="ExternalInput").ap()
    bq_t = nc.dram_tensor("bq_t", [128, NDT], F32, kind="ExternalInput").ap()
    bk_t = nc.dram_tensor("bk_t", [128, HPC], F32, kind="ExternalInput").ap()
    bv_t = nc.dram_tensor("bv_t", [128, HPC], F32, kind="ExternalInput").ap()
    bo_r = nc.dram_tensor("bo_r", [1, D], F32, kind="ExternalInput").ap()
    bf_r = nc.dram_tensor("bf_r", [1, D], F32, kind="ExternalInput").ap()
    masks_in = nc.dram_tensor("masks", [128, 4 * QB], BF16, kind="ExternalInput").ap()
    ident_in = nc.dram_tensor("ident", [128, 128], F32, kind="ExternalInput").ap()
    identb_in = nc.dram_tensor("identb", [128, 128], BF16, kind="ExternalInput").ap()
    ones_r_in = nc.dram_tensor("ones_r", [1, 128], F32, kind="ExternalInput").ap()
    ones_c_in = nc.dram_tensor("ones_c", [128, 1], F32, kind="ExternalInput").ap()
    po_idx_in = nc.dram_tensor("po_idx", [128, NH], U32, kind="ExternalInput").ap()

    y = nc.dram_tensor("y", [TL, D], F32, kind="ExternalOutput").ap()

    ln_stage = [nc.dram_tensor(f"ln_stage{k}", [D, QB], BF16) for k in range(2)]
    ln_ag = [nc.dram_tensor(f"ln_ag{k}", [NC * D, QB], BF16,
                            addr_space="Shared") for k in range(2)]
    wo_stage = nc.dram_tensor("wo_stage", [D // NC, D], BF16)
    wo_ag = nc.dram_tensor("wo_ag", [D, D], BF16, addr_space="Shared")
    bwt_stage = nc.dram_tensor("bwt_stage", [D, D // NC], BF16)
    bwt_ag = nc.dram_tensor("bwt_ag", [NC * D, D // NC], BF16,
                            addr_space="Shared")
    ot_stage = [nc.dram_tensor(f"ot_stage{j}", [HPC * DH, 4 * QB], BF16)
                for j in range(4)]
    ot_ag = nc.dram_tensor("ot_ag", [T, 4 * QB], BF16,
                           addr_space="Shared")

    dbg_aps = {}

    def dbg_t(name, shape, dtype):
        if name in dbg:
            dbg_aps[name] = nc.dram_tensor("dbg_" + name, shape, dtype,
                                           kind="ExternalOutput").ap()

    dbg_t("lnxt", [128, NDT * TL], BF16)
    dbg_t("st", [128, 7 * NRT], F32)
    dbg_t("qt", [128, HPC * T], BF16)
    dbg_t("kt", [128, HPC * T], BF16)
    dbg_t("v", [128, HPC * T], BF16)
    dbg_t("ot", [128, HPC * T], BF16)
    dbg_t("bwt", [128, NDT * 256], BF16)
    dbg_t("bal", [128, 8], F32)
    dbg_t("st2", [128, 8 * NRT], F32)
    dbg_t("sums", [32, 512], F32)
    dbg_t("kt4", [128, 2 * 8192], BF16)
    dbg_t("v4", [128, 2 * 8192], BF16)
    dbg_t("qt4", [128, 2 * 8192], BF16)
    dbg_t("x1", [128, NRT * D], F32)

    cms = []
    cm = {}

    def salloc(name, shape, dtype, side="left"):
        c = nc.sbuf_tensor(name, shape, dtype, side=side)
        t = c.__enter__()
        cm[name] = c
        return t

    def sfree(*names):
        for n in names:
            cm.pop(n).__exit__(None, None, None)

    big_a = salloc("big_a", [128, NDT * TL], BF16)
    masks_sb = salloc("masks_sb", [128, 4 * QB], BF16)
    ident = salloc("ident_sb", [128, 128], F32)
    identB = salloc("identB_sb", [128, 128], BF16)
    ones_r = salloc("ones_r_sb", [1, 128], F32)
    ones_rb = salloc("ones_rb_sb", [1, 128], BF16)
    ones_c = salloc("ones_c_sb", [128, 1], F32)
    bqt = salloc("bqt_sb", [128, NDT], F32)
    bkt = salloc("bkt_sb", [128, HPC], F32)
    bvt = salloc("bvt_sb", [128, HPC], F32)
    bor = salloc("bor_sb", [1, D], BF16)
    bfr = salloc("bfr_sb", [1, D], BF16)
    poix = salloc("poix_sb", [128, NH], U32)
    st_mu = salloc("st_mu", [128, NRT], F32)
    st_muv = salloc("st_muv", [128, NRT], F32)
    st_m2 = salloc("st_m2", [128, NRT], F32)
    st_m2e = salloc("st_m2e", [128, NRT], F32)
    st_sq = salloc("st_sq", [128, NRT], F32)
    st_v = salloc("st_v", [128, NRT], F32)
    st_l = salloc("st_l", [128, NRT], F32)
    st_r = salloc("st_r", [128, NRT], F32)
    st_sqd = salloc("st_sqd", [128, NRT], F32)
    t_sqd = salloc("t_sqd", [128, NRT], F32)
    jnk = salloc("jnk", [128, 4], F32)
    jnk2 = salloc("jnk2", [128, 4], F32)
    jexp = salloc("jexp", [128, 512], BF16)
    bscr = salloc("bscr", [128, 1024], F32)
    ascr = salloc("ascr", [128, 1024], F32)

    pA_cm = nc.psum_tensor("pA", [128, 3 * 512], F32)
    pB_cm = nc.psum_tensor("pB", [128, 3 * 512], F32)
    pout_cm = nc.psum_tensor("pout", [128, 512], F32)
    paux_cm = nc.psum_tensor("paux", [128, 512], F32)
    pA, pB = pA_cm.__enter__(), pB_cm.__enter__()
    pout, paux = pout_cm.__enter__(), paux_cm.__enter__()
    cms += [pA_cm, pB_cm, pout_cm, paux_cm]

    def pslot(i):
        i = i % LOG_RING
        t = pA if i < 3 else pB
        return t[:, (i % 3) * 512:((i % 3) + 1) * 512]

    def pgroup(slot0, w):
        t = pA if slot0 == 0 else pB
        return t[:, :w]

    semcms = []

    def sem(name):
        c = nc.semaphore(name)
        s = c.__enter__()
        semcms.append(c)
        return s

    s0_ld = sem("s0_ld"); s0_ca = sem("s0_ca")
    s1_ld = sem("s1_ld"); s1_sq = sem("s1_sq"); s1_s2 = sem("s1_s2")
    s1_red = sem("s1_red"); s1_rs = sem("s1_rs")
    s1_st = sem("s1_st"); s1_ln = sem("s1_ln"); s1_tp = sem("s1_tp")
    s1_te = sem("s1_te"); s1_gp = sem("s1_gp")
    cc_ln = [sem("cc_ln0"), sem("cc_ln1")]
    s2_ld = sem("s2_ld"); s2_ca = sem("s2_ca"); s2_mm = sem("s2_mm")
    s2_ev = sem("s2_ev")
    sw_ld = sem("sw_ld"); sw_vx = sem("sw_vx"); sw_tp = sem("sw_tp")
    sw_te = sem("sw_te"); sw_gp = sem("sw_gp")
    cc_wo = sem("cc_wo"); cc_bw = sem("cc_bw")
    s3_ld = sem("s3_ld"); s3_mm = sem("s3_mm"); s3_ev = sem("s3_ev")
    s3_evV = sem("s3_evV"); s3_tp = sem("s3_tp"); s3_te = sem("s3_te")
    s4_log = sem("s4_log")
    s4_xr = sem("s4_xr")
    s4_acc = sem("s4_acc")
    s3_eg = sem("s3_eg"); s3_tg = sem("s3_tg")
    s4_wr = sem("s4_wr"); s4_warm = sem("s4_warm"); s_chain = sem("s_chain")
    gp1 = sem("gp1"); gp2 = sem("gp2"); gpw = sem("gpw"); gp3 = sem("gp3")
    g4b = sem("g4b"); gp5 = sem("gp5"); gp6 = sem("gp6"); gp7 = sem("gp7")
    ga2 = sem("ga2"); ga3 = sem("ga3"); s_stb = sem("s_stb")
    s0_fl = sem("s0_fl"); s1_fl = sem("s1_fl"); s2_fl = sem("s2_fl")
    s3_fl = sem("s3_fl"); sw_fl = sem("sw_fl"); s5_fl = sem("s5_fl")
    s7_fl = sem("s7_fl"); s5_gf = sem("s5_gf")
    s3_lk = sem("s3_lk"); s3_lv = sem("s3_lv"); s3_lb = sem("s3_lb")
    s2_lq = sem("s2_lq")
    s4_prs = sem("s4_prs"); s4_out = sem("s4_out"); s4_sum = sem("s4_sum")
    s4_se = sem("s4_se")
    s4_rcp = sem("s4_rcp"); s4_bct = sem("s4_bct"); s4_div = sem("s4_div")
    s4_gp = sem("s4_gp")
    cc_ot = [sem(f"cc_ot{j}") for j in range(4)]
    s5_ld = sem("s5_ld"); s5_gp = sem("s5_gp"); s5_mm = sem("s5_mm")
    s5_ev = sem("s5_ev")
    s6_rs = sem("s6_rs"); s6_ln = sem("s6_ln"); s6_tp = sem("s6_tp"); s6_te = sem("s6_te")
    s7_ld = sem("s7_ld"); s7_mm = sem("s7_mm"); s7_ev = sem("s7_ev")
    s_st = sem("s_st")
    s_dbg = sem("s_dbg")
    sfl_v = sem("sfl_v"); sfl_s = sem("sfl_s")

    def _mkord(entries):
        return entries, {k: 16 * (i + 1) for i, (k, _, _) in enumerate(entries)}

    _o = []
    for _p in range(NRT):
        _o.append((("ln", _p), s1_ln, _p + 1))
        for _g in range(4 * _p, 4 * _p + 4):
            _o.append((("tp", _g), s1_tp, _g + 1))
        if _p == 3:
            _o.append((("te", 0), s1_te, 16))
    _o.append((("te", 1), s1_te, 32))
    g1_ord, g1_idx = _mkord(_o)
    g1a_ord = g1_ord[:21]
    g1b_ord = g1_ord[21:]

    g2_ord, g2_idx = _mkord([(("ca", _c), s2_ca, 2 * _c + 2)
                             for _c in range(NDT)])

    _o = [(("bw", 0), sw_vx, 4)]
    _o += [(("tp", _k), sw_tp, _k + 1) for _k in range(4)]
    _o.append((("bw", 1), sw_vx, 6))
    _o += [(("tp", _k), sw_tp, _k + 1) for _k in range(4, 8)]
    gw_ord, gw_idx = _mkord(_o)

    _o = [(("wv",), sw_vx, 14)]
    for _i in range(16):
        for _h in range(HPC):
            _o.append((("vt", _i, _h), s3_mm, 4 * _i + 2 + _h + 1))
        for _h in range(HPC):
            _o.append((("vf", _i, _h), s3_evV, 2 * _i + _h + 1))
        for _h in range(HPC):
            _o.append((("tp3", _i, _h), s3_tp, 2 * _i + _h + 1))
        for _h in range(HPC):
            _o.append((("ev", _i, _h), s3_ev, 2 * _i + _h + 1))
    g3_ord, g3_idx = _mkord(_o)

    g5_ord, g5_idx = _mkord([(("mm", _m), s5_mm, _m + 1) for _m in range(32)])

    _o = []
    for _p in range(NRT):
        _o.append((("ln6", _p), s6_ln, _p + 1))
        for _g in range(4 * _p, 4 * _p + 4):
            _o.append((("tp6", _g), s6_tp, _g + 1))
        _o.append((("te6", _p), s6_te, 4 * _p + 4))
    g6_ord, g6_idx = _mkord(_o)

    g7_ord, g7_idx = _mkord([(("mm", _m), s7_mm, _m + 1) for _m in range(32)])

    def emit_grants(e, gsem, entries, row):
        for _, raw, val in entries:
            e.wait_ge(raw, val)
            e.dma_start(out=jnk[row:row + 1, 0:4], in_=ident_in[0:1, 0:4]
                        ).then_inc(gsem, 16)

    def trailer(e, fl):
        e.dma_start(out=jnk[3:4, 0:4], in_=ident_in[0:1, 0:4]).then_inc(fl, 16)

    gsb_n = [0]
    spb_n = [0]

    def sp_selfb(e, raw, val):
        e.wait_ge(raw, val)
        e.dma_start(out=jnk[3:4, 0:4], in_=ident_in[0:1, 0:4]).then_inc(s_stb, 16)
        spb_n[0] += 16
        e.wait_ge(s_stb, spb_n[0])

    def gp_selfb(e, raw, val):
        e.wait_ge(raw, val)
        e.dma_start(out=jnk[1:2, 0:4], in_=ident_in[0:1, 0:4]).then_inc(g4b, 16)
        gsb_n[0] += 16
        e.wait_ge(g4b, gsb_n[0])

    N_P0C = 9
    blk_cm = nc.Block()
    blk = blk_cm.__enter__()

    def finish(extra_waits=()):
        ndbg = len(dbg_aps)

        @blk.gpsimd
        def _(e):
            for s, v in extra_waits:
                e.wait_ge(s, v)
            if ndbg:
                e.wait_ge(s_dbg, 16 * ndbg)

        blk_cm.__exit__(None, None, None)
        return nc, dbg_aps

    btmp = salloc("btmp", [1, D], F32)

    @blk.sync
    def _(e):
        p0list = [(masks_sb, masks_in), (ident, ident_in),
                  (identB, identb_in),
                  (ones_r, ones_r_in), (ones_c, ones_c_in),
                  (bqt, bq_t), (bkt, bk_t), (bvt, bv_t),
                  (poix, po_idx_in)]
        assert len(p0list) == N_P0C
        for dst, src in p0list:
            e.dma_start(out=dst[:], in_=src).then_inc(s0_ld, 16)
            trailer(e, s0_fl)
        e.dma_start(out=btmp[:], in_=bo_r).then_inc(s0_ld, 16)
        trailer(e, s0_fl)
        sp_selfb(e, s0_ca, 1)
        e.dma_start(out=btmp[:], in_=bf_r).then_inc(s0_ld, 16)
        trailer(e, s0_fl)

    @blk.vector
    def _(e):
        e.wait_ge(s0_ld, 16 * (N_P0C + 1))
        e.wait_ge(s0_fl, 16 * (N_P0C + 1))
        e.tensor_copy(bor[:], btmp[:]).then_inc(s0_ca, 1)
        e.wait_ge(s0_ld, 16 * (N_P0C + 2))
        e.wait_ge(s0_fl, 16 * (N_P0C + 2))
        e.tensor_copy(bfr[:], btmp[:]).then_inc(s0_ca, 1)
        e.tensor_copy(ones_rb[:], ones_r[:]).then_inc(s0_ca, 1)

    sfree("btmp")
    qt_sb = salloc("qt_sb", [128, HPC * T], BF16)
    kt_sb = salloc("kt_sb", [128, HPC * T], BF16)
    v_sb = salloc("v_sb", [128, HPC * T], BF16)
    xstg = [salloc(f"xstg{i}", [128, D], F32) for i in range(2)]
    xln = [salloc(f"xln{i}", [128, D], F32) for i in range(2)]
    scr1 = [salloc(f"scr1{i}", [128, D], F32) for i in range(2)]

    @blk.sync
    def _(e):
        for rt in range(NRT):
            if rt >= 2:
                sp_selfb(e, s1_red, rt - 1)
            e.dma_start(out=xstg[rt % 2][:],
                        in_=x_own[rt * 128:(rt + 1) * 128, :]).then_inc(s1_ld, 16)
            trailer(e, s1_fl)
        for rt in range(NRT):
            if rt == 0:
                e.wait_ge(s1_red, NRT)
                e.wait_ge(sfl_s, 1)
            if rt >= 2:
                sp_selfb(e, s1_ln, rt - 1)
            e.dma_start(out=xstg[rt % 2][:],
                        in_=x_own[rt * 128:(rt + 1) * 128, :]).then_inc(s1_ld, 16)
            trailer(e, s1_fl)

    @blk.scalar
    def _(e):
        e.wait_ge(sfl_v, 1)
        e.activation(ascr[:], masks_sb[:, 0:1024], AF.Identity)
        e.activation(ascr[:], masks_sb[:, 0:1024], AF.Identity).then_inc(sfl_s, 1)
        e.wait_ge(sfl_v, 2)
        e.activation(ascr[:], masks_sb[:, 0:1024], AF.Identity)
        e.activation(ascr[:], masks_sb[:, 0:1024], AF.Identity)
        e.activation(st_l[:], st_v[:], AF.Ln)
        e.activation(ascr[:], masks_sb[:, 0:1024], AF.Identity)
        e.activation(ascr[:], masks_sb[:, 0:1024], AF.Identity)
        e.activation(st_r[:], st_l[:], AF.Exp, scale=-0.5)
        e.activation(ascr[:], masks_sb[:, 0:1024], AF.Identity)
        e.activation(ascr[:], masks_sb[:, 0:1024], AF.Identity).then_inc(s1_rs, 1)

    @blk.vector
    def _(e):
        for r in range(NRT):
            e.wait_ge(s1_ld, 16 * (r + 1))
            e.wait_ge(s1_fl, 16 * (r + 1))
            e.tensor_tensor(scr1[r % 2][:], xstg[r % 2][:], xstg[r % 2][:],
                            ALU.mult)
            e.reduce_sum(st_mu[:, r:r + 1], xstg[r % 2][:],
                         axis=AX).then_inc(s1_red, 1)
            if r < NRT - 1:
                e.reduce_sum(st_sq[:, r:r + 1], scr1[r % 2][:], axis=AX)
            else:
                e.reduce_sum(st_sq[:, r:r + 1], scr1[r % 2][:],
                             axis=AX).then_inc(sfl_v, 1)
        e.wait_ge(sfl_s, 1)
        e.tensor_scalar(st_muv[:], st_mu[:], 1.0 / D, None, ALU.mult)
        e.tensor_scalar(st_sqd[:], st_sq[:], 1.0 / D, None, ALU.mult)
        e.tensor_copy(bscr[:], scr1[0][:, 0:1024])
        e.tensor_copy(bscr[:], scr1[1][:, 0:1024])
        e.tensor_tensor(st_m2[:], st_muv[:], st_muv[:], ALU.mult)
        e.tensor_copy(bscr[:], scr1[0][:, 0:1024])
        e.tensor_copy(bscr[:], scr1[1][:, 0:1024])
        e.tensor_scalar(st_m2e[:], st_m2[:], EPS, None, ALU.subtract)
        e.tensor_copy(bscr[:], scr1[0][:, 0:1024])
        e.tensor_copy(bscr[:], scr1[1][:, 0:1024])
        e.tensor_tensor(st_v[:], st_sqd[:], st_m2e[:],
                        ALU.subtract).then_inc(sfl_v, 1)
        for rr in range(NRT + 2):
            rte = rr - 2
            if 0 <= rte < NRT:
                for grp4 in range(4):
                    gidx = 4 * rte + grp4
                    bank = paux if gidx % 2 == 0 else pout
                    e.wait_ge(gp1, g1_idx[("tp", gidx)])
                    dt0 = grp4 * 4
                    dst = big_a[:].rearrange("p (dt t) -> p dt t", dt=NDT)[
                        :, dt0:dt0 + 4, rte * 128:(rte + 1) * 128]
                    srcv = bank[:].rearrange("p (j t) -> p j t", j=4)
                    e.tensor_copy(dst, srcv).then_inc(s1_te, 1)
            p = rr - 1
            if 0 <= p < NRT:
                e.wait_ge(s1_ld, 16 * (NRT + p + 1))
                e.wait_ge(s1_fl, 16 * (NRT + p + 1))
                e.wait_ge(s1_rs, 1)
                if p >= 2:
                    e.wait_ge(s1_tp, 4 * (p - 1))
                e.tensor_scalar(xln[p % 2][:], xstg[p % 2][:],
                                st_muv[:, p:p + 1], st_r[:, p:p + 1],
                                ALU.subtract, ALU.mult).then_inc(s1_ln, 1)

    @blk.tensor
    def _(e):
        for rt in range(NRT):
            e.wait_ge(gp1, g1_idx[("ln", rt)])
            for grp4 in range(4):
                gidx = 4 * rt + grp4
                bank = paux if gidx % 2 == 0 else pout
                if gidx >= 2:
                    e.wait_ge(s1_te, gidx - 1)
                for j in range(4):
                    dt = grp4 * 4 + j
                    last = (j == 3)
                    ins = e.transpose(bank[:, j * 128:(j + 1) * 128],
                                      xln[rt % 2][:, dt * 128:(dt + 1) * 128],
                                      ident[:])
                    if last:
                        ins.then_inc(s1_tp, 1)

    @blk.gpsimd
    def _(e):
        for k in range(2):
            emit_grants(e, gp1, g1a_ord if k == 0 else g1b_ord, 1)
            gp_selfb(e, s1_te, 16 * (k + 1))
            for dt in range(NDT):
                e.dma_start(
                    out=ln_stage[k].ap()[dt * 128:(dt + 1) * 128, :],
                    in_=big_a[:, dt * TL + k * 512:dt * TL + (k + 1) * 512],
                ).then_inc(s1_gp, 16)
            e.wait_ge(s1_gp, 16 * NDT * (k + 1))
            e.collective_compute(
                "AllGather", ALU.bypass, replica_groups=RG,
                ins=[ln_stage[k].ap().opt()],
                outs=[ln_ag[k].ap().opt()],
            ).then_inc(cc_ln[k])

    if "lnxt" in dbg_aps:
        @blk.sync
        def _(e):
            e.wait_ge(s1_te, 32)
            e.dma_start(out=dbg_aps["lnxt"], in_=big_a[:]).then_inc(s_dbg, 16)

    if "st" in dbg_aps:
        @blk.sync
        def _(e):
            e.wait_ge(s1_te, 32)
            for ii, tile in enumerate([st_mu, st_muv, st_m2, st_m2e, st_sq,
                                       st_v, st_r]):
                e.dma_start(out=dbg_aps["st"][:, ii * NRT:(ii + 1) * NRT],
                            in_=tile[:]).then_inc(s_dbg, 16)
    if "st2" in dbg_aps:
        @blk.sync
        def _(e):
            e.wait_ge(s1_te, 32)
            e.dma_start(out=dbg_aps["st2"][:, 0:NRT],
                        in_=st_sqd[:]).then_inc(s_dbg, 16)

    if stop_phase <= 1:
        return finish()

    sfree("scr11", "scr10", "xln1", "xln0", "xstg1", "xstg0")
    wqstg = salloc("wqstg", [128, 8 * 128], F32)
    wqB = [salloc(f"wqB{i}", [128, D], BF16) for i in range(2)]

    @blk.sync
    def _(e):
        for ct in range(NDT):
            for half in range(2):
                if ct == 0 and half == 0:
                    e.wait_ge(s1_ln, NRT)
                    e.wait_ge(s1_tp, 4 * NRT)
                if ct > 0 or half > 0:
                    sp_selfb(e, s2_ca, 2 * ct + half)
                for j in range(8):
                    r0 = (half * 8 + j) * 128
                    e.dma_start(out=wqstg[:, j * 128:(j + 1) * 128],
                                in_=wq[r0:r0 + 128, ct * 128:(ct + 1) * 128]
                                ).then_inc(s2_lq, 16)
                e.dma_start(out=jnk[3:4, 0:4], in_=ident_in[0:1, 0:4]
                            ).then_inc(s2_ld, 16)
                trailer(e, s2_fl)

    @blk.vector
    def _(e):
        for ct in range(NDT):
            for half in range(2):
                e.wait_ge(s2_lq, 16 * 8 * (2 * ct + half + 1))
                e.wait_ge(s2_ld, 16 * (2 * ct + half + 1))
                e.wait_ge(s2_fl, 16 * (2 * ct + half + 1))
                if ct >= 2 and half == 0:
                    e.wait_ge(s2_mm, 2 * (ct - 1))
                e.tensor_copy(wqB[ct % 2][:, half * 1024:(half + 1) * 1024],
                              wqstg[:]).then_inc(s2_ca, 1)

    @blk.tensor
    def _(e):
        for ct in range(NDT):
            for tch in range(2):
                slot = (0 if ct % 2 == 0 else 3) + tch
                if ct == 0 and tch == 0:
                    e.wait_ge(s0_ca, 3)
                    e.wait_ge(gp1, g1_idx[("te", 0)])
                if ct == 0 and tch == 1:
                    e.wait_ge(gp1, g1_idx[("te", 1)])
                if tch == 0:
                    e.wait_ge(gp2, g2_idx[("ca", ct)])
                    if ct >= 2:
                        e.wait_ge(s2_ev, min(2 * (ct - 2) + 3, 32))
                ps = pslot(slot)
                for dt in range(NDT):
                    ins = e.matmul(ps, wqB[ct % 2][:, dt * 128:(dt + 1) * 128],
                                   big_a[:, dt * TL + tch * 512:
                                         dt * TL + (tch + 1) * 512],
                                   start=(dt == 0), stop=(dt == NDT - 1))
                    if dt == NDT - 1:
                        ins.then_inc(s2_mm, 1)

    @blk.scalar
    def _(e):
        for ct in range(NDT):
            for tch in range(2):
                slot = (0 if ct % 2 == 0 else 3) + tch
                k = 2 * ct + tch + 1
                if k <= 31:
                    e.wait_ge(s2_mm, k + 1)
                else:
                    e.wait_ge(s2_mm, 32)
                    e.dma_start(out=jnk[2:3, 0:4], in_=ident_in[0:1, 0:4]
                                ).then_inc(ga2, 16)
                    e.wait_ge(ga2, 16)
                dst = qt_sb[:].rearrange("p (hh t j) -> p hh t j",
                                         hh=HPC, j=NDT)[:, tch, :, ct]
                e.activation(dst, pslot(slot), AF.Identity,
                             bias=bqt[:, ct:ct + 1]).then_inc(s2_ev, 1)

    @blk.gpsimd
    def _(e):
        emit_grants(e, gp2, g2_ord, 1)

    if "qt" in dbg_aps:
        @blk.sync
        def _(e):
            sp_selfb(e, s2_ev, 32)
            e.dma_start(out=dbg_aps["qt"], in_=qt_sb[:]).then_inc(s_dbg, 16)

    if stop_phase <= 2:
        return finish()

    wfstg = salloc("wfstg", [128, D], F32)
    wfstg2 = salloc("wfstg2", [128, D], F32)
    bwf = salloc("bwf", [128, D], F32)
    bsgn = salloc("bsgn", [128, D], F32)
    bal = salloc("bal", [128, 4], F32)
    bal2 = salloc("bal2", [128, 2], F32)
    wo_bf = salloc("wo_bf", [128, D], BF16)
    bwt_sb = salloc("bwt_sb", [128, NDT * 256], BF16)

    @blk.sync
    def _(e):
        for rt in range(2):
            if rt > 0:
                sp_selfb(e, sw_vx, 1)
            e.dma_start(out=wfstg[:],
                        in_=wo_sl[rt * 128:(rt + 1) * 128, :]).then_inc(sw_ld, 16)
            trailer(e, sw_fl)
        sp_selfb(e, sw_vx, 2)
        e.dma_start(out=wfstg[:],
                    in_=wf_sl[0:128, :]).then_inc(sw_ld, 16)
        trailer(e, sw_fl)
        e.dma_start(out=wfstg2[:],
                    in_=wf_sl[128:256, :]).then_inc(sw_ld, 16)
        trailer(e, sw_fl)

    @blk.vector
    def _(e):
        for rt in range(2):
            e.wait_ge(sw_ld, 16 * (rt + 1))
            e.wait_ge(sw_fl, 16 * (rt + 1))
            if rt > 0:
                e.wait_ge(sw_gp, 16)
            e.tensor_copy(wo_bf[:], wfstg[:]).then_inc(sw_vx, 1)
        e.wait_ge(sw_ld, 48)
        e.wait_ge(sw_fl, 48)
        e.reduce_sum(bal[:, 0:1], wfstg[:], axis=AX,
                     apply_absolute_value=True)
        e.wait_ge(sw_ld, 64)
        e.wait_ge(sw_fl, 64)
        e.reduce_sum(bal[:, 1:2], wfstg2[:], axis=AX,
                     apply_absolute_value=True).then_inc(sfl_v, 1)
        e.wait_ge(sfl_s, 2)
        e.tensor_scalar(bal2[:, 0:1], bal[:, 0:1], 1.0 / D, None, ALU.mult)
        e.tensor_scalar(bal2[:, 1:2], bal[:, 1:2], 1.0 / D, None, ALU.mult)
        for rt in range(2):
            wsrc = wfstg if rt == 0 else wfstg2
            if rt > 0:
                e.wait_ge(sw_tp, 4)
            e.tensor_scalar(bwf[:], wsrc[:], 0.0, None, ALU.is_ge)
            e.tensor_scalar(bsgn[:], bwf[:], 2.0, 1.0,
                            ALU.mult, ALU.subtract).then_inc(sw_vx, 1)
            e.tensor_scalar(bwf[:], bsgn[:], bal2[:, rt:rt + 1], None,
                            ALU.mult).then_inc(sw_vx, 1)
            for grp4 in range(4):
                gidx = 32 + 4 * rt + grp4
                bank = paux if gidx % 2 == 0 else pout
                e.wait_ge(gpw, gw_idx[("tp", 4 * rt + grp4)])
                dt0 = grp4 * 4
                dst = bwt_sb[:].rearrange("p (dt c) -> p dt c", dt=NDT)[
                    :, dt0:dt0 + 4, rt * 128:(rt + 1) * 128]
                srcv = bank[:].rearrange("p (j t) -> p j t", j=4)
                e.tensor_copy(dst, srcv).then_inc(sw_te, 1)

    @blk.scalar
    def _(e):
        e.wait_ge(sfl_v, 3)
        e.activation(ascr[:], masks_sb[:, 0:1024], AF.Identity)
        e.activation(ascr[:], masks_sb[:, 0:1024], AF.Identity).then_inc(sfl_s, 1)

    @blk.tensor
    def _(e):
        for rt in range(2):
            e.wait_ge(gpw, gw_idx[("bw", rt)])
            for grp4 in range(4):
                gidx = 32 + 4 * rt + grp4
                bank = paux if gidx % 2 == 0 else pout
                prev = gidx - 2
                if prev < 32:
                    e.wait_ge(s1_te, prev + 1)
                else:
                    e.wait_ge(sw_te, prev - 32 + 1)
                for j in range(4):
                    dt = grp4 * 4 + j
                    ins = e.transpose(bank[:, j * 128:(j + 1) * 128],
                                      bwf[:, dt * 128:(dt + 1) * 128],
                                      ident[:])
                    if j == 3:
                        ins.then_inc(sw_tp, 1)

    @blk.gpsimd
    def _(e):
        for rt in range(2):
            gp_selfb(e, sw_vx, rt + 1)
            e.dma_start(out=wo_stage.ap()[rt * 128:(rt + 1) * 128, :],
                        in_=wo_bf[:]).then_inc(sw_gp, 16)
        e.wait_ge(sw_gp, 32)
        e.collective_compute("AllGather", ALU.bypass, replica_groups=RG,
                             ins=[wo_stage.ap().opt()],
                             outs=[wo_ag.ap().opt()]).then_inc(cc_wo)
        emit_grants(e, gpw, gw_ord, 1)
        gp_selfb(e, sw_te, 8)
        for dt in range(NDT):
            e.dma_start(out=bwt_stage.ap()[dt * 128:(dt + 1) * 128, :],
                        in_=bwt_sb[:, dt * 256:(dt + 1) * 256]
                        ).then_inc(sw_gp, 16)
        e.wait_ge(sw_gp, 32 + 16 * NDT)
        e.collective_compute("AllGather", ALU.bypass, replica_groups=RG,
                             ins=[bwt_stage.ap().opt()],
                             outs=[bwt_ag.ap().opt()]).then_inc(cc_bw)

    if "bwt" in dbg_aps:
        @blk.sync
        def _(e):
            sp_selfb(e, sw_te, 8)
            e.dma_start(out=dbg_aps["bwt"], in_=bwt_sb[:]).then_inc(s_dbg, 16)
    if "bal" in dbg_aps:
        @blk.sync
        def _(e):
            e.wait_ge(sw_te, 8)
            e.dma_start(out=dbg_aps["bal"][:, 0:4], in_=bal[:]).then_inc(s_dbg, 16)
            e.dma_start(out=dbg_aps["bal"][:, 4:6], in_=bal2[:]).then_inc(s_dbg, 16)

    if stop_phase <= 2.5:
        return finish()

    sfree("bwt_sb", "wo_bf", "bal2", "bal", "bsgn", "bwf", "wfstg2", "wfstg",
          "wqB1", "wqB0", "wqstg")
    big_b = salloc("big_b", [128, 2 * NDT * 512], BF16)
    wkB = salloc("wkB", [128, NDT * HPC * DH], BF16)
    wvB = salloc("wvB", [128, NDT * HPC * DH], BF16)
    wkvstg = salloc("wkvstg", [128, 4 * HPC * DH], F32)
    vtf = [salloc(f"vtf{h}", [128, 512], F32) for h in range(HPC)]

    @blk.sync
    def _(e):
        e.wait_ge(s2_ev, 32)
        e.wait_ge(sw_gp, 32 + 16 * NDT)
        e.wait_ge(sw_te, 8)
        for q in range(4):
            if q > 0:
                sp_selfb(e, sw_vx, 6 + q)
            for j in range(4):
                dt = 4 * q + j
                e.dma_start(out=wkvstg[:, j * 256:(j + 1) * 256],
                            in_=wk_sl[dt * 128:(dt + 1) * 128, :]
                            ).then_inc(s3_lk, 16)
            e.dma_start(out=jnk[3:4, 0:4], in_=ident_in[0:1, 0:4]
                        ).then_inc(s3_ld, 16)
            trailer(e, s3_fl)
        for i in range(16):
            k, r = i % 2, i // 2
            e.wait_ge(cc_ln[k], 1)
            if i == 1:
                for q in range(4):
                    sp_selfb(e, sw_vx, 10 + q)
                    for j in range(4):
                        dt = 4 * q + j
                        e.dma_start(out=wkvstg[:, j * 256:(j + 1) * 256],
                                    in_=wv_sl[dt * 128:(dt + 1) * 128, :]
                                    ).then_inc(s3_lv, 16)
                    e.dma_start(out=jnk[3:4, 0:4], in_=ident_in[0:1, 0:4]
                                ).then_inc(s3_ld, 16)
                    trailer(e, s3_fl)
            if i >= 2:
                e.wait_ge(s3_mm, 4 * (i - 1))
            for dt in range(NDT):
                e.dma_start(
                    out=big_b[:, (i % 2) * 8192 + dt * 512:
                              (i % 2) * 8192 + (dt + 1) * 512],
                    in_=ln_ag[k].ap()[r * D + dt * 128:r * D + (dt + 1) * 128, :]
                ).then_inc(s3_lb, 16)
            e.dma_start(out=jnk[3:4, 0:4], in_=ident_in[0:1, 0:4]
                        ).then_inc(s3_ld, 16)
            trailer(e, s3_fl)

    @blk.vector
    def _(e):
        for q in range(4):
            e.wait_ge(s3_lk, 16 * 4 * (q + 1))
            e.wait_ge(s3_ld, 16 * (q + 1))
            e.wait_ge(s3_fl, 16 * (q + 1))
            e.tensor_copy(wkB[:, q * 1024:(q + 1) * 1024],
                          wkvstg[:]).then_inc(sw_vx, 1)
        for q in range(4):
            e.wait_ge(s3_lv, 16 * 4 * (q + 1))
            e.wait_ge(s3_ld, 80 + 16 * (q + 1))
            e.wait_ge(s3_fl, 80 + 16 * (q + 1))
            e.tensor_copy(wvB[:, q * 1024:(q + 1) * 1024],
                          wkvstg[:]).then_inc(sw_vx, 1)

    @blk.tensor
    def _(e):
        for i in range(16):
            e.wait_ge(s3_lb, 16 * 16 * (i + 1))
            e.wait_ge(s3_ld, 80 if i == 0 else 144 + 16 * i)
            e.wait_ge(s3_fl, 80 if i == 0 else 144 + 16 * i)
            if i == 0:
                e.wait_ge(gp3, g3_idx[("wv",)])
                e.wait_ge(s2_ev, 32)
            for h in range(HPC):
                slot = h
                if i >= 1:
                    e.wait_ge(gp3, g3_idx[("ev", i - 1, h)])
                ps = pslot(slot)
                for dt in range(NDT):
                    ins = e.matmul(
                        ps, wkB[:, dt * 256 + h * 128:dt * 256 + (h + 1) * 128],
                        big_b[:, (i % 2) * 8192 + dt * 512:
                              (i % 2) * 8192 + (dt + 1) * 512],
                        start=(dt == 0), stop=(dt == NDT - 1))
                    if dt == NDT - 1:
                        ins.then_inc(s3_mm, 1)
            for h in range(HPC):
                slot = 2 + h
                if i >= 1:
                    e.wait_ge(gp3, g3_idx[("vf", i - 1, h)])
                ps = pslot(slot)
                for dt in range(NDT):
                    ins = e.matmul(
                        ps, wvB[:, dt * 256 + h * 128:dt * 256 + (h + 1) * 128],
                        big_b[:, (i % 2) * 8192 + dt * 512:
                              (i % 2) * 8192 + (dt + 1) * 512],
                        start=(dt == 0), stop=(dt == NDT - 1))
                    if dt == NDT - 1:
                        ins.then_inc(s3_mm, 1)
            for h in range(HPC):
                gidx = 40 + 2 * i + h
                bank = paux if gidx % 2 == 0 else pout
                e.wait_ge(gp3, g3_idx[("vf", i, h)])
                prev = gidx - 2
                if prev < 40:
                    e.wait_ge(sw_te, prev - 32 + 1)
                else:
                    e.wait_ge(s3_te, prev - 40 + 1)
                for j in range(4):
                    ins = e.transpose(bank[:, j * 128:(j + 1) * 128],
                                      vtf[h][:, j * 128:(j + 1) * 128],
                                      ident[:])
                    if j == 3:
                        ins.then_inc(s3_tp, 1)

    @blk.scalar
    def _(e):
        for i in range(16):
            for h in range(HPC):
                e.wait_ge(s3_mm, 4 * i + h + 3)
                ins = e.activation(kt_sb[:, h * T + i * 512:h * T + (i + 1) * 512],
                                   pslot(h), AF.Identity,
                                   bias=bkt[:, h:h + 1])
                ins.then_inc(s3_ev, 1)
        e.activation(jexp[:], masks_sb[:, 0:512], AF.Exp).then_inc(s4_wr, 1)

    @blk.vector
    def _(e):
        for i in range(16):
            for h in range(HPC):
                e.wait_ge(gp3, g3_idx[("vt", i, h)])
                if i >= 1:
                    e.wait_ge(s3_tp, 2 * (i - 1) + h + 1)
                e.tensor_scalar(vtf[h][:], pslot(2 + h), bvt[:, h:h + 1],
                                None, ALU.add).then_inc(s3_evV, 1)
            for h in range(HPC):
                gidx = 40 + 2 * i + h
                bank = paux if gidx % 2 == 0 else pout
                e.wait_ge(gp3, g3_idx[("tp3", i, h)])
                e.tensor_copy(v_sb[:, h * T + i * 512:h * T + (i + 1) * 512],
                              bank[:]).then_inc(s3_te, 1)

    @blk.gpsimd
    def _(e):
        emit_grants(e, gp3, g3_ord, 1)

    if "kt" in dbg_aps:
        @blk.sync
        def _(e):
            sp_selfb(e, s3_ev, 32)
            e.dma_start(out=dbg_aps["kt"], in_=kt_sb[:]).then_inc(s_dbg, 16)
    if "v" in dbg_aps:
        @blk.sync
        def _(e):
            sp_selfb(e, s3_te, 32)
            e.dma_start(out=dbg_aps["v"], in_=v_sb[:]).then_inc(s_dbg, 16)

    if stop_phase <= 3:
        return finish()

    sfree("vtf1", "vtf0", "wkvstg", "wvB", "wkB", "big_b")
    ot_sb = salloc("ot_sb", [128, HPC * T], BF16)
    exp_sb = salloc("exp_sb", [128, EXP_RING * 512], BF16)
    ptmp = salloc("ptmp", [128, 512], BF16)
    acc = salloc("acc", [128, 512], F32)
    rrow = salloc("rrow", [1, 512], F32)
    srow = salloc("srow", [1, 512], F32)
    s_dbs = sem("s_dbs") if "sums" in dbg_aps else None
    rbc = salloc("rbc", [128, 512], F32)

    @blk.tensor
    def _(e):
        e.wait_ge(s3_eg, 16)
        e.wait_ge(s3_evV, 32)
        e.wait_ge(s3_tg, 16)
        for blko in SCHED:
            b, qb, h = blko["b"], blko["qb"], blko["h"]
            e.wait_ge(s3_ev, min(2 * qb + 4, 32))
            e.wait_ge(s3_te, min(2 * qb + 4, 32))
            grps = blko["grps"]
            ng = len(grps)
            first_out = blko["outmm_after"] - blko["nt"] + 1

            def emit_L(gi):
                grp = grps[gi]
                if grp["war_eg"]:
                    e.wait_ge(s4_xr, grp["war_eg"] + 1)
                for ti, tt in enumerate(grp["tiles"]):
                    kvt = tt["kvt"]
                    ins = e.matmul(
                        pslot(tt["g"]),
                        kt_sb[:, h * T + kvt * 128:h * T + (kvt + 1) * 128],
                        qt_sb[:, h * T + qb * 512:h * T + (qb + 1) * 512],
                        start=True, stop=not tt["diag"])
                    if tt["diag"]:
                        md = tt["kvt"] - (blko["nt"] - 4)
                        ins = e.matmul(pslot(tt["g"]), identB[:],
                                       masks_sb[:, md * 512:(md + 1) * 512],
                                       start=False, stop=True)
                    if ti == len(grp["tiles"]) - 1:
                        ins.then_inc(s4_log, 1)

            def emit_O(gi):
                grp = grps[gi]
                e.wait_ge(s4_xr, grp["eg"] + 1)
                for tt in grp["tiles"]:
                    if tt["out_idx"] == first_out and b >= 1:
                        e.wait_ge(s4_div, b)
                    sl = tt["esl"]
                    e.matmul(pout[:],
                             v_sb[:, h * T + tt["kvt"] * 128:
                                  h * T + (tt["kvt"] + 1) * 128],
                             exp_sb[:, sl * 512:(sl + 1) * 512],
                             start=(tt["out_idx"] == first_out),
                             stop=(tt["out_idx"] == blko["outmm_after"]),
                             ).then_inc(s4_out, 1)

            for gi in range(min(4, ng)):
                emit_L(gi)
            for gi in range(4, ng):
                emit_O(gi - 4)
                emit_L(gi)
            for gi in range(max(0, ng - 4), ng):
                emit_O(gi)
            e.wait_ge(s4_acc, b + 1)
            if b >= 1:
                e.wait_ge(s4_div, b)
            e.matmul(paux[0:1, :], ones_c[:], acc[:],
                     start=True, stop=True).then_inc(s4_sum, 1)
            e.wait_ge(s4_se, b + 1)
            e.matmul(paux[:], ones_r[:], srow[:],
                     start=True, stop=True).then_inc(s4_bct, 1)

    @blk.scalar
    def _(e):
        e.wait_ge(s4_warm, 16)
        for blko in SCHED:
            for grp in blko["grps"]:
                e.wait_ge(s4_log, grp["gseq"] + 1)
                if grp["war_out"]:
                    e.wait_ge(s4_out, grp["war_out"])
                if grp["war_pair"]:
                    e.wait_ge(s4_prs, grp["war_pair"])
                w = 512 * len(grp["tiles"])
                esl0 = grp["esl0"]
                e.activation(exp_sb[:, esl0 * 512:esl0 * 512 + w],
                             pgroup(grp["slot0"], w), AF.Exp,
                             scale=SM_SCALE).then_inc(s4_xr, 1)
            for _ in range(3):
                e.activation(jexp[:], masks_sb[:, 0:512],
                             AF.Exp).then_inc(s4_xr, 1)
            e.wait_ge(s4_sum, blko["b"] + 1)
            e.activation(srow[:], paux[0:1, :], AF.Identity)
            e.activation(jexp[:], masks_sb[:, 0:512],
                         AF.Exp).then_inc(s4_se, 1)

    @blk.vector
    def _(e):
        for blko in SCHED:
            b, qb, h = blko["b"], blko["qb"], blko["h"]
            for pr in blko["prs"]:
                t0, t1 = pr["t0"], pr["t1"]
                e.wait_ge(s4_xr, t1["eg"] + 1)
                s0, s1x = t0["esl"], t1["esl"]
                e.tensor_tensor(ptmp[:], exp_sb[:, s0 * 512:(s0 + 1) * 512],
                                exp_sb[:, s1x * 512:(s1x + 1) * 512], ALU.add)
                if pr["idx"] == blko["pairs_after"] - blko["nt"] // 2 + 1:
                    e.tensor_copy(acc[:], ptmp[:]).then_inc(s4_prs, 1)
                else:
                    e.tensor_tensor(acc[:], acc[:], ptmp[:],
                                    ALU.add).then_inc(s4_prs, 1)
            e.tensor_copy(bscr[:], exp_sb[:, 0:1024])
            e.tensor_copy(bscr[:], exp_sb[:, 1024:2048]).then_inc(s4_acc, 1)
            if s_dbs is not None and b >= 1:
                e.wait_ge(s_dbs, 16 * b)
            e.wait_ge(s4_bct, b + 1)
            e.tensor_copy(bscr[:], exp_sb[:, 0:1024])

            e.reciprocal(rbc[:], paux[:]).then_inc(s4_rcp, 1)
            e.wait_ge(s4_out, blko["outmm_after"])
            e.tensor_tensor(ot_sb[:, h * T + qb * 512:h * T + (qb + 1) * 512],
                            pout[:], rbc[:], ALU.mult)
            e.tensor_copy(bscr[:], exp_sb[:, 1024:2048]).then_inc(s4_div, 1)
        e.tensor_copy(bscr[:], exp_sb[:, 0:1024]).then_inc(s4_div, 1)

    @blk.sync
    def _(e):
        def bounce(sem_raw, val, sem_granted):
            e.wait_ge(sem_raw, val)
            e.dma_start(out=jnk[0:1, 0:4], in_=ident_in[0:1, 0:4]
                        ).then_inc(sem_granted, 16)

        chain_n = [0]

        def long_bounce(sem_raw, val, sem_granted, n=5):
            e.wait_ge(sem_raw, val)
            for _ in range(n):
                chain_n[0] += 16
                e.dma_start(out=jnk[0:1, 0:4], in_=ident_in[0:1, 0:4]
                            ).then_inc(s_chain, 16)
                e.wait_ge(s_chain, chain_n[0])
            e.dma_start(out=jnk[0:1, 0:4], in_=ident_in[0:1, 0:4]
                        ).then_inc(sem_granted, 16)

        long_bounce(s3_ev, 32, s3_eg)
        long_bounce(s3_te, 32, s3_tg)
        long_bounce(s4_wr, 1, s4_warm)
        if s_dbs is not None:
            for blko in SCHED:
                b = blko["b"]
                e.wait_ge(s4_se, b + 1)
                e.dma_start(out=dbg_aps["sums"][b:b + 1, :], in_=srow[:]
                            ).then_inc(s_dbs, 16)

    @blk.gpsimd
    def _(e):
        for j in range(4):
            gp_selfb(e, s4_div, 8 * (j + 1))
            for h in range(HPC):
                e.dma_start(out=ot_stage[j].ap()[h * 128:(h + 1) * 128, :],
                            in_=ot_sb[:, h * T + j * 2048:h * T + (j + 1) * 2048]
                            ).then_inc(s4_gp, 16)
            e.wait_ge(s4_gp, 32 * (j + 1))
            e.collective_compute(
                "AllGather", ALU.bypass, replica_groups=RG,
                ins=[ot_stage[j].ap().opt()],
                outs=[ot_ag.ap()[j * 2048:(j + 1) * 2048, :].opt()],
            ).then_inc(cc_ot[j])

    if "ot" in dbg_aps:
        @blk.sync
        def _(e):
            sp_selfb(e, s4_div, 32)
            e.dma_start(out=dbg_aps["ot"], in_=ot_sb[:]).then_inc(s_dbg, 16)
    for _nm, _sb in (("kt4", kt_sb), ("v4", v_sb), ("qt4", qt_sb)):
        if _nm in dbg_aps:
            @blk.sync
            def _(e, _sb=_sb, _nm=_nm):
                e.wait_ge(s4_div, 33)
                e.dma_start(out=dbg_aps[_nm], in_=_sb[:]).then_inc(s_dbg, 16)
    if "sums" in dbg_aps:
        @blk.sync
        def _(e):
            e.wait_ge(s_dbs, 16 * 32)
            e.dma_start(out=jnk[0:1, 0:4], in_=ident_in[0:1, 0:4]
                        ).then_inc(s_dbg, 16)

    if stop_phase <= 4:
        return finish()

    sfree("rbc", "srow", "rrow", "acc", "ptmp", "exp_sb", "ot_sb",
          "v_sb", "kt_sb", "qt_sb")
    x1_sb = salloc("x1_sb", [128, NRT * D], F32)
    wob = salloc("wob", [128, 2 * NDT * 512], BF16)

    @blk.gpsimd
    def _(e):
        for j in range(4):
            e.wait_ge(cc_ot[j], 1)
        for ht in range(NH):
            src = ot_ag.ap().rearrange("a (s t) -> (a s) t", s=2)
            e.indirect_dma_start(
                big_a[:, ht * 1024:(ht + 1) * 1024], None, src,
                bass.IndirectOffsetOnAxis(ap=poix[:, ht:ht + 1], axis=0),
            ).then_inc(s5_gp, 16)
            e.dma_start(out=jnk[1:2, 0:4], in_=ident_in[0:1, 0:4]
                        ).then_inc(s5_gf, 16)
        emit_grants(e, gp5, g5_ord, 1)

    @blk.sync
    def _(e):
        e.wait_ge(s4_div, 32)
        for qt in range(NRT):
            e.dma_start(out=x1_sb[:, qt * D:(qt + 1) * D],
                        in_=x_own[qt * 128:(qt + 1) * 128, :]).then_inc(s5_ld, 16)
            trailer(e, s5_fl)
        e.wait_ge(cc_wo, 1)
        for och in range(4):
            if och >= 2:
                e.wait_ge(s5_mm, 8 * (och - 1))
            for ht in range(NH):
                e.dma_start(
                    out=wob[:, (och % 2) * 8192 + ht * 512:
                            (och % 2) * 8192 + (ht + 1) * 512],
                    in_=wo_ag.ap()[ht * 128:(ht + 1) * 128,
                                   och * 512:(och + 1) * 512],
                ).then_inc(s5_ld, 16)
                trailer(e, s5_fl)

    @blk.tensor
    def _(e):
        e.wait_ge(s4_xr, SCHED[-1]["eg_after"])
        e.wait_ge(s4_div, 32)
        e.wait_ge(s5_gp, 16 * NH)
        e.wait_ge(s5_gf, 16 * NH)
        for och in range(4):
            e.wait_ge(s5_ld, 16 * (NRT + NH * (och + 1)))
            e.wait_ge(s5_fl, 16 * (NRT + NH * (och + 1)))
            for qt in range(NRT):
                m = och * NRT + qt
                if m >= LOG_RING:
                    e.wait_ge(s5_ev, m - LOG_RING + 1)
                ps = pslot(m)
                for ht in range(NH):
                    e.matmul(ps, big_a[:, ht * 1024 + qt * 128:
                                       ht * 1024 + (qt + 1) * 128],
                             wob[:, (och % 2) * 8192 + ht * 512:
                                 (och % 2) * 8192 + (ht + 1) * 512],
                             start=(ht == 0), stop=False)
                e.matmul(ps, ones_rb[:], bor[0:1, och * 512:(och + 1) * 512],
                         start=False, stop=True).then_inc(s5_mm, 1)

    @blk.vector
    def _(e):
        for och in range(4):
            for qt in range(NRT):
                m = och * NRT + qt
                if m < 30:
                    e.wait_ge(s5_mm, m + 2)
                else:
                    e.wait_ge(gp5, g5_idx[("mm", m)])
                if och == 0:
                    e.wait_ge(s5_ld, 16 * (qt + 1))
                    e.wait_ge(s5_fl, 16 * (qt + 1))
                sl = x1_sb[:, qt * D + och * 512:qt * D + (och + 1) * 512]
                e.tensor_tensor(sl, sl, pslot(m), ALU.add).then_inc(s5_ev, 1)

    if stop_phase <= 5:
        if "x1" in dbg_aps:
            @blk.sync
            def _(e):
                sp_selfb(e, s5_ev, 32)
                e.dma_start(out=dbg_aps["x1"], in_=x1_sb[:]).then_inc(s_dbg, 16)
        return finish()

    x2ln = [salloc(f"x2ln{i}", [128, D], F32) for i in range(2)]
    scr6 = [salloc(f"scr6{i}", [128, D], F32) for i in range(2)]
    t_mu = salloc("t_mu", [128, NRT], F32)
    t_muv = salloc("t_muv", [128, NRT], F32)
    t_m2 = salloc("t_m2", [128, NRT], F32)
    t_m2e = salloc("t_m2e", [128, NRT], F32)
    t_sq = salloc("t_sq", [128, NRT], F32)
    t_v = salloc("t_v", [128, NRT], F32)
    t_l = salloc("t_l", [128, NRT], F32)
    t_r = salloc("t_r", [128, NRT], F32)

    @blk.scalar
    def _(e):
        e.wait_ge(sfl_v, 4)
        e.activation(ascr[:], masks_sb[:, 0:1024], AF.Identity)
        e.activation(ascr[:], masks_sb[:, 0:1024], AF.Identity).then_inc(sfl_s, 1)
        e.wait_ge(sfl_v, 5)
        e.activation(ascr[:], masks_sb[:, 0:1024], AF.Identity)
        e.activation(ascr[:], masks_sb[:, 0:1024], AF.Identity)
        e.activation(t_l[:], t_v[:], AF.Ln)
        e.activation(ascr[:], masks_sb[:, 0:1024], AF.Identity)
        e.activation(ascr[:], masks_sb[:, 0:1024], AF.Identity)
        e.activation(t_r[:], t_l[:], AF.Exp, scale=-0.5)
        e.activation(ascr[:], masks_sb[:, 0:1024], AF.Identity)
        e.activation(ascr[:], masks_sb[:, 0:1024], AF.Identity).then_inc(s6_rs, 1)

    @blk.vector
    def _(e):
        for r in range(NRT):
            e.wait_ge(s5_ev, 24 + r + 1)
            e.tensor_tensor(scr6[r % 2][:], x1_sb[:, r * D:(r + 1) * D],
                            x1_sb[:, r * D:(r + 1) * D], ALU.mult)
            e.reduce_sum(t_mu[:, r:r + 1], x1_sb[:, r * D:(r + 1) * D],
                         axis=AX)
            if r < NRT - 1:
                e.reduce_sum(t_sq[:, r:r + 1], scr6[r % 2][:], axis=AX)
            else:
                e.reduce_sum(t_sq[:, r:r + 1], scr6[r % 2][:],
                             axis=AX).then_inc(sfl_v, 1)
        e.wait_ge(sfl_s, 3)
        e.tensor_scalar(t_muv[:], t_mu[:], 1.0 / D, None, ALU.mult)
        e.tensor_scalar(t_sqd[:], t_sq[:], 1.0 / D, None, ALU.mult)
        e.tensor_copy(bscr[:], scr6[0][:, 0:1024])
        e.tensor_copy(bscr[:], scr6[1][:, 0:1024])
        e.tensor_tensor(t_m2[:], t_muv[:], t_muv[:], ALU.mult)
        e.tensor_copy(bscr[:], scr6[0][:, 0:1024])
        e.tensor_copy(bscr[:], scr6[1][:, 0:1024])
        e.tensor_scalar(t_m2e[:], t_m2[:], EPS, None, ALU.subtract)
        e.tensor_copy(bscr[:], scr6[0][:, 0:1024])
        e.tensor_copy(bscr[:], scr6[1][:, 0:1024])
        e.tensor_tensor(t_v[:], t_sqd[:], t_m2e[:],
                        ALU.subtract).then_inc(sfl_v, 1)
        for rr in range(NRT + 2):
            p = rr - 1
            if 0 <= p < NRT:
                e.wait_ge(s6_rs, 1)
                if p >= 2:
                    e.wait_ge(s6_tp, 4 * (p - 1))
                e.tensor_scalar(x2ln[p % 2][:], x1_sb[:, p * D:(p + 1) * D],
                                t_muv[:, p:p + 1], t_r[:, p:p + 1],
                                ALU.subtract, ALU.mult).then_inc(s6_ln, 1)
                if p == NRT - 1:
                    e.tensor_copy(bscr[:], scr6[0][:, 0:1024]).then_inc(s6_ln, 1)
            rte = rr - 2
            if 0 <= rte < NRT:
                e.wait_ge(s5_mm, 24 + rte + 1)
                if rte == 0:
                    e.tensor_copy(bscr[:], scr6[1][:, 0:1024])
                for grp4 in range(4):
                    gidx = 4 * rte + grp4
                    bank = paux if gidx % 2 == 0 else pout
                    e.wait_ge(s6_tp, gidx + 1)
                    dt0 = grp4 * 4
                    dst = big_a[:].rearrange("p (dt t) -> p dt t", dt=NDT)[
                        :, dt0:dt0 + 4, rte * 128:(rte + 1) * 128]
                    srcv = bank[:].rearrange("p (j t) -> p j t", j=4)
                    e.tensor_copy(dst, srcv).then_inc(s6_te, 1)

    @blk.tensor
    def _(e):
        for rt in range(NRT):
            e.wait_ge(s6_ln, rt + 2)
            for grp4 in range(4):
                gidx = 4 * rt + grp4
                bank = paux if gidx % 2 == 0 else pout
                if gidx >= 2:
                    e.wait_ge(s6_te, gidx - 1)
                elif rt == 0 and grp4 == 0:
                    e.wait_ge(s4_div, 32)
                for j in range(4):
                    dt = grp4 * 4 + j
                    ins = e.transpose(bank[:, j * 128:(j + 1) * 128],
                                      x2ln[rt % 2][:, dt * 128:(dt + 1) * 128],
                                      ident[:])
                    if j == 3:
                        ins.then_inc(s6_tp, 1)

    @blk.sync
    def _(e):
        e.wait_ge(cc_bw, 1)
        e.wait_ge(s5_mm, 32)
        for och in range(4):
            if och >= 2:
                e.wait_ge(s7_mm, 8 * (och - 1))
            for dt in range(NDT):
                for rr in range(2):
                    r = 2 * och + rr
                    e.dma_start(
                        out=wob[:, (och % 2) * 8192 + dt * 512 + rr * 256:
                                (och % 2) * 8192 + dt * 512 + (rr + 1) * 256],
                        in_=bwt_ag.ap()[r * D + dt * 128:
                                        r * D + (dt + 1) * 128, :],
                    ).then_inc(s7_ld, 16)
                    trailer(e, s7_fl)

    @blk.tensor
    def _(e):
        e.wait_ge(s5_ev, 32)
        for och in range(4):
            e.wait_ge(s7_ld, 16 * 32 * (och + 1))
            e.wait_ge(s7_fl, 16 * 32 * (och + 1))
            for qt in range(NRT):
                m = och * NRT + qt
                if och == 0:
                    if qt < NRT - 1:
                        e.wait_ge(s6_te, 4 * qt + 5)
                    else:
                        e.wait_ge(gp6, g6_idx[("te6", qt)])
                if m >= LOG_RING:
                    e.wait_ge(s7_ev, m - LOG_RING + 1)
                ps = pslot(m)
                for dt in range(NDT):
                    e.matmul(ps, big_a[:, dt * 1024 + qt * 128:
                                       dt * 1024 + (qt + 1) * 128],
                             wob[:, (och % 2) * 8192 + dt * 512:
                                 (och % 2) * 8192 + (dt + 1) * 512],
                             start=(dt == 0), stop=False)
                e.matmul(ps, ones_rb[:], bfr[0:1, och * 512:(och + 1) * 512],
                         start=False, stop=True).then_inc(s7_mm, 1)

    @blk.vector
    def _(e):
        for och in range(4):
            for qt in range(NRT):
                m = och * NRT + qt
                if m < 30:
                    e.wait_ge(s7_mm, m + 2)
                else:
                    e.wait_ge(gp7, g7_idx[("mm", m)])
                sl = x1_sb[:, qt * D + och * 512:qt * D + (och + 1) * 512]
                e.tensor_tensor(sl, sl, pslot(m), ALU.add).then_inc(s7_ev, 1)

    @blk.gpsimd
    def _(e):
        emit_grants(e, gp6, g6_ord, 1)
        emit_grants(e, gp7, g7_ord, 1)

    @blk.sync
    def _(e):
        for qt in range(NRT):
            if qt < NRT - 2:
                e.wait_ge(s7_ev, 24 + qt + 3)
            else:
                sp_selfb(e, s7_ev, 32)
            e.dma_start(out=y[qt * 128:(qt + 1) * 128, :],
                        in_=x1_sb[:, qt * D:(qt + 1) * D]).then_inc(s_st, 16)

    return finish(extra_waits=[(s_st, 16 * NRT)])



def make_in_maps(inputs):
    x = np.asarray(inputs["L0_attn_x"])
    wq = np.ascontiguousarray(np.asarray(inputs["L0_attn_Wq"]), dtype=np.float32)
    wk = np.asarray(inputs["L0_attn_Wk"])
    wv = np.asarray(inputs["L0_attn_Wv"])
    wo = np.asarray(inputs["L0_attn_Wo"])
    wf = np.asarray(inputs["L0_ffn_weight"])
    bq = np.asarray(inputs["L0_attn_bq"])
    bk = np.asarray(inputs["L0_attn_bk"])
    bv = np.asarray(inputs["L0_attn_bv"])
    bo = np.asarray(inputs["L0_attn_bo"])
    bf = np.asarray(inputs["L0_ffn_bias"])

    xf = np.asarray(x).reshape(T, D)
    masks = np.zeros((128, 4 * QB), dtype=ml_dtypes.bfloat16)
    p = np.arange(128)[:, None]
    i = np.arange(QB)[None, :]
    for jd in range(4):
        masks[:, jd * QB:(jd + 1) * QB] = np.where(
            i >= jd * 128 + p, 0.0, -30000.0).astype(ml_dtypes.bfloat16)
    ident = np.eye(128, dtype=np.float32)
    identb = np.eye(128, dtype=ml_dtypes.bfloat16)
    ones_r = np.ones((1, 128), np.float32)
    ones_c = np.ones((128, 1), np.float32)
    bq_t = np.ascontiguousarray(np.asarray(bq).reshape(NDT, 128).T,
                                dtype=np.float32)
    bo_r = np.ascontiguousarray(np.asarray(bo).reshape(1, D), dtype=np.float32)
    bf_r = np.ascontiguousarray(np.asarray(bf).reshape(1, D), dtype=np.float32)

    in_maps = []
    for c in range(NC):
        d = np.arange(128)[:, None]
        ht = np.arange(NH)[None, :]
        po_idx = (((c // 2) * 2048 + (ht // 2) * 256 + (ht % 2) * 128 + d) * 2
                  + (c % 2)).astype(np.uint32)
        in_maps.append({
            "x_own": np.ascontiguousarray(xf[c * TL:(c + 1) * TL, :],
                                          dtype=np.float32),
            "wq": wq,
            "wk_sl": np.ascontiguousarray(wk[:, c * 256:(c + 1) * 256],
                                          dtype=np.float32),
            "wv_sl": np.ascontiguousarray(wv[:, c * 256:(c + 1) * 256],
                                          dtype=np.float32),
            "wo_sl": np.ascontiguousarray(wo[c * 256:(c + 1) * 256, :],
                                          dtype=np.float32),
            "wf_sl": np.ascontiguousarray(wf[c * 256:(c + 1) * 256, :],
                                          dtype=np.float32),
            "bq_t": bq_t,
            "bk_t": np.ascontiguousarray(
                np.asarray(bk)[c * 256:(c + 1) * 256].reshape(HPC, 128).T,
                dtype=np.float32),
            "bv_t": np.ascontiguousarray(
                np.asarray(bv)[c * 256:(c + 1) * 256].reshape(HPC, 128).T,
                dtype=np.float32),
            "bo_r": bo_r,
            "bf_r": bf_r,
            "masks": masks,
            "ident": ident,
            "identb": identb,
            "ones_r": ones_r,
            "ones_c": ones_c,
            "po_idx": np.ascontiguousarray(po_idx),
        })
    return in_maps


def kernel(**inputs):
    nc, _ = build_nc()
    in_maps = make_in_maps(inputs)
    res = run_bass_kernel_spmd(nc, in_maps, core_ids=list(range(NC)))
    out = np.concatenate([np.asarray(res.results[c]["y"]) for c in range(NC)],
                         axis=0)
    return out.reshape(1, T, D).astype(np.float32, copy=False)
